# version 04
# baseline (speedup 1.0000x reference)
import numpy as np
import concourse.bass as bass
import concourse.mybir as mybir
from concourse.bass_utils import run_bass_kernel_spmd

F32 = mybir.dt.float32
BF16 = mybir.dt.bfloat16
AF = mybir.ActivationFunctionType
ALU = mybir.AluOpType
AX = mybir.AxisListType

NCORES = 8
T, D = 8192, 2048
TPC = T // NCORES
NTILE = TPC // 128
DI = 8192
LN_EPS = 1e-5

ENG_SEM = {"vector": "dv", "scalar": "ac", "gpsimd": "gp", "tensor": "pe"}


class Ctx:

    def __init__(self, nc):
        self.nc = nc
        self.hold = []
        self.blk = nc.Block()
        self.block = self.blk.__enter__()
        self.sems = {}
        self.cnt = {}
        for n in ["pe", "dv", "ac", "gp"]:
            self._mksem(n)

    def _mksem(self, n):
        h = self.nc.semaphore(n)
        self.sems[n] = h.__enter__()
        self.hold.append(h)
        self.cnt[n] = 0

    def qsem(self, n):
        if n not in self.sems:
            self._mksem(n)
        return n

    def close(self):
        self.blk.__exit__(None, None, None)
        for h in reversed(self.hold):
            h.__exit__(None, None, None)

    def on(self, engine, fn):
        getattr(self.block, engine)(fn)

    def w(self, eng, ev):
        if ev is None:
            return
        sem, val = ev
        if val > 0:
            eng.wait_ge(self.sems[sem], val)

    def wown(self, eng, engine_name):
        sem = ENG_SEM[engine_name]
        if self.cnt[sem] > 0:
            eng.wait_ge(self.sems[sem], self.cnt[sem])

    def step(self, engine_name, instr):
        sem = ENG_SEM[engine_name]
        instr.then_inc(self.sems[sem], 1)
        self.cnt[sem] += 1
        return (sem, self.cnt[sem])

    def dma(self, eng, slot_sem, out, in_):
        s = self.qsem(slot_sem)
        if self.cnt[s] > 0:
            eng.wait_ge(self.sems[s], self.cnt[s])
        i = eng.dma_start(out=out, in_=in_)
        i.then_inc(self.sems[s], 16)
        self.cnt[s] += 16
        return (s, self.cnt[s])


def sb(nc, hold, name, shape, dt=F32):
    h = nc.sbuf_tensor(name, shape, dt)
    t = h.__enter__()
    hold.append(h)
    return t


def psum(nc, hold, name, shape, dt=F32):
    h = nc.psum_tensor(name, shape, dt)
    t = h.__enter__()
    hold.append(h)
    return t



def build_ffn0():
    nc = bass.Bass("TRN2", target_bir_lowering=False)
    hold = []
    x_d = nc.dram_tensor("x", [TPC, D], F32, kind="ExternalInput")
    wg_d = nc.dram_tensor("wg", [D, DI], F32, kind="ExternalInput")
    wu_d = nc.dram_tensor("wu", [D, DI], F32, kind="ExternalInput")
    wd_d = nc.dram_tensor("wd", [DI, D], F32, kind="ExternalInput")
    id_d = nc.dram_tensor("ident", [128, 128], F32, kind="ExternalInput")
    xo_d = nc.dram_tensor("xo", [TPC, D], F32, kind="ExternalOutput")

    xs = sb(nc, hold, "xs", [128, 4 * D])
    ztmp = sb(nc, hold, "ztmp", [128, 2 * D], BF16)
    zfm = sb(nc, hold, "zfm", [128, 16 * 512], BF16)
    hsp = sb(nc, hold, "hsp", [128, 4 * DI], BF16)
    wgb = sb(nc, hold, "wgb", [128, 2 * 8192], BF16)
    wub = sb(nc, hold, "wub", [128, 8192], BF16)
    sg = sb(nc, hold, "sg", [128, 2 * 512])
    tmpa = sb(nc, hold, "tmpa", [128, 1024], BF16)
    tmpm = sb(nc, hold, "tmpm", [128, 1024], BF16)
    tmpm2 = sb(nc, hold, "tmpm2", [128, 1024], BF16)
    tmps = sb(nc, hold, "tmps", [128, 256], BF16)
    tmpk = sb(nc, hold, "tmpk", [128, 1024], BF16)
    bn6 = sb(nc, hold, "bn6", [128, 24])
    mv = sb(nc, hold, "mv", [128, 2])
    sd = sb(nc, hold, "sd", [128, 1])
    rstd = sb(nc, hold, "rstd", [128, 1])
    epst = sb(nc, hold, "epst", [128, 1])
    identb = sb(nc, hold, "identb", [128, 128], BF16)

    p = [psum(nc, hold, f"p{i}", [128, 512]) for i in range(4)]
    pt = [psum(nc, hold, f"pt{i}", [128, 1024], BF16) for i in range(2)]

    c = Ctx(nc)
    ev = {}
    free = {}

    def _pre(g):
        ev["ident"] = c.dma(g, "q_id", identb[:, :], id_d[:, :])
    c.on("gpsimd", _pre)

    def _pre_v(v):
        ev["eps"] = c.step("vector", v.memset(epst[:, :], LN_EPS))
    c.on("vector", _pre_v)

    NB = DI // 512
    NC64 = DI // 128
    NIG = DI // 1024
    NDB = D // 512
    for half in range(2):
        tiles = [half * 4 + i for i in range(4)]

        for idx, j in enumerate(tiles):
            def _ld(e, idx=idx, j=j):
                c.w(e, ev.get(("xstored", half - 1, idx)))
                ev[("xld", half, idx)] = c.dma(
                    e, f"q_x{idx}", xs[:, idx * D:(idx + 1) * D],
                    x_d[j * 128:(j + 1) * 128, :])
            c.on("sync", _ld)

        for idx, j in enumerate(tiles):
            xsl = xs[:, idx * D:(idx + 1) * D]
            zsl = ztmp[:, (idx % 2) * D:((idx % 2) + 1) * D]

            def _stats(v, xsl=xsl, idx=idx):
                c.wown(v, "vector")
                c.w(v, ev[("xld", half, idx)])
                for k4 in range(4):
                    c.step("vector", v.bn_stats(bn6[:, k4 * 6:(k4 + 1) * 6],
                                                xsl[:, k4 * 512:(k4 + 1) * 512]))
                c.wown(v, "vector")
                c.w(v, ev.get(("sd", half, idx - 1)) if idx else
                    ev.get(("sd", half - 1, 3)))
                ev[("mv", half, idx)] = c.step(
                    "vector", v.bn_aggr(mv[:, :], bn6[:, :]))
            c.on("vector", _stats)

            def _sqrt(s, idx=idx):
                c.wown(s, "scalar")
                c.w(s, ev[("mv", half, idx)])
                ev[("sd", half, idx)] = c.step(
                    "scalar",
                    s.activation(sd[:, :], mv[:, 1:2], AF.Sqrt, bias=epst[:, :]))
            c.on("scalar", _sqrt)

            def _apply(v, xsl=xsl, zsl=zsl, idx=idx):
                c.wown(v, "vector")
                c.w(v, ev[("sd", half, idx)])
                c.step("vector", v.reciprocal(rstd[:, :], sd[:, :]))
                c.wown(v, "vector")
                key = ("ztr_done", half, idx - 2) if idx >= 2 else \
                      ("ztr_done", half - 1, idx + 2)
                if key in ev:
                    c.w(v, ("pe", ev[key]))
                ev[("z", half, idx)] = c.step(
                    "vector",
                    v.tensor_scalar(zsl, xsl, mv[:, 0:1], rstd[:, :],
                                    ALU.subtract, ALU.mult))
            c.on("vector", _apply)

            for ch in range(16):
                def _trc(t, zsl=zsl, idx=idx, ch=ch):
                    if ch == 0:
                        c.w(t, ev[("z", half, idx)])
                        c.w(t, ev["ident"])
                    if ("pt_free", ch % 2) in ev:
                        c.w(t, ("dv", ev[("pt_free", ch % 2)]))
                    ev[("ztrp", half, idx, ch)] = c.step(
                        "tensor", t.transpose(
                            pt[ch % 2][:, 0:128],
                            zsl[:, ch * 128:(ch + 1) * 128], identb[:, :]))
                c.on("tensor", _trc)

                def _evc(v, idx=idx, ch=ch):
                    c.wown(v, "vector")
                    c.w(v, ev[("ztrp", half, idx, ch)])
                    c.step("vector", v.tensor_copy(
                        zfm[:, ch * 512 + idx * 128: ch * 512 + (idx + 1) * 128],
                        pt[ch % 2][:, 0:128]))
                    ev[("pt_free", ch % 2)] = c.cnt["dv"]
                c.on("vector", _evc)
            ev[("ztr_done", half, idx)] = c.cnt["pe"]
        ev[("zfm_ready", half)] = ("dv", c.cnt["dv"])

        for b in range(NB):
            def _ldw(g, b=b):
                c.w(g, free.get(("wgb_free", b % 2)))
                if b % 2 == 0:
                    c.w(g, free.get(("wdslot_free", 0)))
                    c.w(g, free.get(("wdslot_free", 1)))
                src = wg_d[:, b * 512:(b + 1) * 512].rearrange(
                    "(c p) i -> p c i", p=128)
                dst = wgb[:, (b % 2) * 8192:(b % 2 + 1) * 8192].rearrange(
                    "pp (c i) -> pp c i", i=512)
                ev[("wg", half, b)] = c.dma(g, f"q_wg{b % 2}", dst, src)
                c.w(g, free.get(("wub_free",)))
                src2 = wu_d[:, b * 512:(b + 1) * 512].rearrange(
                    "(c p) i -> p c i", p=128)
                ev[("wu", half, b)] = c.dma(
                    g, "q_wu", wub[:, :].rearrange("pp (c i) -> pp c i", i=512), src2)
            c.on("gpsimd", _ldw)

            for jj in range(4):
                gb = p[jj % 2]
                ub = p[2 + jj % 2]

                def _mm(t, b=b, jj=jj, gb=gb, ub=ub):
                    c.w(t, ev[("zfm_ready", half)])
                    c.w(t, ev[("wg", half, b)])
                    c.w(t, free.get(("pfree", jj % 2)))
                    last = None
                    for ch in range(16):
                        last = t.matmul(
                            gb[:, :],
                            zfm[:, ch * 512 + jj * 128: ch * 512 + (jj + 1) * 128],
                            wgb[:, (b % 2) * 8192 + ch * 512:
                                (b % 2) * 8192 + (ch + 1) * 512],
                            start=(ch == 0), stop=(ch == 15))
                    ev[("gmm", half, b, jj)] = c.step("tensor", last)
                    c.w(t, ev[("wu", half, b)])
                    c.w(t, free.get(("pfree", 2 + jj % 2)))
                    for ch in range(16):
                        last = t.matmul(
                            ub[:, :],
                            zfm[:, ch * 512 + jj * 128: ch * 512 + (jj + 1) * 128],
                            wub[:, ch * 512:(ch + 1) * 512],
                            start=(ch == 0), stop=(ch == 15))
                    ev[("umm", half, b, jj)] = c.step("tensor", last)
                    if jj == 3:
                        free[("wgb_free", b % 2)] = ev[("gmm", half, b, jj)]
                        free[("wub_free",)] = ev[("umm", half, b, jj)]
                c.on("tensor", _mm)

                def _silu(s, b=b, jj=jj, gb=gb):
                    c.wown(s, "scalar")
                    c.w(s, ev[("gmm", half, b, jj)])
                    c.w(s, free.get(("sg_free", jj % 2)))
                    ev[("sg", half, b, jj)] = c.step(
                        "scalar",
                        s.activation(sg[:, (jj % 2) * 512:(jj % 2 + 1) * 512],
                                     gb[:, :], AF.Silu))
                    free[("pfree", jj % 2)] = ev[("sg", half, b, jj)]
                c.on("scalar", _silu)

                def _h(v, b=b, jj=jj, ub=ub):
                    c.wown(v, "vector")
                    c.w(v, ev[("sg", half, b, jj)])
                    c.w(v, ev[("umm", half, b, jj)])
                    ev[("hev", half, b, jj)] = c.step(
                        "vector",
                        v.tensor_tensor(
                            hsp[:, jj * DI + b * 512: jj * DI + (b + 1) * 512],
                            sg[:, (jj % 2) * 512:(jj % 2 + 1) * 512],
                            ub[:, :], ALU.mult))
                    free[("pfree", 2 + jj % 2)] = ev[("hev", half, b, jj)]
                    free[("sg_free", jj % 2)] = ev[("hev", half, b, jj)]
                c.on("vector", _h)

        for jj in range(4):
            for s4 in range(NIG):
                base = jj * DI + s4 * 1024
                hspan = hsp[:, base: base + 1024]

                def _spa(s, jj=jj, s4=s4, hspan=hspan):
                    c.wown(s, "scalar")
                    c.w(s, ev[("hev", half, 2 * s4 + 1, jj)])
                    c.w(s, ev.get(("sp_aused",)))
                    ev[("spabs", half, jj, s4)] = c.step(
                        "scalar", s.activation(tmpa[:, :], hspan, AF.Abs))
                c.on("scalar", _spa)

                def _sp(v, jj=jj, s4=s4, hspan=hspan):
                    def st(mk):
                        c.wown(v, "vector")
                        return c.step("vector", mk())
                    c.wown(v, "vector")
                    c.w(v, ev[("spabs", half, jj, s4)])
                    a0 = tmpa[:, 0::4]; a1 = tmpa[:, 1::4]
                    a2 = tmpa[:, 2::4]; a3 = tmpa[:, 3::4]
                    st(lambda: v.tensor_tensor(tmpm[:, 0:256], a0, a1, ALU.max))
                    st(lambda: v.tensor_tensor(tmpm[:, 256:512], a2, a3, ALU.max))
                    st(lambda: v.tensor_tensor(tmpm[:, 512:768], a0, a1, ALU.min))
                    st(lambda: v.tensor_tensor(tmpm[:, 768:1024], a2, a3, ALU.min))
                    m01 = tmpm[:, 0:256]; m23 = tmpm[:, 256:512]
                    n01 = tmpm[:, 512:768]; n23 = tmpm[:, 768:1024]
                    st(lambda: v.tensor_tensor(tmpm2[:, 0:256], m01, m23, ALU.max))
                    st(lambda: v.tensor_tensor(tmpm2[:, 256:512], m01, m23, ALU.min))
                    st(lambda: v.tensor_tensor(tmpm2[:, 512:768], n01, n23, ALU.max))
                    mx = tmpm2[:, 0:256]; mn = tmpm2[:, 256:512]
                    nx = tmpm2[:, 512:768]
                    st(lambda: v.tensor_tensor(tmpm2[:, 768:1024], mx, nx, ALU.min))
                    st(lambda: v.tensor_tensor(tmps[:, :], mn, tmpm2[:, 768:1024], ALU.max))
                    sec_b = bass.AP(tmps.ap().tensor, 0,
                                    [[256, 128], [1, 256], [0, 4]])
                    av = tmpa[:, :].rearrange("p (g j) -> p g j", j=4)
                    kv = tmpk[:, :].rearrange("p (g j) -> p g j", j=4)
                    ev[("sp_aused",)] = st(
                        lambda: v.tensor_tensor(kv, av, sec_b, ALU.is_ge))
                    ev[("sp", half, jj, s4)] = st(
                        lambda: v.tensor_tensor(hspan, hspan, tmpk[:, :], ALU.mult))
                c.on("vector", _sp)

        for jj in range(4):
            for ch in range(NC64):
                def _trh(t, jj=jj, ch=ch):
                    if ch == 0:
                        c.w(t, ev[("sp", half, jj, DI // 1024 - 1)])
                    if ("pt_free", ch % 2) in ev:
                        c.w(t, ("dv", ev[("pt_free", ch % 2)]))
                    ev[("htrp", half, jj, ch)] = c.step(
                        "tensor", t.transpose(
                            pt[ch % 2][:, 0:128],
                            hsp[:, jj * DI + ch * 128: jj * DI + (ch + 1) * 128],
                            identb[:, :]))
                c.on("tensor", _trh)

                def _evh(v, jj=jj, ch=ch):
                    c.wown(v, "vector")
                    c.w(v, ev[("htrp", half, jj, ch)])
                    c.step("vector", v.tensor_copy(
                        hsp[:, jj * DI + ch * 128: jj * DI + (ch + 1) * 128],
                        pt[ch % 2][:, 0:128]))
                    ev[("pt_free", ch % 2)] = c.cnt["dv"]
                c.on("vector", _evh)
            ev[("htr_done", half, jj)] = ("dv", c.cnt["dv"])

        for db in range(NDB):
            for icg in range(NIG):
                def _ldwd(g, db=db, icg=icg):
                    k = db * NIG + icg
                    slot = k % 2
                    if ("wdslot_free", slot) in free:
                        c.w(g, free[("wdslot_free", slot)])
                    elif k < 2:
                        c.w(g, ev[("gmm", half, NB - 1, 3)])
                    src = wd_d[icg * 1024:(icg + 1) * 1024,
                               db * 512:(db + 1) * 512].rearrange(
                        "(c p) d -> p c d", p=128)
                    dst = wgb[:, slot * 4096:(slot + 1) * 4096].rearrange(
                        "pp (c d) -> pp c d", d=512)
                    ev[("wd", half, k)] = c.dma(g, f"q_wg{slot}", dst, src)
                c.on("gpsimd", _ldwd)

                def _dmm(t, db=db, icg=icg):
                    k = db * NIG + icg
                    slot = k % 2
                    c.w(t, ev[("wd", half, k)])
                    last = None
                    for jj in range(4):
                        if icg == 0:
                            c.w(t, ev[("htr_done", half, jj)])
                            c.w(t, free.get(("dp_free", jj)))
                            c.w(t, ev[("hev", half, NB - 1, 3)])
                            c.w(t, ev[("sg", half, NB - 1, 3)])
                        for ch in range(8):
                            last = t.matmul(
                                p[jj][:, :],
                                hsp[:, jj * DI + (icg * 8 + ch) * 128:
                                    jj * DI + (icg * 8 + ch + 1) * 128],
                                wgb[:, slot * 4096 + ch * 512:
                                    slot * 4096 + (ch + 1) * 512],
                                start=(icg == 0 and ch == 0),
                                stop=(icg == NIG - 1 and ch == 7))
                    ev[("dmm", half, k)] = c.step("tensor", last)
                    free[("wdslot_free", slot)] = ev[("dmm", half, k)]
                c.on("tensor", _dmm)

            for jj in range(4):
                def _dev(v, db=db, jj=jj):
                    c.wown(v, "vector")
                    c.w(v, ev[("dmm", half, db * NIG + NIG - 1)])
                    xsl = xs[:, jj * D + db * 512: jj * D + (db + 1) * 512]
                    ev[("resid", half, db, jj)] = c.step(
                        "vector",
                        v.scalar_tensor_tensor(xsl, p[jj][:, :], 0.5, xsl,
                                               ALU.mult, ALU.add))
                    free[("dp_free", jj)] = ev[("resid", half, db, jj)]
                    if db == NDB - 1:
                        free[("pfree", jj % 2)] = ev[("resid", half, db, jj)]
                        free[("pfree", 2 + jj % 2)] = ev[("resid", half, db, jj)]
                c.on("vector", _dev)

        for idx, j in enumerate(tiles):
            def _st(e, idx=idx, j=j):
                for db in range(NDB):
                    c.w(e, ev[("resid", half, db, idx)])
                ev[("xstored", half, idx)] = c.dma(
                    e, f"q_xst{idx}", xo_d[j * 128:(j + 1) * 128, :],
                    xs[:, idx * D:(idx + 1) * D])
            c.on("sync", _st)

    def _fin(e):
        for idx in range(4):
            c.w(e, ev[("xstored", 1, idx)])
    c.on("sync", _fin)

    c.close()
    for h in reversed(hold):
        h.__exit__(None, None, None)
    return nc



_IDENT = np.eye(128, dtype=np.float32)
_CACHE = {}


def _get(name, builder):
    if name not in _CACHE:
        _CACHE[name] = builder()
    return _CACHE[name]


def _run_ffn0(x_full, wg, wu, wd):
    nc = _get("ffn0", build_ffn0)
    shards = x_full.reshape(NCORES, TPC, D)
    in_maps = [{"x": np.ascontiguousarray(shards[i]), "wg": wg, "wu": wu,
                "wd": wd, "ident": _IDENT} for i in range(NCORES)]
    res = run_bass_kernel_spmd(nc, in_maps, core_ids=list(range(NCORES)))
    return np.concatenate([r["xo"] for r in res.results], 0)


def _rope_tables(core):
    NT = TPC // 128
    t = np.arange(core * TPC, (core + 1) * TPC, dtype=np.float64)
    invf = 1.0 / (10000.0 ** (np.arange(0, 128, 2, dtype=np.float64) / 128.0))
    ang = (t[:, None] * invf[None, :]).reshape(NT, 128, 64)
    tab = np.stack([np.cos(ang), np.sin(ang)], axis=1)
    return np.ascontiguousarray(tab.transpose(2, 0, 1, 3)).astype(np.float32)


def _run_attn0(x_full, inp):
    nc_a = _get("attn0a", build_attn0a)
    nc_b = _get("attn0b", build_attn0b)
    shards = x_full.reshape(NCORES, TPC, D)
    wq = np.ascontiguousarray(inp["L0_attn_Wq"].T)
    wk = np.ascontiguousarray(inp["L0_attn_Wk"].T)
    wv = np.ascontiguousarray(inp["L0_attn_Wv"].T)
    wg = np.ascontiguousarray(inp["L0_attn_Wg"].T)
    wo = np.ascontiguousarray(inp["L0_attn_Wo"].T)
    bias = np.ascontiguousarray(np.stack(
        [inp["L0_attn_bq"], inp["L0_attn_bk"], inp["L0_attn_bv"],
         inp["L0_attn_bg"]]))
    bo = np.ascontiguousarray(inp["L0_attn_bo"].reshape(1, D))
    in_a = [{"x": np.ascontiguousarray(shards[i]), "wq": wq, "wk": wk,
             "wv": wv, "wgt": wg, "bias": bias, "rope": _rope_tables(i),
             "ident": _IDENT} for i in range(NCORES)]
    ra = run_bass_kernel_spmd(nc_a, in_a, core_ids=list(range(NCORES)))
    zS = np.zeros((128, 16, 128), np.float32)
    zZ = np.zeros((128, 16), np.float32)
    in_b = []
    for i in range(NCORES):
        dsS = np.stack([ra.results[r]["ds"] if r < i else zS
                        for r in range(NCORES)]).transpose(2, 1, 0, 3)
        dsZ = np.stack([ra.results[r]["dz"] if r < i else zZ
                        for r in range(NCORES)]).transpose(1, 0, 2)
        in_b.append({"x": np.ascontiguousarray(shards[i]),
                     "qf": ra.results[i]["qf"], "kt": ra.results[i]["kt"],
                     "vt": ra.results[i]["vt"], "gt": ra.results[i]["gt"],
                     "dsS": np.ascontiguousarray(dsS),
                     "dsZ": np.ascontiguousarray(dsZ),
                     "wo": wo, "bo": bo, "ident": _IDENT})
    rb = run_bass_kernel_spmd(nc_b, in_b, core_ids=list(range(NCORES)))
    return np.concatenate([r["xo"] for r in rb.results], 0)


_SWT_CACHE = {}


def _run_ffn1(x_full, inp):
    nc_a = _get("ffn1a", build_ffn1a)
    nc_b = _get("ffn1b", build_ffn1b)
    shards = x_full.reshape(NCORES, TPC, D)
    key = id(inp["L1_ffn_switch_w"])
    if key not in _SWT_CACHE:
        _SWT_CACHE.clear()
        sw = inp["L1_ffn_switch_w"].reshape(8, 2048, D)
        _SWT_CACHE[key] = [
            np.ascontiguousarray(sw[:, i * 256:(i + 1) * 256, :].transpose(0, 2, 1))
            for i in range(NCORES)]
    swt = _SWT_CACHE[key]
    xr = x_full[:2048].reshape(NCORES, 256, D)
    in_a = [{"xr": np.ascontiguousarray(xr[i]), "swt": swt[i],
             "ident": _IDENT} for i in range(NCORES)]
    ra = run_bass_kernel_spmd(nc_a, in_a, core_ids=list(range(NCORES)))
    pl8 = np.ascontiguousarray(
        np.stack([ra.results[i]["pl"][0] for i in range(NCORES)]))
    w1h = np.ascontiguousarray(inp["L1_ffn_w1"].transpose(2, 0, 1))
    w2h = np.ascontiguousarray(inp["L1_ffn_w2"].transpose(2, 0, 1))
    b1h = np.ascontiguousarray(inp["L1_ffn_b1"].T)
    b2h = np.ascontiguousarray(inp["L1_ffn_b2"].T)
    w3t = np.ascontiguousarray(inp["L1_ffn_w3"].T)
    b3 = np.ascontiguousarray(inp["L1_ffn_b3"].reshape(1, D))
    swb = np.ascontiguousarray(inp["L1_ffn_switch_b"].reshape(1, 8))
    in_b = [{"x": np.ascontiguousarray(shards[i]), "pl8": pl8, "swb": swb,
             "w1h": w1h, "b1h": b1h, "w2h": w2h, "b2h": b2h, "w3t": w3t,
             "b3": b3, "ident": _IDENT} for i in range(NCORES)]
    rb = run_bass_kernel_spmd(nc_b, in_b, core_ids=list(range(NCORES)))
    return np.concatenate([r["xo"] for r in rb.results], 0)


def kernel(**inp):
    x = np.ascontiguousarray(
        inp["L0_attn_x"].reshape(T, D)).astype(np.float32, copy=False)
    x1 = _run_ffn0(x, inp["L0_ffn_w_gate"], inp["L0_ffn_w_up"],
                   inp["L0_ffn_w_down"])
    x2 = _run_attn0(x1, inp)
    x3 = _run_ffn0(x2, inp["L0_ffn_w_gate"], inp["L0_ffn_w_up"],
                   inp["L0_ffn_w_down"])
    x4 = _run_ffn1(x3, inp)
    x5 = _run_attn1(x4, inp)
    x6 = _run_ffn1(x5, inp)
    return x6.reshape(1, T, D)



A0A_PHASES = 9


def build_attn0a():
    nc = bass.Bass("TRN2", target_bir_lowering=False)
    hold = []
    NT = TPC // 128
    NCH = TPC // 64
    x_d = nc.dram_tensor("x", [TPC, D], F32, kind="ExternalInput")
    wq_d = nc.dram_tensor("wq", [D, D], F32, kind="ExternalInput")
    wk_d = nc.dram_tensor("wk", [D, D], F32, kind="ExternalInput")
    wv_d = nc.dram_tensor("wv", [D, D], F32, kind="ExternalInput")
    wgt_d = nc.dram_tensor("wgt", [D, D], F32, kind="ExternalInput")
    bias_d = nc.dram_tensor("bias", [4, D], F32, kind="ExternalInput")
    rope_d = nc.dram_tensor("rope", [128, NT, 2, 64], F32, kind="ExternalInput")
    id_d = nc.dram_tensor("ident", [128, 128], F32, kind="ExternalInput")
    qf_d = nc.dram_tensor("qf", [D, TPC], BF16, kind="ExternalOutput")
    kt_d = nc.dram_tensor("kt", [TPC, D], BF16, kind="ExternalOutput")
    vt_d = nc.dram_tensor("vt", [TPC, D], BF16, kind="ExternalOutput")
    gt_d = nc.dram_tensor("gt", [TPC, D], BF16, kind="ExternalOutput")
    ds_d = nc.dram_tensor("ds", [128, 16, 128], F32, kind="ExternalOutput")
    dz_d = nc.dram_tensor("dz", [128, 16], F32, kind="ExternalOutput")

    xs = sb(nc, hold, "xs", [128, D])
    ztmp = sb(nc, hold, "ztmp", [128, D], BF16)
    zfm = sb(nc, hold, "zfm", [128, 16 * TPC], BF16)
    wb = sb(nc, hold, "wb", [128, 2 * 8192], BF16)
    qk = sb(nc, hold, "qk", [128, NT * D], BF16)
    vtm = sb(nc, hold, "vtm", [128, NT * D], BF16)
    gsl = sb(nc, hold, "gsl", [128, (TPC // 128) * D], BF16)
    ropes = sb(nc, hold, "ropes", [128, NT * 2 * 64])
    tr1 = sb(nc, hold, "tr1", [128, 1024], BF16)
    tr2 = sb(nc, hold, "tr2", [128, 1024], BF16)
    te1 = sb(nc, hold, "te1", [128, 2048], BF16)
    te2 = sb(nc, hold, "te2", [128, 2048], BF16)
    biasb = sb(nc, hold, "biasb", [1, D], BF16)
    onesr = sb(nc, hold, "onesr", [1, 128], BF16)
    onesc = sb(nc, hold, "onesc", [128, 1], BF16)
    dsout = sb(nc, hold, "dsout", [128, 16 * 128])
    dzout = sb(nc, hold, "dzout", [128, 16])
    bn6 = sb(nc, hold, "bn6", [128, 24])
    mv = sb(nc, hold, "mv", [128, 2])
    sd = sb(nc, hold, "sd", [128, 1])
    rstd = sb(nc, hold, "rstd", [128, 1])
    epst = sb(nc, hold, "epst", [128, 1])
    identb = sb(nc, hold, "identb", [128, 128], BF16)

    pp = [psum(nc, hold, f"pp{i}", [128, 512]) for i in range(6)]
    pt = [psum(nc, hold, f"pt{i}", [128, 1024], BF16) for i in range(2)]

    c = Ctx(nc)
    ev = {}
    free = {}
    PPELEM = NT * 2 * 64
    QKPP = NT * D

    def _pre(g):
        ev["ident"] = c.dma(g, "q_id", identb[:, :], id_d[:, :])
        pass
        ev["rope"] = c.dma(
            g, "q_r",
            ropes[:, :].rearrange("p (tt s i) -> p tt s i", s=2, i=64),
            rope_d[:, :, :, :])
    c.on("gpsimd", _pre)

    def _pre_v(v):
        c.step("vector", v.memset(epst[:, :], LN_EPS))
        c.wown(v, "vector")
        c.step("vector", v.memset(onesr[:, :], 1.0))
        c.wown(v, "vector")
        ev["ones"] = c.step("vector", v.memset(onesc[:, :], 1.0))
    c.on("vector", _pre_v)

    for idx in range(NT):
        def _ld(e, idx=idx):
            if idx > 0:
                c.w(e, ev[("z", idx - 1)])
            ev[("xld", idx)] = c.dma(
                e, "q_x", xs[:, :], x_d[idx * 128:(idx + 1) * 128, :])
        c.on("sync", _ld)

        def _stats(v, idx=idx):
            c.wown(v, "vector")
            c.w(v, ev[("xld", idx)])
            for k4 in range(4):
                c.step("vector", v.bn_stats(bn6[:, k4 * 6:(k4 + 1) * 6],
                                            xs[:, k4 * 512:(k4 + 1) * 512]))
            c.wown(v, "vector")
            c.w(v, ev.get(("sd", idx - 1)))
            ev[("mv", idx)] = c.step("vector", v.bn_aggr(mv[:, :], bn6[:, :]))
        c.on("vector", _stats)

        def _sqrt(s, idx=idx):
            c.wown(s, "scalar")
            c.w(s, ev[("mv", idx)])
            ev[("sd", idx)] = c.step(
                "scalar", s.activation(sd[:, :], mv[:, 1:2], AF.Sqrt,
                                       bias=epst[:, :]))
        c.on("scalar", _sqrt)

        def _apply(v, idx=idx):
            c.wown(v, "vector")
            c.w(v, ev[("sd", idx)])
            c.step("vector", v.reciprocal(rstd[:, :], sd[:, :]))
            c.wown(v, "vector")
            if ("ztr_done", idx - 1) in ev:
                c.w(v, ("pe", ev[("ztr_done", idx - 1)]))
            ev[("z", idx)] = c.step(
                "vector", v.tensor_scalar(ztmp[:, :], xs[:, :], mv[:, 0:1],
                                          rstd[:, :], ALU.subtract, ALU.mult))
        c.on("vector", _apply)

        for ch in range(16):
            def _trc(t, idx=idx, ch=ch):
                if ch == 0:
                    c.w(t, ev[("z", idx)])
                    c.w(t, ev["ident"])
                if ("pt_free", ch % 2) in ev:
                    c.w(t, ("dv", ev[("pt_free", ch % 2)]))
                ev[("ztrp", idx, ch)] = c.step(
                    "tensor", t.transpose(pt[ch % 2][:, 0:128],
                                          ztmp[:, ch * 128:(ch + 1) * 128],
                                          identb[:, :]))
            c.on("tensor", _trc)

            def _evc(v, idx=idx, ch=ch):
                c.wown(v, "vector")
                c.w(v, ev[("ztrp", idx, ch)])
                c.step("vector", v.tensor_copy(
                    zfm[:, ch * TPC + idx * 128: ch * TPC + (idx + 1) * 128],
                    pt[ch % 2][:, 0:128]))
                ev[("pt_free", ch % 2)] = c.cnt["dv"]
            c.on("vector", _evc)
        ev[("ztr_done", idx)] = c.cnt["pe"]
    ev["zfm_ready"] = ("dv", c.cnt["dv"])
    ev["consts"] = ("dv", 3)

    def emit_ds():
        for h in range(16):
            bank = pp[h % 4]
            off = (h // 4) * 128
            for gidx, base in enumerate((0, 64)):
                def _mmg(t, h=h, bank=bank, off=off, gidx=gidx, base=base):
                    if h == 0 and gidx == 0:
                        c.w(t, free[("ppb", 0)])
                        c.w(t, free[("ppb", 1)])
                        c.w(t, ev[("elu", "k", NT - 1)])
                        c.w(t, ev["ones"])
                    if gidx == 0 and ("dsevB", h - 4) in ev:
                        c.w(t, ev[("dsevB", h - 4)])
                    if gidx == 1:
                        c.w(t, ev[("dsevA", h)])
                    last = None
                    for cc8 in range(NCH // 2):
                        lhs = qk[base:base + 64,
                                 cc8 * D + h * 128: cc8 * D + (h + 1) * 128]
                        rhs = vtm[base:base + 64,
                                  cc8 * D + h * 128: cc8 * D + (h + 1) * 128]
                        last = t.matmul(bank[:, off:off + 128], lhs, rhs,
                                        start=(cc8 == 0),
                                        stop=(cc8 == NCH // 2 - 1))
                    ev[("dsmmS", h, gidx)] = c.step("tensor", last)
                    zc = 4 * h + 2 * gidx
                    for cc8 in range(NCH // 2):
                        lhs = qk[base:base + 64,
                                 cc8 * D + h * 128: cc8 * D + (h + 1) * 128]
                        last = t.matmul(pp[4][:, zc:zc + 1], lhs,
                                        onesc[base:base + 64, :],
                                        start=(cc8 == 0),
                                        stop=(cc8 == NCH // 2 - 1))
                    ev[("dsmm", h, gidx)] = c.step("tensor", last)
                c.on("tensor", _mmg)

                if gidx == 0:
                    def _evA(v, h=h, bank=bank, off=off):
                        c.wown(v, "vector")
                        c.w(v, ev[("dsmm", h, 0)])
                        ev[("dsevA", h)] = c.step("vector", v.tensor_copy(
                            dsout[:, h * 128:(h + 1) * 128],
                            bank[:, off:off + 128]))
                    c.on("vector", _evA)
                else:
                    def _evB(v, h=h, bank=bank, off=off):
                        c.wown(v, "vector")
                        c.w(v, ev[("dsmm", h, 1)])
                        ev[("dsevB", h)] = c.step("vector", v.tensor_tensor(
                            dsout[:, h * 128:(h + 1) * 128],
                            dsout[:, h * 128:(h + 1) * 128],
                            bank[:, off:off + 128], ALU.add))
                    c.on("vector", _evB)

        def _evz(v):
            c.wown(v, "vector")
            c.w(v, ev[("dsmm", 15, 1)])
            c.step("vector", v.tensor_copy(dzout[:, :], pp[4][:, 0:64:4]))
            c.wown(v, "vector")
            ev["dzev"] = c.step("vector", v.tensor_tensor(
                dzout[:, :], dzout[:, :], pp[4][:, 2:64:4], ALU.add))
        c.on("vector", _evz)
        free[("ppb", 0)] = ("dv", c.cnt["dv"])
        free[("ppb", 1)] = ("dv", c.cnt["dv"])

        def _dsst(e):
            c.w(e, ev["dzev"])
            ev["dsst"] = c.dma(e, "q_ds", ds_d[:, :, :],
                               dsout[:, :].rearrange("p (h e) -> p h e", e=128))
            ev["dzst"] = c.dma(e, "q_dz", dz_d[:, :], dzout[:, :])
            c.w(e, ev["dsst"])
            c.w(e, ev["dzst"])
        c.on("sync", _dsst)

    projs = [("v", wv_d, 2), ("g", wgt_d, 3), ("k", wk_d, 1), ("q", wq_d, 0)]
    PHN = {"v": 1, "g": 2, "k": 3, "q": 5}
    for pname, w_d, bi in projs:
        if A0A_PHASES < PHN[pname]:
            continue
        def _ldb(g, pname=pname, bi=bi):
            if ("pmm_last",) in ev:
                c.w(g, ev[("pmm_last",)])
            ev[("biasld", pname)] = c.dma(g, "q_bias", biasb[0:1, :],
                                          bias_d[bi:bi + 1, :])
        c.on("gpsimd", _ldb)
        for jb in range(4):
            def _ldw(g, pname=pname, w_d=w_d, jb=jb):
                slot = jb % 2
                c.w(g, free.get(("wb", slot)))
                src = w_d[:, jb * 512:(jb + 1) * 512].rearrange(
                    "(c p) j -> p c j", p=128)
                dst = wb[:, slot * 8192:(slot + 1) * 8192].rearrange(
                    "pp (c j) -> pp c j", j=512)
                ev[("w", pname, jb)] = c.dma(g, f"q_w{slot}", dst, src)
            c.on("gpsimd", _ldw)

            for tt in range(NT):
                bank = pp[tt % 2]

                def _mm(t, pname=pname, jb=jb, tt=tt, bank=bank, bi=bi):
                    if jb == 0 and tt == 0:
                        if pname == "v":
                            c.w(t, ev["zfm_ready"])
                            c.w(t, ev["consts"])
                        c.w(t, ev[("biasld", pname)])
                    c.w(t, ev[("w", pname, jb)])
                    c.w(t, free.get(("ppb", tt % 2)))
                    slot = jb % 2
                    for ch in range(16):
                        t.matmul(
                            bank[:, :],
                            zfm[:, ch * TPC + tt * 128: ch * TPC + (tt + 1) * 128],
                            wb[:, slot * 8192 + ch * 512:
                               slot * 8192 + (ch + 1) * 512],
                            start=(ch == 0), stop=False)
                    last = t.matmul(
                        bank[:, :], onesr[:, 0:128],
                        biasb[0:1, jb * 512:(jb + 1) * 512],
                        start=False, stop=True)
                    ev[("pmm", pname, jb, tt)] = c.step("tensor", last)
                    ev[("pmm_last",)] = ev[("pmm", pname, jb, tt)]
                    if tt == NT - 1:
                        free[("wb", jb % 2)] = ev[("pmm", pname, jb, tt)]
                c.on("tensor", _mm)

                if pname == "v":
                    def _ev(v, jb=jb, tt=tt, bank=bank):
                        c.wown(v, "vector")
                        c.w(v, ev[("pmm", "v", jb, tt)])
                        e = c.step("vector", v.tensor_copy(
                            vtm[:, tt * D + jb * 512: tt * D + (jb + 1) * 512],
                            bank[:, :]))
                        free[("ppb", tt % 2)] = e
                    c.on("vector", _ev)
                elif pname == "g":
                    def _ev(s, jb=jb, tt=tt, bank=bank):
                        c.wown(s, "scalar")
                        c.w(s, ev[("pmm", "g", jb, tt)])
                        e = c.step("scalar", s.activation(
                            gsl[:, tt * D + jb * 512: tt * D + (jb + 1) * 512],
                            bank[:, :], AF.Sigmoid))
                        ev[("pev", "g", jb, tt)] = e
                        free[("ppb", tt % 2)] = e
                    c.on("scalar", _ev)
                    if jb == 3:
                        def _stg(e, tt=tt):
                            c.w(e, ev[("pev", "g", 3, tt)])
                            ev[("gst", tt)] = c.dma(
                                e, f"q_g{tt % 2}",
                                gt_d[tt * 128:(tt + 1) * 128, :],
                                gsl[:, tt * D:(tt + 1) * D])
                        c.on("sync", _stg)
                else:
                    def _ev(v, pname=pname, jb=jb, tt=tt, bank=bank):
                        c.wown(v, "vector")
                        c.w(v, ev[("pmm", pname, jb, tt)])
                        if pname == "q" and jb == 0:
                            c.w(v, ev[("kst", tt)])
                        e = c.step("vector", v.tensor_copy(
                            qk[:, tt * D + jb * 512: tt * D + (jb + 1) * 512],
                            bank[:, :]))
                        ev[("pev", pname, jb, tt)] = e
                        free[("ppb", tt % 2)] = e
                    c.on("vector", _ev)

        if pname == "v":
            for tt in range(NT):
                def _stv(e, tt=tt):
                    c.w(e, free[("ppb", tt % 2)])
                    c.dma(e, f"q_v{tt % 2}", vt_d[tt * 128:(tt + 1) * 128, :],
                          vtm[:, tt * D:(tt + 1) * D])
                c.on("sync", _stv)
        if pname in ("q", "k"):
            for tt in range(NT):
                def _rope(v, pname=pname, tt=tt):
                    c.wown(v, "vector")
                    c.w(v, ev[("pev", pname, 3, tt)])
                    c.w(v, ev["rope"])
                    prev = (("eexp", pname, tt - 1) if tt > 0 else
                            (("eexp", "q", NT - 1) if pname == "k" else None))
                    if prev and prev in ev:
                        c.w(v, ev[prev])
                    base = tt * D
                    x1 = bass.AP(qk.ap().tensor, base,
                                 [[QKPP, 128], [128, 16], [2, 64]])
                    x2 = bass.AP(qk.ap().tensor, base + 1,
                                 [[QKPP, 128], [128, 16], [2, 64]])
                    cosb = bass.AP(ropes.ap().tensor, tt * 128,
                                   [[PPELEM, 128], [0, 16], [1, 64]])
                    sinb = bass.AP(ropes.ap().tensor, tt * 128 + 64,
                                   [[PPELEM, 128], [0, 16], [1, 64]])
                    t1v = tr1[:, :].rearrange("p (h i) -> p h i", i=64)
                    t2v = tr2[:, :].rearrange("p (h i) -> p h i", i=64)

                    def st(mk):
                        c.wown(v, "vector")
                        return c.step("vector", mk())
                    c.step("vector", v.tensor_tensor(t1v, x1, sinb, ALU.mult))
                    st(lambda: v.tensor_tensor(t2v, x2, sinb, ALU.mult))
                    st(lambda: v.tensor_tensor(x1, x1, cosb, ALU.mult))
                    st(lambda: v.tensor_tensor(x2, x2, cosb, ALU.mult))
                    st(lambda: v.tensor_tensor(x1, x1, t2v, ALU.subtract))
                    st(lambda: v.tensor_tensor(x2, x2, t1v, ALU.add))
                    row = qk[:, base:base + D]
                    ev[("ropem", pname, tt)] = st(
                        lambda: v.tensor_scalar(te1[:, :], row, 0.0, None,
                                                ALU.min))
                c.on("vector", _rope)

                def _eexp(s, pname=pname, tt=tt):
                    c.wown(s, "scalar")
                    c.w(s, ev[("ropem", pname, tt)])
                    ev[("eexp", pname, tt)] = c.step(
                        "scalar", s.activation(te2[:, :], te1[:, :], AF.Exp))
                c.on("scalar", _eexp)

                def _efin(v, pname=pname, tt=tt):
                    c.wown(v, "vector")
                    c.w(v, ev[("eexp", pname, tt)])
                    row = qk[:, tt * D:(tt + 1) * D]
                    c.step("vector", v.tensor_scalar(row, row, 0.0, None,
                                                     ALU.max))
                    c.wown(v, "vector")
                    ev[("elu", pname, tt)] = c.step(
                        "vector", v.tensor_tensor(row, row, te2[:, :], ALU.add))
                c.on("vector", _efin)

                if pname == "q":
                    for ch in range(16):
                        def _trq(t, tt=tt, ch=ch):
                            if ch == 0:
                                c.w(t, ev[("elu", "q", tt)])
                            if ("pt_free", ch % 2) in ev:
                                c.w(t, ("dv", ev[("pt_free", ch % 2)]))
                            ev[("qtr", tt, ch)] = c.step(
                                "tensor", t.transpose(
                                    pt[ch % 2][:, 0:128],
                                    qk[:, tt * D + ch * 128:
                                       tt * D + (ch + 1) * 128],
                                    identb[:, :]))
                        c.on("tensor", _trq)

                        def _evq(v, tt=tt, ch=ch):
                            c.wown(v, "vector")
                            c.w(v, ev[("qtr", tt, ch)])
                            c.step("vector", v.tensor_copy(
                                zfm[:, ch * TPC + tt * 128:
                                    ch * TPC + (tt + 1) * 128],
                                pt[ch % 2][:, 0:128]))
                            ev[("pt_free", ch % 2)] = c.cnt["dv"]
                            if ch == 15:
                                ev[("qfst_done", tt)] = ("dv", c.cnt["dv"])
                        c.on("vector", _evq)

                    if tt == NT - 1:
                        def _stq(e):
                            c.w(e, ev[("qfst_done", NT - 1)])
                            ev["qfstored"] = c.dma(
                                e, "q_qf",
                                qf_d.rearrange("(cc p) t -> p cc t", p=128),
                                zfm[:, :].rearrange("p (cc t) -> p cc t",
                                                    t=TPC))
                        c.on("sync", _stq)
                else:
                    def _stk(e, tt=tt):
                        c.w(e, ev[("elu", "k", tt)])
                        ev[("kst", tt)] = c.dma(
                            e, f"q_k{tt % 2}",
                            kt_d[tt * 128:(tt + 1) * 128, :],
                            qk[:, tt * D:(tt + 1) * D])
                    c.on("sync", _stk)
        if pname == "k" and A0A_PHASES >= 4:
            emit_ds()


    def _fin2(e):
        if "qfstored" in ev:
            c.w(e, ev["qfstored"])
        for tt in range(NT):
            if ("kst", tt) in ev:
                c.w(e, ev[("kst", tt)])
            if ("gst", tt) in ev:
                c.w(e, ev[("gst", tt)])
    c.on("sync", _fin2)

    c.close()
    for h in reversed(hold):
        h.__exit__(None, None, None)
    return nc



def build_attn0b():
    nc = bass.Bass("TRN2", target_bir_lowering=False)
    hold = []
    NT = TPC // 128
    NCH = TPC // 64
    x_d = nc.dram_tensor("x", [TPC, D], F32, kind="ExternalInput")
    qf_d = nc.dram_tensor("qf", [D, TPC], BF16, kind="ExternalInput")
    kt_d = nc.dram_tensor("kt", [TPC, D], BF16, kind="ExternalInput")
    vt_d = nc.dram_tensor("vt", [TPC, D], BF16, kind="ExternalInput")
    gt_d = nc.dram_tensor("gt", [TPC, D], BF16, kind="ExternalInput")
    dsS_d = nc.dram_tensor("dsS", [16, 128, 8, 128], F32, kind="ExternalInput")
    dsZ_d = nc.dram_tensor("dsZ", [128, 8, 16], F32, kind="ExternalInput")
    wo_d = nc.dram_tensor("wo", [D, D], F32, kind="ExternalInput")
    bo_d = nc.dram_tensor("bo", [1, D], F32, kind="ExternalInput")
    id_d = nc.dram_tensor("ident", [128, 128], F32, kind="ExternalInput")
    xo_d = nc.dram_tensor("xo", [TPC, D], F32, kind="ExternalOutput")

    qfm = sb(nc, hold, "qfm", [128, 16 * TPC], BF16)
    ktm = sb(nc, hold, "ktm", [128, NT * D], BF16)
    vtm = sb(nc, hold, "vtm", [128, NT * D], BF16)
    gtm = sb(nc, hold, "gtm", [128, NT * D], BF16)
    otm = sb(nc, hold, "otm", [128, NT * D], BF16)
    ssf = sb(nc, hold, "ssf", [128, 16 * 129])
    ssb = sb(nc, hold, "ssb", [128, 16 * 129], BF16)
    dsl = sb(nc, hold, "dsl", [128, 2 * 8 * 128])
    dzl = sb(nc, hold, "dzl", [128, 8 * 16])
    wob = sb(nc, hold, "wob", [128, 2 * 512], BF16)
    bob = sb(nc, hold, "bob", [1, D], BF16)
    onesr = sb(nc, hold, "onesr", [1, 128], BF16)
    onesc = sb(nc, hold, "onesc", [128, 1], BF16)
    xsl = sb(nc, hold, "xsl", [128, 2 * 512])
    dent = sb(nc, hold, "dent", [128, 2])
    identb = sb(nc, hold, "identb", [128, 128], BF16)

    pp = [psum(nc, hold, f"pp{i}", [128, 512]) for i in range(6)]
    pt = [psum(nc, hold, f"pt{i}", [128, 1024], BF16) for i in range(2)]

    c = Ctx(nc)
    ev = {}
    free = {}

    def _pre(g):
        ev["ident"] = c.dma(g, "q_id", identb[:, :], id_d[:, :])
        ev["bo"] = c.dma(g, "q_bo", bob[:, :], bo_d[:, :])
        ev["qfm"] = c.dma(g, "q_qf", qfm[:, :].rearrange(
            "p (cc t) -> p cc t", t=TPC),
            qf_d.rearrange("(cc p) t -> p cc t", p=128))
        ev["ktm"] = c.dma(g, "q_kt", ktm[:, :].rearrange(
            "p (tt j) -> p tt j", j=D),
            kt_d.rearrange("(tt p) j -> p tt j", p=128))
        ev["vtm"] = c.dma(g, "q_vt", vtm[:, :].rearrange(
            "p (tt j) -> p tt j", j=D),
            vt_d.rearrange("(tt p) j -> p tt j", p=128))
        ev["gtm"] = c.dma(g, "q_gt", gtm[:, :].rearrange(
            "p (tt j) -> p tt j", j=D),
            gt_d.rearrange("(tt p) j -> p tt j", p=128))
    c.on("gpsimd", _pre)

    def _pre_v(v):
        c.step("vector", v.memset(onesr[:, :], 1.0))
        c.wown(v, "vector")
        ev["ones"] = c.step("vector", v.memset(onesc[:, :], 1.0))
    c.on("vector", _pre_v)

    for h in range(16):
        def _ldh(g, h=h):
            c.w(g, ev.get(("pfdone", h - 2)))
            ev[("dslh", h)] = c.dma(
                g, f"q_dsl{h % 2}",
                dsl[:, (h % 2) * 1024:(h % 2 + 1) * 1024].rearrange(
                    "p (r e) -> p r e", e=128),
                dsS_d[h])
        c.on("gpsimd", _ldh)

        def _sumh(v, h=h):
            c.wown(v, "vector")
            c.w(v, ev[("dslh", h)])
            base = (h % 2) * 1024
            c.step("vector", v.tensor_tensor(
                ssf[:, h * 129: h * 129 + 128], dsl[:, base:base + 128],
                dsl[:, base + 128:base + 256], ALU.add))
            for r in range(2, 8):
                c.wown(v, "vector")
                e = c.step("vector", v.tensor_tensor(
                    ssf[:, h * 129: h * 129 + 128],
                    ssf[:, h * 129: h * 129 + 128],
                    dsl[:, base + r * 128: base + (r + 1) * 128], ALU.add))
            ev[("pfdone", h)] = e
        c.on("vector", _sumh)

    def _ldz(g):
        c.w(g, ev.get(("pfdone", 14)))
        c.w(g, ev.get(("pfdone", 15)))
        ev["dzl"] = c.dma(g, "q_dzl",
                          dzl[:, :].rearrange("p (r hh) -> p r hh", hh=16),
                          dsZ_d[:, :, :])
    c.on("gpsimd", _ldz)

    def _sumz(v):
        c.wown(v, "vector")
        c.w(v, ev["dzl"])
        stg = dsl[:, 0:16]
        c.step("vector", v.tensor_tensor(stg, dzl[:, 0:16], dzl[:, 16:32],
                                         ALU.add))
        for r in range(2, 8):
            c.wown(v, "vector")
            c.step("vector", v.tensor_tensor(
                stg, stg, dzl[:, r * 16:(r + 1) * 16], ALU.add))
        c.wown(v, "vector")
        zb = bass.AP(ssf.ap().tensor, 128, [[16 * 129, 128], [129, 16]])
        ev["zpf"] = c.step("vector", v.tensor_copy(zb, stg))
    c.on("vector", _sumz)

    for cch in range(NCH):
        trow = (cch % 2) * 64
        ttj = cch // 2

        def _smm(t, cch=cch, trow=trow, ttj=ttj):
            if cch == 0:
                c.w(t, ev["ktm"])
                c.w(t, ev["vtm"])
                c.w(t, ev["ones"])
            else:
                c.w(t, ev[("sadd", cch - 1)])
            for h in range(16):
                sbank = pp[h % 4]
                soff = (h // 4) * 128
                lhs = ktm[trow:trow + 64,
                          ttj * D + h * 128: ttj * D + (h + 1) * 128]
                rhs = vtm[trow:trow + 64,
                          ttj * D + h * 128: ttj * D + (h + 1) * 128]
                t.matmul(sbank[:, soff:soff + 128], lhs, rhs,
                         start=True, stop=True)
                last = t.matmul(pp[4][:, 2 * h:2 * h + 1], lhs,
                                onesc[trow:trow + 64, :],
                                start=True, stop=True)
            ev[("smm", cch)] = c.step("tensor", last)
        c.on("tensor", _smm)

        def _sadd(v, cch=cch):
            c.wown(v, "vector")
            c.w(v, ev[("smm", cch)])
            if cch == 0:
                for h in range(16):
                    c.w(v, ev[("pfdone", h)])
                c.w(v, ev["zpf"])
            for h in range(16):
                sbank = pp[h % 4]
                soff = (h // 4) * 128
                c.step("vector", v.tensor_tensor(
                    ssf[:, h * 129: h * 129 + 128],
                    ssf[:, h * 129: h * 129 + 128],
                    sbank[:, soff:soff + 128], ALU.add))
                c.wown(v, "vector")
            zb = bass.AP(ssf.ap().tensor, 128, [[16 * 129, 128], [129, 16]])
            e = c.step("vector", v.tensor_tensor(zb, zb, pp[4][:, 0:32:2],
                                                 ALU.add))
            ev[("sadd", cch)] = e
        c.on("vector", _sadd)

        def _scast(s, cch=cch):
            c.wown(s, "scalar")
            c.w(s, ev[("sadd", cch)])
            if cch > 0:
                c.w(s, ev[("nmm", cch - 1, 15)])
            ev[("scast", cch)] = c.step(
                "scalar", s.copy(ssb[:, :], ssf[:, :]))
        c.on("scalar", _scast)

        for h in range(16):
            nb = pp[5]

            def _nmm(t, cch=cch, h=h, nb=nb, trow=trow):
                if cch == 0 and h == 0:
                    c.w(t, ev["qfm"])
                c.w(t, ev[("scast", cch)])
                c.w(t, free.get(("nb", 0)))
                lhs = qfm[:, h * TPC + cch * 64: h * TPC + (cch + 1) * 64]
                last = t.matmul(nb[trow:trow + 64, 0:129], lhs,
                                ssb[:, h * 129:(h + 1) * 129],
                                start=True, stop=True)
                ev[("nmm", cch, h)] = c.step("tensor", last)
            c.on("tensor", _nmm)

            def _nev(v, cch=cch, h=h, nb=nb, trow=trow, ttj=ttj):
                c.wown(v, "vector")
                c.w(v, ev[("nmm", cch, h)])
                if cch == 0 and h == 0:
                    c.w(v, ev["gtm"])
                c.step("vector", v.tensor_scalar(
                    dent[trow:trow + 64, 0:1], nb[trow:trow + 64, 128:129],
                    1e-6, None, ALU.add))
                c.wown(v, "vector")
                c.step("vector", v.reciprocal(dent[trow:trow + 64, 1:2],
                                              dent[trow:trow + 64, 0:1]))
                c.wown(v, "vector")
                osl = otm[trow:trow + 64,
                          ttj * D + h * 128: ttj * D + (h + 1) * 128]
                gsl = gtm[trow:trow + 64,
                          ttj * D + h * 128: ttj * D + (h + 1) * 128]
                e = c.step("vector", v.scalar_tensor_tensor(
                    osl, nb[trow:trow + 64, 0:128], dent[trow:trow + 64, 1:2],
                    gsl, ALU.mult, ALU.mult))
                ev[("oev", cch, h)] = e
                free[("nb", 0)] = e
            c.on("vector", _nev)

    for tt in range(NT):
        for ch in range(16):
            def _tro(t, tt=tt, ch=ch):
                if tt == 0 and ch == 0:
                    c.w(t, ev[("oev", NCH - 1, 15)])
                    c.w(t, ev["ident"])
                if ("pt_free", ch % 2) in ev:
                    c.w(t, ("dv", ev[("pt_free", ch % 2)]))
                ev[("otr", tt, ch)] = c.step(
                    "tensor", t.transpose(
                        pt[ch % 2][:, 0:128],
                        otm[:, tt * D + ch * 128: tt * D + (ch + 1) * 128],
                        identb[:, :]))
            c.on("tensor", _tro)

            def _evo(v, tt=tt, ch=ch):
                c.wown(v, "vector")
                c.w(v, ev[("otr", tt, ch)])
                c.step("vector", v.tensor_copy(
                    qfm[:, ch * TPC + tt * 128: ch * TPC + (tt + 1) * 128],
                    pt[ch % 2][:, 0:128]))
                ev[("pt_free", ch % 2)] = c.cnt["dv"]
            c.on("vector", _evo)
    ev["ofm_done"] = ("dv", c.cnt["dv"])

    NGRP = (NT + 3) // 4
    for half in range(NGRP):
        tts = list(range(half * 4, min(NT, half * 4 + 4)))
        for db in range(4):
            for ch in range(16):
                def _ldwo(g, half=half, db=db, ch=ch):
                    k = (half * 4 + db) * 16 + ch
                    slot = k % 2
                    c.w(g, free.get(("wob", slot)))
                    src = wo_d[ch * 128:(ch + 1) * 128,
                               db * 512:(db + 1) * 512]
                    ev[("wo", half, db, ch)] = c.dma(
                        g, f"q_wo{slot}", wob[:, slot * 512:(slot + 1) * 512],
                        src)
                c.on("gpsimd", _ldwo)

                def _wmm(t, half=half, db=db, ch=ch):
                    k = (half * 4 + db) * 16 + ch
                    slot = k % 2
                    c.w(t, ev[("wo", half, db, ch)])
                    if ch == 0:
                        c.w(t, ev["ofm_done"])
                        c.w(t, ev["bo"])
                        for i4 in range(4):
                            c.w(t, free.get(("wop", i4)))
                    last = None
                    for i4, tt in enumerate(tts):
                        last = t.matmul(
                            pp[i4][:, :],
                            qfm[:, ch * TPC + tt * 128: ch * TPC + (tt + 1) * 128],
                            wob[:, slot * 512:(slot + 1) * 512],
                            start=(ch == 0), stop=False)
                        if ch == 15:
                            last = t.matmul(
                                pp[i4][:, :], onesr[:, 0:128],
                                bob[0:1, db * 512:(db + 1) * 512],
                                start=False, stop=True)
                    ev[("wmm", half, db, ch)] = c.step("tensor", last)
                    free[("wob", slot)] = ev[("wmm", half, db, ch)]
                c.on("tensor", _wmm)

            for i4, tt in enumerate(tts):
                def _ldx(e, half=half, db=db, i4=i4, tt=tt):
                    c.w(e, free.get(("xsl", i4 % 2)))
                    ev[("xld", half, db, i4)] = c.dma(
                        e, f"q_x{i4 % 2}", xsl[:, (i4 % 2) * 512:(i4 % 2 + 1) * 512],
                        x_d[tt * 128:(tt + 1) * 128, db * 512:(db + 1) * 512])
                c.on("sync", _ldx)

                def _res(v, half=half, db=db, i4=i4, tt=tt):
                    c.wown(v, "vector")
                    c.w(v, ev[("wmm", half, db, 15)])
                    c.w(v, ev[("xld", half, db, i4)])
                    e = c.step("vector", v.tensor_tensor(
                        xsl[:, (i4 % 2) * 512:(i4 % 2 + 1) * 512],
                        xsl[:, (i4 % 2) * 512:(i4 % 2 + 1) * 512],
                        pp[i4][:, :], ALU.add))
                    ev[("res", half, db, i4)] = e
                    free[("wop", i4)] = e
                c.on("vector", _res)

                def _sto(e, half=half, db=db, i4=i4, tt=tt):
                    c.w(e, ev[("res", half, db, i4)])
                    ee = c.dma(e, f"q_xo{i4 % 2}",
                               xo_d[tt * 128:(tt + 1) * 128,
                                    db * 512:(db + 1) * 512],
                               xsl[:, (i4 % 2) * 512:(i4 % 2 + 1) * 512])
                    free[("xsl", i4 % 2)] = ee
                    ev[("sto", half, db, i4)] = ee
                c.on("sync", _sto)

    def _fin(e):
        for i4 in range(len(list(range((NGRP - 1) * 4, min(NT, NGRP * 4))))):
            c.w(e, ev[("sto", NGRP - 1, 3, i4)])
    c.on("sync", _fin)

    c.close()
    for h in reversed(hold):
        h.__exit__(None, None, None)
    return nc



def build_ffn1a():
    nc = bass.Bass("TRN2", target_bir_lowering=False)
    hold = []
    xr_d = nc.dram_tensor("xr", [256, D], F32, kind="ExternalInput")
    swt_d = nc.dram_tensor("swt", [8, D, 256], F32, kind="ExternalInput")
    id_d = nc.dram_tensor("ident", [128, 128], F32, kind="ExternalInput")
    pl_d = nc.dram_tensor("pl", [1, 8], F32, kind="ExternalOutput")

    xs = sb(nc, hold, "xs", [128, D])
    ztmp = sb(nc, hold, "ztmp", [128, D], BF16)
    zr = sb(nc, hold, "zr", [128, 16 * 256], BF16)
    swb = sb(nc, hold, "swb", [128, 2 * 16 * 256], BF16)
    prt = sb(nc, hold, "prt", [128, 8])
    dum = sb(nc, hold, "dum", [128, 256])
    acc1 = sb(nc, hold, "acc1", [128, 2])
    onesc = sb(nc, hold, "onesc", [128, 1])
    row = sb(nc, hold, "row", [1, 128])
    lg = sb(nc, hold, "lg", [1, 8])
    bn6 = sb(nc, hold, "bn6", [128, 24])
    mv = sb(nc, hold, "mv", [128, 2])
    sd = sb(nc, hold, "sd", [128, 1])
    rstd = sb(nc, hold, "rstd", [128, 1])
    epst = sb(nc, hold, "epst", [128, 1])
    identb = sb(nc, hold, "identb", [128, 128], BF16)

    pp = [psum(nc, hold, f"pp{i}", [128, 512]) for i in range(2)]
    pt = [psum(nc, hold, f"pt{i}", [128, 1024], BF16) for i in range(2)]

    c = Ctx(nc)
    ev = {}

    def _pre(g):
        ev["ident"] = c.dma(g, "q_id", identb[:, :], id_d[:, :])
    c.on("gpsimd", _pre)

    def _pre_v(v):
        c.step("vector", v.memset(epst[:, :], LN_EPS))
        c.wown(v, "vector")
        ev["ones"] = c.step("vector", v.memset(onesc[:, :], 1.0))
    c.on("vector", _pre_v)

    for idx in range(2):
        def _ld(e, idx=idx):
            if idx > 0:
                c.w(e, ev[("z", idx - 1)])
            ev[("xld", idx)] = c.dma(e, "q_x", xs[:, :],
                                     xr_d[idx * 128:(idx + 1) * 128, :])
        c.on("sync", _ld)

        def _stats(v, idx=idx):
            c.wown(v, "vector")
            c.w(v, ev[("xld", idx)])
            for k4 in range(4):
                c.step("vector", v.bn_stats(bn6[:, k4 * 6:(k4 + 1) * 6],
                                            xs[:, k4 * 512:(k4 + 1) * 512]))
            c.wown(v, "vector")
            c.w(v, ev.get(("sd", idx - 1)))
            ev[("mv", idx)] = c.step("vector", v.bn_aggr(mv[:, :], bn6[:, :]))
        c.on("vector", _stats)

        def _sqrt(s, idx=idx):
            c.wown(s, "scalar")
            c.w(s, ev[("mv", idx)])
            ev[("sd", idx)] = c.step(
                "scalar", s.activation(sd[:, :], mv[:, 1:2], AF.Sqrt,
                                       bias=epst[:, :]))
        c.on("scalar", _sqrt)

        def _apply(v, idx=idx):
            c.wown(v, "vector")
            c.w(v, ev[("sd", idx)])
            c.step("vector", v.reciprocal(rstd[:, :], sd[:, :]))
            c.wown(v, "vector")
            if ("ztr_done", idx - 1) in ev:
                c.w(v, ("pe", ev[("ztr_done", idx - 1)]))
            ev[("z", idx)] = c.step(
                "vector", v.tensor_scalar(ztmp[:, :], xs[:, :], mv[:, 0:1],
                                          rstd[:, :], ALU.subtract, ALU.mult))
        c.on("vector", _apply)

        for ch in range(16):
            def _trc(t, idx=idx, ch=ch):
                if ch == 0:
                    c.w(t, ev[("z", idx)])
                    c.w(t, ev["ident"])
                if ("pt_free", ch % 2) in ev:
                    c.w(t, ("dv", ev[("pt_free", ch % 2)]))
                ev[("ztrp", idx, ch)] = c.step(
                    "tensor", t.transpose(pt[ch % 2][:, 0:128],
                                          ztmp[:, ch * 128:(ch + 1) * 128],
                                          identb[:, :]))
            c.on("tensor", _trc)

            def _evc(v, idx=idx, ch=ch):
                c.wown(v, "vector")
                c.w(v, ev[("ztrp", idx, ch)])
                c.step("vector", v.tensor_copy(
                    zr[:, ch * 256 + idx * 128: ch * 256 + (idx + 1) * 128],
                    pt[ch % 2][:, 0:128]))
                ev[("pt_free", ch % 2)] = c.cnt["dv"]
            c.on("vector", _evc)
        ev[("ztr_done", idx)] = c.cnt["pe"]

    for e8 in range(8):
        def _ldsw(g, e8=e8):
            if ("swused", e8 - 2) in ev:
                c.w(g, ev[("swused", e8 - 2)])
            src = swt_d[e8].rearrange("(cc p) t -> p cc t", p=128)
            dst = swb[:, (e8 % 2) * 4096:(e8 % 2 + 1) * 4096].rearrange(
                "pp (cc t) -> pp cc t", t=256)
            ev[("sw", e8)] = c.dma(g, f"q_sw{e8 % 2}", dst, src)
        c.on("gpsimd", _ldsw)

        def _ttr(v, e8=e8):
            c.wown(v, "vector")
            c.w(v, ev[("sw", e8)])
            c.w(v, ("dv", ev[("pt_free", 0)]))
            c.w(v, ("dv", ev[("pt_free", 1)]))
            for cc in range(16):
                c.wown(v, "vector")
                c.step("vector", v.tensor_tensor(
                    dum[:, :], zr[:, cc * 256:(cc + 1) * 256],
                    swb[:, (e8 % 2) * 4096 + cc * 256:
                        (e8 % 2) * 4096 + (cc + 1) * 256], ALU.mult))
                c.wown(v, "vector")
                if cc == 0:
                    c.step("vector", v.tensor_reduce(
                        acc1[:, 0:1], dum[:, :], AX.X, ALU.add))
                else:
                    c.step("vector", v.tensor_reduce(
                        acc1[:, 1:2], dum[:, :], AX.X, ALU.add))
                    c.wown(v, "vector")
                    c.step("vector", v.tensor_tensor(
                        acc1[:, 0:1], acc1[:, 0:1], acc1[:, 1:2], ALU.add))
            c.wown(v, "vector")
            e = c.step("vector", v.tensor_copy(prt[:, e8:e8 + 1], acc1[:, 0:1]))
            ev[("swused", e8)] = e
        c.on("vector", _ttr)

    def _sum(t):
        c.w(t, ev[("swused", 7)])
        c.w(t, ev["ones"])
        ev["summ"] = c.step("tensor", t.matmul(pp[0][0:1, 0:8], onesc[:, :],
                                               prt[:, :], start=True, stop=True))
    c.on("tensor", _sum)

    def _fin(v):
        c.wown(v, "vector")
        c.w(v, ev["summ"])
        ev["lg"] = c.step("vector", v.tensor_copy(lg[0:1, :], pp[0][0:1, 0:8]))
    c.on("vector", _fin)

    def _st(e):
        c.w(e, ev["lg"])
        ee = c.dma(e, "q_pl", pl_d[:, :], lg[0:1, :])
        c.w(e, ee)
    c.on("sync", _st)

    c.close()
    for h in reversed(hold):
        h.__exit__(None, None, None)
    return nc



def build_ffn1b():
    nc = bass.Bass("TRN2", target_bir_lowering=False)
    hold = []
    NT = TPC // 128
    x_d = nc.dram_tensor("x", [TPC, D], F32, kind="ExternalInput")
    pl8_d = nc.dram_tensor("pl8", [8, 8], F32, kind="ExternalInput")
    swb_d = nc.dram_tensor("swb", [1, 8], F32, kind="ExternalInput")
    w1_d = nc.dram_tensor("w1h", [128, 8, 128], F32, kind="ExternalInput")
    b1_d = nc.dram_tensor("b1h", [128, 8], F32, kind="ExternalInput")
    w2_d = nc.dram_tensor("w2h", [128, 8, 128], F32, kind="ExternalInput")
    b2_d = nc.dram_tensor("b2h", [128, 8], F32, kind="ExternalInput")
    w3_d = nc.dram_tensor("w3t", [D, D], F32, kind="ExternalInput")
    b3_d = nc.dram_tensor("b3", [1, D], F32, kind="ExternalInput")
    id_d = nc.dram_tensor("ident", [128, 128], F32, kind="ExternalInput")
    xo_d = nc.dram_tensor("xo", [TPC, D], F32, kind="ExternalOutput")

    xs = sb(nc, hold, "xs", [128, D])
    ztmp = sb(nc, hold, "ztmp", [128, D], BF16)
    zfm = sb(nc, hold, "zfm", [128, 16 * TPC], BF16)
    w1b = sb(nc, hold, "w1b", [128, 8 * 128], BF16)
    w2b = sb(nc, hold, "w2b", [128, 8 * 128], BF16)
    b1b = sb(nc, hold, "b1b", [128, 8])
    b2b = sb(nc, hold, "b2b", [128, 8])
    w1s = sb(nc, hold, "w1s", [128, 2 * 128], BF16)
    w2s = sb(nc, hold, "w2s", [128, 2 * 128], BF16)
    b1s = sb(nc, hold, "b1s", [128, 2])
    b2s = sb(nc, hold, "b2s", [128, 3])
    selB = sb(nc, hold, "selB", [128, 32])
    pl8s = sb(nc, hold, "pl8s", [8, 8])
    dum8 = sb(nc, hold, "dum8", [128, 8])
    lgr = sb(nc, hold, "lgr", [1, 48])
    onesc = sb(nc, hold, "onesc", [128, 1], BF16)
    onesr = sb(nc, hold, "onesr", [1, 128], BF16)
    ones8 = sb(nc, hold, "ones8", [8, 1])
    onesrf = sb(nc, hold, "onesrf", [1, 128])
    hbuf = sb(nc, hold, "hbuf", [128, 2 * 512], BF16)
    w3b = sb(nc, hold, "w3b", [128, 2 * 512], BF16)
    b3b = sb(nc, hold, "b3b", [1, D], BF16)
    xsl = sb(nc, hold, "xsl", [128, 2 * 512])
    bn6 = sb(nc, hold, "bn6", [128, 24])
    mv = sb(nc, hold, "mv", [128, 2])
    sd = sb(nc, hold, "sd", [128, 1])
    rstd = sb(nc, hold, "rstd", [128, 1])
    epst = sb(nc, hold, "epst", [128, 1])
    identb = sb(nc, hold, "identb", [128, 128], BF16)

    pw = [psum(nc, hold, f"pw{i}", [128, 512]) for i in range(6)]
    pt = [psum(nc, hold, f"pt{i}", [128, 1024], BF16) for i in range(2)]

    c = Ctx(nc)
    ev = {}
    free = {}

    def _pre(g):
        ev["ident"] = c.dma(g, "q_id", identb[:, :], id_d[:, :])
        ev["pl8"] = c.dma(g, "q_pl", pl8s[:, :], pl8_d[:, :])
        ev["swb"] = c.dma(g, "q_swb", lgr[0:1, 40:48], swb_d[:, :])
        ev["w1"] = c.dma(g, "q_w1", w1b[:, :].rearrange(
            "p (e f) -> p e f", f=128), w1_d[:, :, :])
        ev["w2"] = c.dma(g, "q_w2", w2b[:, :].rearrange(
            "p (e f) -> p e f", f=128), w2_d[:, :, :])
        ev["b1"] = c.dma(g, "q_b1", b1b[:, :], b1_d[:, :])
        ev["b2"] = c.dma(g, "q_b2", b2b[:, :], b2_d[:, :])
        ev["b3"] = c.dma(g, "q_b3", b3b[:, :], b3_d[:, :])
    c.on("gpsimd", _pre)

    def _pre_v(v):
        c.step("vector", v.memset(epst[:, :], LN_EPS))
        c.wown(v, "vector")
        c.step("vector", v.memset(onesc[:, :], 1.0))
        c.wown(v, "vector")
        c.step("vector", v.memset(onesr[:, :], 1.0))
        c.wown(v, "vector")
        c.step("vector", v.memset(onesrf[:, :], 1.0))
        c.wown(v, "vector")
        ev["ones"] = c.step("vector", v.memset(ones8[:, :], 1.0))
    c.on("vector", _pre_v)

    def _lmm(t):
        c.w(t, ev["pl8"])
        c.w(t, ev["ones"])
        ev["lmm"] = c.step("tensor", t.matmul(
            pw[0][0:1, 0:8], ones8[:, :], pl8s[:, :], start=True, stop=True))
    c.on("tensor", _lmm)

    def _route(v):
        LG, M1, NM1, PU, S1, RS, PR, SEL1, MSK, M2, SEL2, G1, G2, C2A, C2B, DU = (
            0, 8, 9, 10, 18, 19, 20, 28, 36, 44, 45, 46, 47, 28, 36, 40)
        c.wown(v, "vector")
        c.w(v, ev["lmm"])
        c.w(v, ev["swb"])
        c.step("vector", v.tensor_tensor(lgr[0:1, 0:8], pw[0][0:1, 0:8],
                                         lgr[0:1, 40:48], ALU.add))
        c.wown(v, "vector")
        c.step("vector", v.tensor_reduce(lgr[0:1, 8:9], lgr[0:1, 0:8], AX.X, ALU.max))
        c.wown(v, "vector")
        c.step("vector", v.tensor_scalar(lgr[0:1, 9:10], lgr[0:1, 8:9], -1.0,
                                         None, ALU.mult))
        c.wown(v, "vector")
        ev["r1"] = c.cnt["dv"]
    c.on("vector", _route)

    def _rexp(s):
        c.wown(s, "scalar")
        c.w(s, ("dv", ev["r1"]))
        ev["rexp"] = c.step("scalar", s.activation(
            lgr[0:1, 10:18], lgr[0:1, 0:8], AF.Exp, bias=lgr[0:1, 9:10]))
    c.on("scalar", _rexp)

    def _route2(v):
        c.wown(v, "vector")
        c.w(v, ev["rexp"])
        c.step("vector", v.tensor_reduce(lgr[0:1, 18:19], lgr[0:1, 10:18],
                                         AX.X, ALU.add))
        c.wown(v, "vector")
        c.step("vector", v.reciprocal(lgr[0:1, 19:20], lgr[0:1, 18:19]))
        c.wown(v, "vector")
        c.step("vector", v.tensor_scalar(lgr[0:1, 20:28], lgr[0:1, 10:18],
                                         lgr[0:1, 19:20], None, ALU.mult))
        c.wown(v, "vector")
        c.step("vector", v.tensor_scalar(lgr[0:1, 28:36], lgr[0:1, 0:8],
                                         lgr[0:1, 8:9], None, ALU.is_ge))
        c.wown(v, "vector")
        c.step("vector", v.scalar_tensor_tensor(
            lgr[0:1, 36:44], lgr[0:1, 28:36], -1e30, lgr[0:1, 0:8],
            ALU.mult, ALU.add))
        c.wown(v, "vector")
        c.step("vector", v.tensor_reduce(lgr[0:1, 44:45], lgr[0:1, 36:44],
                                         AX.X, ALU.max))
        c.wown(v, "vector")
        c.step("vector", v.tensor_scalar(lgr[0:1, 36:44], lgr[0:1, 36:44],
                                         lgr[0:1, 44:45], None, ALU.is_ge))
        c.wown(v, "vector")
        c.step("vector", v.tensor_tensor(lgr[0:1, 10:18], lgr[0:1, 20:28],
                                         lgr[0:1, 28:36], ALU.mult))
        c.wown(v, "vector")
        c.step("vector", v.tensor_reduce(lgr[0:1, 45:46], lgr[0:1, 10:18],
                                         AX.X, ALU.add))
        c.wown(v, "vector")
        c.step("vector", v.tensor_tensor(lgr[0:1, 10:18], lgr[0:1, 20:28],
                                         lgr[0:1, 36:44], ALU.mult))
        c.wown(v, "vector")
        c.step("vector", v.tensor_reduce(lgr[0:1, 46:47], lgr[0:1, 10:18],
                                         AX.X, ALU.add))
        c.wown(v, "vector")
        c.step("vector", v.tensor_scalar(lgr[0:1, 10:18], lgr[0:1, 28:36],
                                         lgr[0:1, 45:46], None, ALU.mult))
        c.wown(v, "vector")
        ev["route"] = c.step("vector", v.tensor_scalar(
            lgr[0:1, 18:26], lgr[0:1, 36:44], lgr[0:1, 46:47], None, ALU.mult))
    c.on("vector", _route2)

    def _bcast(t):
        c.w(t, ev["route"])
        last = t.matmul(pw[1][:, 0:8], onesrf[:, :], lgr[0:1, 28:36],
                        start=True, stop=True)
        last = t.matmul(pw[1][:, 8:16], onesrf[:, :], lgr[0:1, 10:18],
                        start=True, stop=True)
        last = t.matmul(pw[1][:, 16:24], onesrf[:, :], lgr[0:1, 36:44],
                        start=True, stop=True)
        last = t.matmul(pw[1][:, 24:32], onesrf[:, :], lgr[0:1, 18:26],
                        start=True, stop=True)
        ev["bcast"] = c.step("tensor", last)
    c.on("tensor", _bcast)

    def _selc(v):
        c.wown(v, "vector")
        c.w(v, ev["bcast"])
        ev["selB"] = c.step("vector", v.tensor_copy(selB[:, :], pw[1][:, 0:32]))
    c.on("vector", _selc)

    def _comb(v):
        c.wown(v, "vector")
        c.w(v, ev["selB"])
        c.w(v, ev["w1"])
        c.w(v, ev["w2"])
        c.w(v, ev["b1"])
        c.w(v, ev["b2"])
        for k2 in range(2):
            sc = 0 if k2 == 0 else 16
            cc = 8 if k2 == 0 else 24
            c.step("vector", v.tensor_scalar(
                w1s[:, k2 * 128:(k2 + 1) * 128], w1b[:, 0:128],
                selB[:, sc:sc + 1], None, ALU.mult))
            for e8 in range(1, 8):
                c.wown(v, "vector")
                c.step("vector", v.scalar_tensor_tensor(
                    w1s[:, k2 * 128:(k2 + 1) * 128],
                    w1b[:, e8 * 128:(e8 + 1) * 128], selB[:, sc + e8:sc + e8 + 1],
                    w1s[:, k2 * 128:(k2 + 1) * 128], ALU.mult, ALU.add))
            c.wown(v, "vector")
            c.step("vector", v.tensor_scalar(
                w2s[:, k2 * 128:(k2 + 1) * 128], w2b[:, 0:128],
                selB[:, cc:cc + 1], None, ALU.mult))
            for e8 in range(1, 8):
                c.wown(v, "vector")
                c.step("vector", v.scalar_tensor_tensor(
                    w2s[:, k2 * 128:(k2 + 1) * 128],
                    w2b[:, e8 * 128:(e8 + 1) * 128], selB[:, cc + e8:cc + e8 + 1],
                    w2s[:, k2 * 128:(k2 + 1) * 128], ALU.mult, ALU.add))
            c.wown(v, "vector")
            c.step("vector", v.tensor_tensor(dum8[:, :], b1b[:, :],
                                             selB[:, sc:sc + 8], ALU.mult))
            c.wown(v, "vector")
            c.step("vector", v.tensor_reduce(b1s[:, k2:k2 + 1], dum8[:, :],
                                             AX.X, ALU.add))
            c.wown(v, "vector")
            c.step("vector", v.tensor_tensor(dum8[:, :], b2b[:, :],
                                             selB[:, cc:cc + 8], ALU.mult))
            c.wown(v, "vector")
            c.step("vector", v.tensor_reduce(b2s[:, k2:k2 + 1], dum8[:, :],
                                             AX.X, ALU.add))
            c.wown(v, "vector")
        ev["wsel"] = c.step("vector", v.tensor_tensor(
            b2s[:, 2:3], b2s[:, 0:1], b2s[:, 1:2], ALU.add))
    c.on("vector", _comb)

    for idx in range(NT):
        def _ld(e, idx=idx):
            if idx > 0:
                c.w(e, ev[("z", idx - 1)])
            ev[("xld", idx)] = c.dma(e, "q_x", xs[:, :],
                                     x_d[idx * 128:(idx + 1) * 128, :])
        c.on("sync", _ld)

        def _stats(v, idx=idx):
            c.wown(v, "vector")
            c.w(v, ev[("xld", idx)])
            for k4 in range(4):
                c.step("vector", v.bn_stats(bn6[:, k4 * 6:(k4 + 1) * 6],
                                            xs[:, k4 * 512:(k4 + 1) * 512]))
            c.wown(v, "vector")
            c.w(v, ev.get(("sd", idx - 1)))
            ev[("mv", idx)] = c.step("vector", v.bn_aggr(mv[:, :], bn6[:, :]))
        c.on("vector", _stats)

        def _sqrt(s, idx=idx):
            c.wown(s, "scalar")
            c.w(s, ev[("mv", idx)])
            ev[("sd", idx)] = c.step(
                "scalar", s.activation(sd[:, :], mv[:, 1:2], AF.Sqrt,
                                       bias=epst[:, :]))
        c.on("scalar", _sqrt)

        def _apply(v, idx=idx):
            c.wown(v, "vector")
            c.w(v, ev[("sd", idx)])
            c.step("vector", v.reciprocal(rstd[:, :], sd[:, :]))
            c.wown(v, "vector")
            if ("ztr_done", idx - 1) in ev:
                c.w(v, ("pe", ev[("ztr_done", idx - 1)]))
            ev[("z", idx)] = c.step(
                "vector", v.tensor_scalar(ztmp[:, :], xs[:, :], mv[:, 0:1],
                                          rstd[:, :], ALU.subtract, ALU.mult))
        c.on("vector", _apply)

        for ch in range(16):
            def _trc(t, idx=idx, ch=ch):
                if ch == 0:
                    c.w(t, ev[("z", idx)])
                    c.w(t, ev["ident"])
                if ("pt_free", ch % 2) in ev:
                    c.w(t, ("dv", ev[("pt_free", ch % 2)]))
                ev[("ztrp", idx, ch)] = c.step(
                    "tensor", t.transpose(pt[ch % 2][:, 0:128],
                                          ztmp[:, ch * 128:(ch + 1) * 128],
                                          identb[:, :]))
            c.on("tensor", _trc)

            def _evc(v, idx=idx, ch=ch):
                c.wown(v, "vector")
                c.w(v, ev[("ztrp", idx, ch)])
                c.step("vector", v.tensor_copy(
                    zfm[:, ch * TPC + idx * 128: ch * TPC + (idx + 1) * 128],
                    pt[ch % 2][:, 0:128]))
                ev[("pt_free", ch % 2)] = c.cnt["dv"]
            c.on("vector", _evc)
        ev[("ztr_done", idx)] = c.cnt["pe"]
    ev["zfm_ready"] = ("dv", c.cnt["dv"])

    EW = min(512, TPC)
    NTH = TPC // EW
    for hh in range(16):
        for th in range(NTH):
            pa = pw[2 + (th % 2)]
            pb = pw[4 + (th % 2)]
            for k2 in range(2):
                def _hmm(t, hh=hh, th=th, k2=k2, pa=pa):
                    if hh == 0 and th == 0 and k2 == 0:
                        c.w(t, ev["zfm_ready"])
                        c.w(t, ev["wsel"])
                    c.w(t, free.get(("pa", th % 2)))
                    last = t.matmul(
                        pa[:, 0:EW], w1s[:, k2 * 128:(k2 + 1) * 128],
                        zfm[:, hh * TPC + th * EW: hh * TPC + (th + 1) * EW],
                        start=True, stop=True)
                    ev[("h1", hh, th, k2)] = c.step("tensor", last)
                c.on("tensor", _hmm)

                def _gelu(s, hh=hh, th=th, k2=k2, pa=pa):
                    c.wown(s, "scalar")
                    c.w(s, ev[("h1", hh, th, k2)])
                    c.w(s, free.get(("hb", th % 2)))
                    e = c.step("scalar", s.activation(
                        hbuf[:, (th % 2) * 512:(th % 2) * 512 + EW],
                        pa[:, 0:EW], AF.Gelu, bias=b1s[:, k2:k2 + 1]))
                    ev[("hb", hh, th, k2)] = e
                    free[("pa", th % 2)] = e
                c.on("scalar", _gelu)

                def _omm(t, hh=hh, th=th, k2=k2, pb=pb):
                    c.w(t, ev[("hb", hh, th, k2)])
                    if k2 == 0:
                        c.w(t, free.get(("pb", th % 2)))
                    last = t.matmul(
                        pb[:, 0:EW], w2s[:, k2 * 128:(k2 + 1) * 128],
                        hbuf[:, (th % 2) * 512:(th % 2) * 512 + EW],
                        start=(k2 == 0), stop=(k2 == 1))
                    e2 = c.step("tensor", last)
                    ev[("omm", hh, th, k2)] = e2
                    free[("hb", th % 2)] = e2
                c.on("tensor", _omm)

            def _oev(v, hh=hh, th=th, pb=pb):
                c.wown(v, "vector")
                c.w(v, ev[("omm", hh, th, 1)])
                zsl = zfm[:, hh * TPC + th * EW: hh * TPC + (th + 1) * EW]
                e = c.step("vector", v.scalar_tensor_tensor(
                    zsl, pb[:, 0:EW], b2s[:, 2:3], zsl, ALU.add, ALU.add))
                ev[("zo", hh, th)] = e
                free[("pb", th % 2)] = e
            c.on("vector", _oev)
    ev["zo_done"] = ("dv", c.cnt["dv"])

    NGRP = (NT + 3) // 4
    for half in range(NGRP):
        tts = list(range(half * 4, min(NT, half * 4 + 4)))
        for db in range(4):
            for ch in range(16):
                def _ldw3(g, half=half, db=db, ch=ch):
                    k = (half * 4 + db) * 16 + ch
                    slot = k % 2
                    c.w(g, free.get(("w3b", slot)))
                    src = w3_d[ch * 128:(ch + 1) * 128, db * 512:(db + 1) * 512]
                    ev[("w3", half, db, ch)] = c.dma(
                        g, f"q_w3{slot}", w3b[:, slot * 512:(slot + 1) * 512],
                        src)
                c.on("gpsimd", _ldw3)

                def _wmm(t, half=half, db=db, ch=ch):
                    k = (half * 4 + db) * 16 + ch
                    slot = k % 2
                    c.w(t, ev[("w3", half, db, ch)])
                    if ch == 0:
                        c.w(t, ev["zo_done"])
                        c.w(t, ev["b3"])
                        for i4 in range(len(tts)):
                            c.w(t, free.get(("wop", i4)))
                    last = None
                    for i4, tt in enumerate(tts):
                        last = t.matmul(
                            pw[i4][:, :],
                            zfm[:, ch * TPC + tt * 128: ch * TPC + (tt + 1) * 128],
                            w3b[:, slot * 512:(slot + 1) * 512],
                            start=(ch == 0), stop=False)
                        if ch == 15:
                            last = t.matmul(
                                pw[i4][:, :], onesr[:, 0:128],
                                b3b[0:1, db * 512:(db + 1) * 512],
                                start=False, stop=True)
                    ev[("wmm", half, db, ch)] = c.step("tensor", last)
                    free[("w3b", slot)] = ev[("wmm", half, db, ch)]
                c.on("tensor", _wmm)

            for i4, tt in enumerate(tts):
                def _ldx(e, half=half, db=db, i4=i4, tt=tt):
                    c.w(e, free.get(("xsl", i4 % 2)))
                    ev[("xld2", half, db, i4)] = c.dma(
                        e, f"q_x{i4 % 2}",
                        xsl[:, (i4 % 2) * 512:(i4 % 2 + 1) * 512],
                        x_d[tt * 128:(tt + 1) * 128, db * 512:(db + 1) * 512])
                c.on("sync", _ldx)

                def _res(v, half=half, db=db, i4=i4, tt=tt):
                    c.wown(v, "vector")
                    c.w(v, ev[("wmm", half, db, 15)])
                    c.w(v, ev[("xld2", half, db, i4)])
                    e = c.step("vector", v.scalar_tensor_tensor(
                        xsl[:, (i4 % 2) * 512:(i4 % 2 + 1) * 512],
                        pw[i4][:, :], 0.5,
                        xsl[:, (i4 % 2) * 512:(i4 % 2 + 1) * 512],
                        ALU.mult, ALU.add))
                    ev[("res", half, db, i4)] = e
                    free[("wop", i4)] = e
                c.on("vector", _res)

                def _sto(e, half=half, db=db, i4=i4, tt=tt):
                    c.w(e, ev[("res", half, db, i4)])
                    ee = c.dma(e, f"q_xo{i4 % 2}",
                               xo_d[tt * 128:(tt + 1) * 128,
                                    db * 512:(db + 1) * 512],
                               xsl[:, (i4 % 2) * 512:(i4 % 2 + 1) * 512])
                    free[("xsl", i4 % 2)] = ee
                    ev[("sto", half, db, i4)] = ee
                c.on("sync", _sto)

    def _fin(e):
        nlast = len(list(range((NGRP - 1) * 4, min(NT, NGRP * 4))))
        for i4 in range(nlast):
            c.w(e, ev[("sto", NGRP - 1, 3, i4)])
    c.on("sync", _fin)

    c.close()
    for h in reversed(hold):
        h.__exit__(None, None, None)
    return nc



def build_attn1a():
    nc = bass.Bass("TRN2", target_bir_lowering=False)
    hold = []
    NT = TPC // 128
    x_d = nc.dram_tensor("x", [TPC, D], F32, kind="ExternalInput")
    id_d = nc.dram_tensor("ident", [128, 128], F32, kind="ExternalInput")
    z_d = nc.dram_tensor("z4", [D, TPC], BF16, kind="ExternalOutput")

    xs = sb(nc, hold, "xs", [128, D])
    ztmp = sb(nc, hold, "ztmp", [128, D], BF16)
    zfm = sb(nc, hold, "zfm", [128, 16 * TPC], BF16)
    bn6 = sb(nc, hold, "bn6", [128, 24])
    mv = sb(nc, hold, "mv", [128, 2])
    sd = sb(nc, hold, "sd", [128, 1])
    rstd = sb(nc, hold, "rstd", [128, 1])
    epst = sb(nc, hold, "epst", [128, 1])
    identb = sb(nc, hold, "identb", [128, 128], BF16)
    pt = [psum(nc, hold, f"pt{i}", [128, 1024], BF16) for i in range(2)]

    c = Ctx(nc)
    ev = {}

    def _pre(g):
        ev["ident"] = c.dma(g, "q_id", identb[:, :], id_d[:, :])
    c.on("gpsimd", _pre)

    def _pre_v(v):
        ev["eps"] = c.step("vector", v.memset(epst[:, :], LN_EPS))
    c.on("vector", _pre_v)

    for idx in range(NT):
        def _ld(e, idx=idx):
            if idx > 0:
                c.w(e, ev[("z", idx - 1)])
            ev[("xld", idx)] = c.dma(e, "q_x", xs[:, :],
                                     x_d[idx * 128:(idx + 1) * 128, :])
        c.on("sync", _ld)

        def _stats(v, idx=idx):
            c.wown(v, "vector")
            c.w(v, ev[("xld", idx)])
            for k4 in range(4):
                c.step("vector", v.bn_stats(bn6[:, k4 * 6:(k4 + 1) * 6],
                                            xs[:, k4 * 512:(k4 + 1) * 512]))
            c.wown(v, "vector")
            c.w(v, ev.get(("sd", idx - 1)))
            ev[("mv", idx)] = c.step("vector", v.bn_aggr(mv[:, :], bn6[:, :]))
        c.on("vector", _stats)

        def _sqrt(s, idx=idx):
            c.wown(s, "scalar")
            c.w(s, ev[("mv", idx)])
            ev[("sd", idx)] = c.step(
                "scalar", s.activation(sd[:, :], mv[:, 1:2], AF.Sqrt,
                                       bias=epst[:, :]))
        c.on("scalar", _sqrt)

        def _apply(v, idx=idx):
            c.wown(v, "vector")
            c.w(v, ev[("sd", idx)])
            c.step("vector", v.reciprocal(rstd[:, :], sd[:, :]))
            c.wown(v, "vector")
            if ("ztr_done", idx - 1) in ev:
                c.w(v, ("pe", ev[("ztr_done", idx - 1)]))
            ev[("z", idx)] = c.step(
                "vector", v.tensor_scalar(ztmp[:, :], xs[:, :], mv[:, 0:1],
                                          rstd[:, :], ALU.subtract, ALU.mult))
        c.on("vector", _apply)

        for ch in range(16):
            def _trc(t, idx=idx, ch=ch):
                if ch == 0:
                    c.w(t, ev[("z", idx)])
                    c.w(t, ev["ident"])
                if ("pt_free", ch % 2) in ev:
                    c.w(t, ("dv", ev[("pt_free", ch % 2)]))
                ev[("ztrp", idx, ch)] = c.step(
                    "tensor", t.transpose(pt[ch % 2][:, 0:128],
                                          ztmp[:, ch * 128:(ch + 1) * 128],
                                          identb[:, :]))
            c.on("tensor", _trc)

            def _evc(v, idx=idx, ch=ch):
                c.wown(v, "vector")
                c.w(v, ev[("ztrp", idx, ch)])
                c.step("vector", v.tensor_copy(
                    zfm[:, ch * TPC + idx * 128: ch * TPC + (idx + 1) * 128],
                    pt[ch % 2][:, 0:128]))
                ev[("pt_free", ch % 2)] = c.cnt["dv"]
            c.on("vector", _evc)
        ev[("ztr_done", idx)] = c.cnt["pe"]

    def _st(e):
        c.w(e, ("dv", c.cnt["dv"]))
        ee = c.dma(e, "q_z", z_d.rearrange("(cc p) t -> p cc t", p=128),
                   zfm[:, :].rearrange("p (cc t) -> p cc t", t=TPC))
        c.w(e, ee)
    c.on("sync", _st)

    c.close()
    for h in reversed(hold):
        h.__exit__(None, None, None)
    return nc



def build_attn1c():
    nc = bass.Bass("TRN2", target_bir_lowering=False)
    hold = []
    NT = TPC // 128
    x_d = nc.dram_tensor("x", [TPC, D], F32, kind="ExternalInput")
    o_d = nc.dram_tensor("ofm", [4096, TPC], BF16, kind="ExternalInput")
    wo_d = nc.dram_tensor("wo", [4096, D], F32, kind="ExternalInput")
    xo_d = nc.dram_tensor("xo", [TPC, D], F32, kind="ExternalOutput")

    ofm = sb(nc, hold, "ofms", [128, 32 * TPC], BF16)
    wob = sb(nc, hold, "wob", [128, 2 * 512], BF16)
    xsl = sb(nc, hold, "xsl", [128, 2 * 512])

    pw = [psum(nc, hold, f"pw{i}", [128, 512]) for i in range(4)]

    c = Ctx(nc)
    ev = {}
    free = {}

    def _pre(g):
        ev["ofm"] = c.dma(g, "q_o", ofm[:, :].rearrange(
            "p (cc t) -> p cc t", t=TPC),
            o_d.rearrange("(cc p) t -> p cc t", p=128))
    c.on("gpsimd", _pre)

    NGRP = (NT + 3) // 4
    for half in range(NGRP):
        tts = list(range(half * 4, min(NT, half * 4 + 4)))
        for db in range(4):
            for ch in range(32):
                def _ldwo(g, half=half, db=db, ch=ch):
                    k = (half * 4 + db) * 32 + ch
                    slot = k % 2
                    c.w(g, free.get(("wob", slot)))
                    src = wo_d[ch * 128:(ch + 1) * 128, db * 512:(db + 1) * 512]
                    ev[("wo", half, db, ch)] = c.dma(
                        g, f"q_wo{slot}", wob[:, slot * 512:(slot + 1) * 512],
                        src)
                c.on("gpsimd", _ldwo)

                def _wmm(t, half=half, db=db, ch=ch):
                    k = (half * 4 + db) * 32 + ch
                    slot = k % 2
                    c.w(t, ev[("wo", half, db, ch)])
                    if ch == 0:
                        c.w(t, ev["ofm"])
                        for i4 in range(len(tts)):
                            c.w(t, free.get(("wop", i4)))
                    last = None
                    for i4, tt in enumerate(tts):
                        last = t.matmul(
                            pw[i4][:, :],
                            ofm[:, ch * TPC + tt * 128: ch * TPC + (tt + 1) * 128],
                            wob[:, slot * 512:(slot + 1) * 512],
                            start=(ch == 0), stop=(ch == 31))
                    ev[("wmm", half, db, ch)] = c.step("tensor", last)
                    free[("wob", slot)] = ev[("wmm", half, db, ch)]
                c.on("tensor", _wmm)

            for i4, tt in enumerate(tts):
                def _ldx(e, half=half, db=db, i4=i4, tt=tt):
                    c.w(e, free.get(("xsl", i4 % 2)))
                    ev[("xld", half, db, i4)] = c.dma(
                        e, f"q_x{i4 % 2}",
                        xsl[:, (i4 % 2) * 512:(i4 % 2 + 1) * 512],
                        x_d[tt * 128:(tt + 1) * 128, db * 512:(db + 1) * 512])
                c.on("sync", _ldx)

                def _res(v, half=half, db=db, i4=i4, tt=tt):
                    c.wown(v, "vector")
                    c.w(v, ev[("wmm", half, db, 31)])
                    c.w(v, ev[("xld", half, db, i4)])
                    e = c.step("vector", v.tensor_tensor(
                        xsl[:, (i4 % 2) * 512:(i4 % 2 + 1) * 512],
                        xsl[:, (i4 % 2) * 512:(i4 % 2 + 1) * 512],
                        pw[i4][:, :], ALU.add))
                    ev[("res", half, db, i4)] = e
                    free[("wop", i4)] = e
                c.on("vector", _res)

                def _sto(e, half=half, db=db, i4=i4, tt=tt):
                    c.w(e, ev[("res", half, db, i4)])
                    ee = c.dma(e, f"q_xo{i4 % 2}",
                               xo_d[tt * 128:(tt + 1) * 128,
                                    db * 512:(db + 1) * 512],
                               xsl[:, (i4 % 2) * 512:(i4 % 2 + 1) * 512])
                    free[("xsl", i4 % 2)] = ee
                    ev[("sto", half, db, i4)] = ee
                c.on("sync", _sto)

    def _fin(e):
        nlast = len(list(range((NGRP - 1) * 4, min(NT, NGRP * 4))))
        for i4 in range(nlast):
            c.w(e, ev[("sto", NGRP - 1, 3, i4)])
    c.on("sync", _fin)

    c.close()
    for h in reversed(hold):
        h.__exit__(None, None, None)
    return nc



def emit_chunk(c, ev, free, env, hf, ch, t0, scur):
    ALU_ = ALU
    qb = env["qb"]; kb = env["kb"]; vb = env["vb"]; ob = env["ob"]
    r_b = env["r_b"]; r_L = env["r_L"]; r_LB = env["r_LB"]
    s0f = env["s0f"]; s1f = env["s1f"]; s0b = env["s0b"]
    atp = env["atp"]
    xp_ = env["xp_"]; g_ = env["g_"]
    mt = env["mt"]; dec = env["dec"]; decf = env["decf"]; rhs_s = env["rhs_s"]
    mskPair = env["mskPair"]
    us = env["us"]; usc = env["usc"]; t1f = env["t1f"]; t2f = env["t2f"]
    ktms = env["ktms"]; otile = env["otile"]; colsb = env["colsb"]
    rrowf = env["rrowf"]
    onesrf = env["onesrf"]; onescf = env["onescf"]
    zerot = env["zerot"]; identb = env["identb"]; identf = env["identf"]
    eyemb = env["eyemb"]; mskMb = env["mskMb"]
    pbA = env["pbA"]; pbB = env["pbB"]; pbC = env["pbC"]; pbD = env["pbD"]
    pbE = env["pbE"]; pt = env["pt"]
    HLF = env["HLF"]
    BOF = HLF + 4
    trow = (ch % 2) * 64
    tt2 = ch // 2
    sc = s0f if scur[0] == 0 else s1f
    sn = s1f if scur[0] == 0 else s0f

    def kslice(h):
        return kb[:, h * BOF + 4 + t0: h * BOF + 4 + t0 + 64]

    def qslice(h):
        return qb[:, h * BOF + 4 + t0: h * BOF + 4 + t0 + 64]

    lc_tm = env["lc_tm"]; lbc_tm = env["lbc_tm"]; bc_tm = env["bc_tm"]
    nl_tm = env["nl_tm"]; g_tm = env["g_tm"]; gbn_tm = env["gbn_tm"]
    esc_tm = env["esc_tm"]; gcf_tm = env["gcf_tm"]
    NCHH_ = env["NCHH_"]
    id64b = env["id64b"]
    LBcol = lbc_tm[:, ch:ch + 1]
    nLcol = nl_tm[:, ch:ch + 1]
    Bcol = bc_tm[:, ch:ch + 1]
    Gcol = g_tm[:, ch:ch + 1]
    GBNcol = gbn_tm[:, ch:ch + 1]
    ESCcol = esc_tm[:, ch:ch + 1]

    def _chst(t):
        c.w(t, ev.get(("ochunk", hf, ch - 1)))
        if ch == 0:
            c.w(t, ev.get(("ost", hf - 1)))
            c.w(t, ev[("ctiles", hf)])
            c.w(t, ev[("normall", hf)])
            c.w(t, ev[("vtr_done", hf)])
        ev[("cols", hf, ch)] = c.step("tensor", t.nop())
    c.on("tensor", _chst)

    def _kraw(t):
        c.w(t, ev[("cols", hf, ch)])
        for rep in range(2):
            for h in range(2):
                for h2 in range(2):
                    last = t.matmul(
                        pbA[h * 64:(h + 1) * 64,
                            rep * 128 + h2 * 64: rep * 128 + h2 * 64 + 64],
                        kslice(h), kslice(h2), start=True, stop=True)
        ev[("kraw", hf, ch)] = c.step("tensor", last)
    c.on("tensor", _kraw)
    if A1B_SUB <= 1:
        ev[("ochunk", hf, ch)] = ev[("kraw", hf, ch)]
        return

    def _decmm(t, which, msk, rof, evk, wait_ev):
        mk = {"A": "mskA", "AT": "mskAT", "M": "mskM"}[which]

        def fn(t2):
            c.w(t2, ev[wait_ev])
            c.w(t2, ev[mk])
            for kk in range(4):
                c.w(t2, ev[("rrow", hf, kk)])
            last = None
            for h in range(2):
                rsl = rrowf[0:1, rof + h * HLF + t0: rof + h * HLF + t0 + 64]
                last = t2.matmul(pbC[:, h * 64:(h + 1) * 64],
                                 onesrf[0:1, :], rsl,
                                 start=True, stop=True)
            ev[(evk[0] + "_mm",) + evk[1:]] = c.step("tensor", last)
        return fn, mk

    def _dAAT(t):
        c.w(t, ev[("kraw", hf, ch)])
        c.w(t, ev["mskA"])
        c.w(t, ev["mskAT"])
        for kk in range(4):
            c.w(t, ev[("rrow", hf, kk)])
        last = None
        for rof, cb in ((0, 0), (2 * HLF, 128)):
            for h in range(2):
                rsl = rrowf[0:1, rof + h * HLF + t0:
                            rof + h * HLF + t0 + 64]
                last = t.matmul(pbC[:, cb + h * 64: cb + h * 64 + 64],
                                onesrf[0:1, :], rsl, start=True, stop=True)
        ev[("dAAT_mm", hf, ch)] = c.step("tensor", last)
    c.on("tensor", _dAAT)

    def _dmask(v):
        c.wown(v, "vector")
        c.w(v, ev[("dAAT_mm", hf, ch)])
        ev[("dAAT", hf, ch)] = c.step("vector", v.tensor_tensor(
            decf[:, 0:256], pbC[:, 0:256], mskPair[:, :], ALU_.add))
    c.on("vector", _dmask)

    def _dAact(s):
        c.wown(s, "scalar")
        c.w(s, ev[("dAAT", hf, ch)])
        ev[("dAa", hf, ch)] = c.step("scalar", s.activation(
            dec[:, 0:128], decf[:, 0:128], AF.Exp, bias=LBcol, scale=-1.0))
        c.wown(s, "scalar")
        ev[("dATa", hf, ch)] = c.step("scalar", s.activation(
            dec[:, 128:256], decf[:, 128:256], AF.Exp, bias=nLcol))
    c.on("scalar", _dAact)

    def _Am(v):
        c.wown(v, "vector")
        c.w(v, ev[("dAa", hf, ch)])
        ev[("Am", hf, ch)] = c.step("vector", v.tensor_tensor(
            atp[:, 0:128], dec[:, 0:128], pbA[:, 0:128], ALU_.mult))
    c.on("vector", _Am)
    if A1B_SUB <= 2:
        ev[("ochunk", hf, ch)] = ev[("Am", hf, ch)]
        return

    def _ATm(v):
        c.wown(v, "vector")
        c.w(v, ev[("dATa", hf, ch)])
        ev[("ATm", hf, ch)] = c.step("vector", v.tensor_tensor(
            atp[:, 128:256], dec[:, 128:256], pbA[:, 128:256], ALU_.mult))
    c.on("vector", _ATm)
    if A1B_SUB <= 3:
        ev[("ochunk", hf, ch)] = ev[("ATm", hf, ch)]
        return

    def _kq(t):
        c.w(t, ev[("ATm", hf, ch)])
        for h in range(2):
            for h2 in range(2):
                last = t.matmul(
                    pbA[h * 64:(h + 1) * 64, h2 * 64:h2 * 64 + 64],
                    kslice(h), qslice(h2), start=True, stop=True)
        ev[("kq", hf, ch)] = c.step("tensor", last)
    c.on("tensor", _kq)

    fnM, _ = _decmm(None, "M", mskMb, 0, ("dM", hf, ch), ("kq", hf, ch))
    c.on("tensor", fnM)

    def _dMmask(v):
        c.wown(v, "vector")
        c.w(v, ev[("dM_mm", hf, ch)])
        ev[("dM", hf, ch)] = c.step("vector", v.tensor_tensor(
            decf[:, 0:128], pbC[:, 0:128], mskMb[:, :], ALU_.add))
    c.on("vector", _dMmask)

    def _dMact(s):
        c.wown(s, "scalar")
        c.w(s, ev[("dM", hf, ch)])
        ev[("dMa", hf, ch)] = c.step("scalar", s.activation(
            dec[:, 0:128], decf[:, 0:128], AF.Exp, bias=nLcol))
    c.on("scalar", _dMact)

    def _Mm(v):
        c.wown(v, "vector")
        c.w(v, ev[("dMa", hf, ch)])
        ev[("Mm", hf, ch)] = c.step("vector", v.tensor_tensor(
            mt[:, :], dec[:, 0:128], pbA[:, 0:128], ALU_.mult))
    c.on("vector", _Mm)
    if A1B_SUB <= 4:
        ev[("ochunk", hf, ch)] = ev[("Mm", hf, ch)]
        return

    def _g0(v):
        c.wown(v, "vector")
        c.w(v, ev[("ATm", hf, ch)])
        ev[("g", hf, ch, 0)] = c.step("vector", v.tensor_tensor(
            g_[0][:, :], identb[:, :], atp[:, 128:256], ALU_.subtract))
    c.on("vector", _g0)

    for lvl in range(1, 6):
        if lvl == 1:
            xcur, xtcur = atp[:, 0:128], atp[:, 128:256]
        else:
            xcur = xp_[(lvl - 1) % 2][:, 0:128]
            xtcur = xp_[(lvl - 1) % 2][:, 128:256]
        xnew = xp_[lvl % 2][:, 0:128]
        gcur = g_[(lvl - 1) % 2]
        gnew = g_[lvl % 2]
        ew = 128 if lvl == 5 else 256

        def _sq(t, lvl=lvl, xcur=xcur, xtcur=xtcur, ew=ew):
            c.w(t, ev[("Mm", hf, ch)] if lvl == 1 else
                ev[("gev", hf, ch, lvl - 1)])
            last = t.matmul(pbC[:, 0:128], xtcur, xcur,
                            start=True, stop=True)
            if ew == 256:
                last = t.matmul(pbC[:, 128:256], xcur, xtcur,
                                start=True, stop=True)
            ev[("sqm", hf, ch, lvl)] = c.step("tensor", last)
        c.on("tensor", _sq)

        def _sqe(v, lvl=lvl, ew=ew):
            c.wown(v, "vector")
            c.w(v, ev[("sqm", hf, ch, lvl)])
            c.step("vector", v.tensor_copy(xp_[lvl % 2][:, 0:ew],
                                           pbC[:, 0:ew]))
            ev[("sqe", hf, ch, lvl)] = ("dv", c.cnt["dv"])
        c.on("vector", _sqe)

        def _gp(t, lvl=lvl, xnew=xnew, gcur=gcur):
            c.w(t, ev[("sqe", hf, ch, lvl)])
            c.w(t, ev[("g", hf, ch, lvl - 1)] if lvl == 1 else
                ev[("gev", hf, ch, lvl - 1)])
            t.matmul(pbD[:, 0:128], xnew, gcur[:, :],
                     start=True, stop=False)
            last = t.matmul(pbD[:, 0:128], identb[:, :], gcur[:, :],
                            start=False, stop=True)
            ev[("gpm", hf, ch, lvl)] = c.step("tensor", last)
        c.on("tensor", _gp)

        def _gpe(v, lvl=lvl, gnew=gnew):
            c.wown(v, "vector")
            c.w(v, ev[("gpm", hf, ch, lvl)])
            c.step("vector", v.tensor_copy(gnew[:, :], pbD[:, 0:128]))
            ev[("gev", hf, ch, lvl)] = ("dv", c.cnt["dv"])
        c.on("vector", _gpe)
    gfin = g_[5 % 2]
    if A1B_SUB <= 5:
        ev[("ochunk", hf, ch)] = ev[("gev", hf, ch, 5)]
        return

    def _ks0(t):
        c.w(t, ev[("gev", hf, ch, 5)])
        for h in range(2):
            last = t.matmul(pbC[h * 64:(h + 1) * 64, 0:256], kslice(h),
                            s0b[:, h * 256:(h + 1) * 256],
                            start=True, stop=True)
        ev[("ks0", hf, ch)] = c.step("tensor", last)
    c.on("tensor", _ks0)
    if A1B_SUB <= 6:
        ev[("ochunk", hf, ch)] = ev[("ks0", hf, ch)]
        return

    def _vadd(t):
        c.w(t, ev[("ks0", hf, ch)])
        c.w(t, ev["eyem"])
        vsl = vb[trow:trow + 64, :].rearrange("p (ck t) -> p ck t", t=HLF + 4)
        for h in range(2):
            vap = vsl[:, 2 * h:2 * h + 2, 4 + tt2 * 128: 4 + (tt2 + 1) * 128]
            last = t.matmul(pbD[h * 64:(h + 1) * 64, 0:256],
                            eyemb[trow:trow + 64, h * 64:(h + 1) * 64], vap,
                            start=True, stop=True)
        ev[("vadd", hf, ch)] = c.step("tensor", last)
    c.on("tensor", _vadd)

    def _rhs(v):
        c.wown(v, "vector")
        c.w(v, ev[("vadd", hf, ch)])
        c.step("vector", v.tensor_scalar(t1f[:, :], pbC[:, 0:256],
                                         GBNcol, None, ALU_.mult))
        c.wown(v, "vector")
        ev[("rhs", hf, ch)] = c.step("vector", v.scalar_tensor_tensor(
            rhs_s[:, :], pbD[:, 0:256], Bcol, t1f[:, :],
            ALU_.mult, ALU_.add))
    c.on("vector", _rhs)
    if A1B_SUB <= 7:
        ev[("ochunk", hf, ch)] = ev[("rhs", hf, ch)]
        return

    def _umm(t):
        c.w(t, ev[("rhs", hf, ch)])
        ev[("umm", hf, ch)] = c.step("tensor", t.matmul(
            pbC[:, 0:256], gfin[:, :], rhs_s[:, :], start=True, stop=True))
    c.on("tensor", _umm)

    def _uev(v):
        c.wown(v, "vector")
        c.w(v, ev[("umm", hf, ch)])
        ev[("uev", hf, ch)] = c.step("vector", v.tensor_copy(
            us[:, :], pbC[:, 0:256]))
    c.on("vector", _uev)
    if A1B_SUB <= 8:
        ev[("ochunk", hf, ch)] = ev[("uev", hf, ch)]
        return

    def _qs0(t):
        c.w(t, ev[("uev", hf, ch)])
        for h in range(2):
            t.matmul(pbA[h * 64:(h + 1) * 64, 0:256], qslice(h),
                     s0b[:, h * 256:(h + 1) * 256], start=True, stop=True)
        last = t.matmul(pbD[:, 0:256], mt[:, :], us[:, :],
                        start=True, stop=True)
        ev[("qs0", hf, ch)] = c.step("tensor", last)
    c.on("tensor", _qs0)

    def _oev(v):
        c.wown(v, "vector")
        c.w(v, ev[("qs0", hf, ch)])
        c.step("vector", v.tensor_copy(t2f[:, :], pbD[:, 0:256]))
        c.wown(v, "vector")
        c.step("vector", v.scalar_tensor_tensor(
            otile[:, :], pbA[:, 0:256], Gcol, t2f[:, :],
            ALU_.mult, ALU_.add))
        c.wown(v, "vector")
        ev[("otile", hf, ch)] = c.step("vector", v.tensor_scalar(
            usc[:, :], us[:, :], ESCcol, None, ALU_.mult))
    c.on("vector", _oev)
    if A1B_SUB <= 9:
        ev[("ochunk", hf, ch)] = ev[("otile", hf, ch)]
        return

    def _ktr(t):
        c.w(t, ev[("otile", hf, ch)])
        for h in range(2):
            last = t.transpose(pt[h][0:64, 0:128].tensor[h * 64:h * 64 + 64,
                                                         0:128]
                               if False else pt[0][h * 64:h * 64 + 64, 0:128],
                               kslice(h), identb[:, :])
        ev[("ktr", hf, ch)] = c.step("tensor", last)
    c.on("tensor", _ktr)

    def _kte(v):
        c.wown(v, "vector")
        c.w(v, ev[("ktr", hf, ch)])
        ev[("kte", hf, ch)] = c.step("vector", v.tensor_copy(
            ktms[:, :], pt[0][:, 0:128]))
    c.on("vector", _kte)
    if A1B_SUB <= 10:
        ev[("ochunk", hf, ch)] = ev[("kte", hf, ch)]
        return

    def _smm(t):
        c.w(t, ev[("kte", hf, ch)])
        e0 = c.step("tensor", t.matmul(
            pbE[:, 0:256], ktms[0:64, :], usc[0:64, :],
            start=True, stop=True))
        c.w(t, e0)
        ev[("smm", hf, ch)] = c.step("tensor", t.matmul(
            pbE[:, 256:512], ktms[64:128, :], usc[64:128, :],
            start=True, stop=True))
    c.on("tensor", _smm)

    def _sev(v):
        c.wown(v, "vector")
        c.w(v, ev[("smm", hf, ch)])
        for h in range(2):
            c.step("vector", v.scalar_tensor_tensor(
                sn[:, h * 256:(h + 1) * 256], sc[:, h * 256:(h + 1) * 256],
                gcf_tm[:, h * NCHH_ + ch: h * NCHH_ + ch + 1],
                pbE[:, h * 256:(h + 1) * 256], ALU_.mult, ALU_.add))
            c.wown(v, "vector")
        ev[("sevv", hf, ch)] = ("dv", c.cnt["dv"])
    c.on("vector", _sev)

    def _sevc(s):
        c.wown(s, "scalar")
        c.w(s, ev[("sevv", hf, ch)])
        ev[("sev", hf, ch)] = c.step("scalar", s.copy(s0b[:, :], sn[:, :]))
    c.on("scalar", _sevc)
    if A1B_SUB <= 11:
        ev[("ochunk", hf, ch)] = ev[("sev", hf, ch)]
        return

    def _otr(t):
        c.w(t, ev[("sev", hf, ch)])
        c.w(t, ev["id64"])
        if ("optf",) in ev:
            c.w(t, ("dv", ev[("optf",)]))
        e01 = None
        for ec in range(4):
            h, es = ec // 2, ec % 2
            if ec == 2:
                c.w(t, e01)
            last = t.matmul(
                pbD[0:128, ec * 64:(ec + 1) * 64],
                otile[h * 64:(h + 1) * 64, es * 128:(es + 1) * 128],
                id64b[h * 64:(h + 1) * 64, 0:64],
                start=True, stop=True)
            if ec == 1:
                e01 = c.step("tensor", last)
        ev[("otr", hf, ch)] = c.step("tensor", last)
    c.on("tensor", _otr)

    def _ote(v):
        c.wown(v, "vector")
        c.w(v, ev[("otr", hf, ch)])
        dst = ob[:, :].rearrange("p (ec t) -> p ec t", t=HLF)[:, :,
                                                             t0:t0 + 64]
        srcv = pbD[:, 0:256].rearrange("p (ec t) -> p ec t", t=64)
        c.step("vector", v.tensor_copy(dst, srcv))
        ev[("optf",)] = c.cnt["dv"]
    c.on("vector", _ote)
    ev[("ochunk", hf, ch)] = ("dv", c.cnt["dv"])


ATT1_T = T


A1B_PHASES = 9
A1B_NCH = 10 ** 9
A1B_SUB = 99
A1B_NPT = 10 ** 9


def build_attn1b():
    nc = bass.Bass("TRN2", target_bir_lowering=False)
    hold = []
    AT = ATT1_T
    NPART = 4
    HLF = AT // NPART
    TBW = min(512, HLF)
    NTB = HLF // TBW
    CSP = min(2048, HLF)
    NSP = HLF // CSP
    NCHH = HLF // 64

    z_d = nc.dram_tensor("z4", [D, AT], BF16, kind="ExternalInput")
    wall_d = nc.dram_tensor("wall", [D, 1028], F32, kind="ExternalInput")
    cw_d = nc.dram_tensor("cwh", [128, 8, 4], F32, kind="ExternalInput")
    cb_d = nc.dram_tensor("cbh", [128, 8], F32, kind="ExternalInput")
    id_d = nc.dram_tensor("ident", [128, 128], F32, kind="ExternalInput")
    eye_d = nc.dram_tensor("eyem", [128, 128], F32, kind="ExternalInput")
    mskA_d = nc.dram_tensor("mskA", [128, 128], F32, kind="ExternalInput")
    mskAT_d = nc.dram_tensor("mskAT", [128, 128], F32, kind="ExternalInput")
    mskM_d = nc.dram_tensor("mskM", [128, 128], F32, kind="ExternalInput")
    id64_d = nc.dram_tensor("id64", [128, 64], F32, kind="ExternalInput")
    o_d = nc.dram_tensor("o", [512, AT], BF16, kind="ExternalOutput")
    scr_d = nc.dram_tensor("scr", [6, HLF], F32)

    wallb = sb(nc, hold, "wallb", [128, 16 * 1028], BF16)
    cwb = sb(nc, hold, "cwb", [128, 8 * 4])
    cbb = sb(nc, hold, "cbb", [128, 8])
    zbuf = sb(nc, hold, "zbuf", [128, 16 * TBW], BF16)
    qb = sb(nc, hold, "qb", [128, 2 * (HLF + 4)], BF16)
    kb = sb(nc, hold, "kb", [128, 2 * (HLF + 4)], BF16)
    vb = sb(nc, hold, "vb", [128, 4 * (HLF + 4)], BF16)
    ob = sb(nc, hold, "ob", [128, 4 * HLF], BF16)
    r_b = sb(nc, hold, "r_b", [64, HLF], BF16)
    r_L = sb(nc, hold, "r_L", [64, HLF])
    r_LB = sb(nc, hold, "r_LB", [64, HLF])
    s0f = sb(nc, hold, "s0f", [128, 512])
    s1f = sb(nc, hold, "s1f", [128, 512])
    s0b = sb(nc, hold, "s0b", [128, 512], BF16)
    atp = sb(nc, hold, "atp", [128, 256], BF16)
    xp_ = [sb(nc, hold, f"xp{i}", [128, 256], BF16) for i in range(2)]
    g_ = [sb(nc, hold, f"g{i}", [128, 128], BF16) for i in range(2)]
    mt = sb(nc, hold, "mt", [128, 128], BF16)
    dec = sb(nc, hold, "dec", [128, 256], BF16)
    decf = sb(nc, hold, "decf", [128, 256])
    rhs_s = sb(nc, hold, "rhs_s", [128, 256], BF16)
    us = sb(nc, hold, "us", [128, 256], BF16)
    usc = sb(nc, hold, "usc", [128, 256], BF16)
    t1f = sb(nc, hold, "t1f", [128, 256])
    t2f = sb(nc, hold, "t2f", [128, 256])
    ktms = sb(nc, hold, "ktms", [128, 128], BF16)
    otile = sb(nc, hold, "otile", [128, 256], BF16)
    colsb = sb(nc, hold, "colsb", [128, 20])
    NCHH_ = HLF // 64
    lc_tm = sb(nc, hold, "lc_tm", [128, NCHH_])
    lbc_tm = sb(nc, hold, "lbc_tm", [128, NCHH_])
    bc_tm = sb(nc, hold, "bc_tm", [128, NCHH_])
    nl_tm = sb(nc, hold, "nl_tm", [128, NCHH_])
    g_tm = sb(nc, hold, "g_tm", [128, NCHH_])
    gbn_tm = sb(nc, hold, "gbn_tm", [128, NCHH_])
    esc_tm = sb(nc, hold, "esc_tm", [128, NCHH_])
    gcf_tm = sb(nc, hold, "gcf_tm", [128, 2 * NCHH_])
    lp_tm = sb(nc, hold, "lp_tm", [128, NCHH_])
    lcp_tm = sb(nc, hold, "lcp_tm", [128, NCHH_])
    id64b = sb(nc, hold, "id64b", [128, 64], BF16)
    scT = sb(nc, hold, "scT", [32, 64]) if HLF // 64 <= 32 else sb(nc, hold, "scT", [HLF // 64, 64])
    rrowf = sb(nc, hold, "rrowf", [1, 4 * HLF])
    tails = sb(nc, hold, "tails", [128, 64], BF16)
    tmpc = sb(nc, hold, "tmpc", [128, CSP], BF16)
    sqw = sb(nc, hold, "sqw", [128, TBW], BF16)
    abstg = sb(nc, hold, "abstg", [4, TBW])
    nrow = sb(nc, hold, "nrow", [1, 2 * TBW]) if False else sb(nc, hold, "nrow", [1, 2 * min(TBW, 512)])
    onesr = sb(nc, hold, "onesr", [1, 128], BF16)
    onesrf = sb(nc, hold, "onesrf", [33, 128])
    onescf = sb(nc, hold, "onescf", [128, 1])
    onescb = sb(nc, hold, "onescb", [128, 1], BF16)
    zerot = sb(nc, hold, "zerot", [33, 2])
    identb = sb(nc, hold, "identb", [128, 128], BF16)
    identf = sb(nc, hold, "identf", [128, 128])
    eyemb = sb(nc, hold, "eyemb", [128, 128], BF16)
    mskPair = sb(nc, hold, "mskPair", [128, 256])
    mskMb = sb(nc, hold, "mskMb", [128, 128])

    pbA = psum(nc, hold, "pbA", [128, 512])
    pbB = psum(nc, hold, "pbB", [128, 512])
    pbC = psum(nc, hold, "pbC", [128, 512])
    pbD = psum(nc, hold, "pbD", [128, 512])
    pbE = psum(nc, hold, "pbE", [128, 512])
    pt = [psum(nc, hold, f"pt{i}", [128, 1024], BF16) for i in range(2)]

    c = Ctx(nc)
    ev = {}
    free = {}

    def _pre(g):
        ev["wall"] = c.dma(g, "q_w", wallb[:, :].rearrange(
            "p (cc j) -> p cc j", j=1028),
            wall_d.rearrange("(cc p) j -> p cc j", p=128))
        ev["cw"] = c.dma(g, "q_cw", cwb[:, :].rearrange(
            "p (k m) -> p k m", m=4), cw_d[:, :, :])
        ev["cb"] = c.dma(g, "q_cb", cbb[:, :], cb_d[:, :])
        ev["ident"] = c.dma(g, "q_id", identb[:, :], id_d[:, :])
        ev["identf"] = c.dma(g, "q_idf", identf[:, :], id_d[:, :])
        ev["eyem"] = c.dma(g, "q_ey", eyemb[:, :], eye_d[:, :])
        ev["mskA"] = c.dma(g, "q_mA", mskPair[:, 0:128], mskA_d[:, :])
        ev["mskAT"] = c.dma(g, "q_mAT", mskPair[:, 128:256], mskAT_d[:, :])
        ev["mskM"] = c.dma(g, "q_mM", mskMb[:, :], mskM_d[:, :])
        ev["id64"] = c.dma(g, "q_i64", id64b[:, :], id64_d[:, :])
    c.on("gpsimd", _pre)

    def _pre_v(v):
        for tsr in (qb, kb, vb):
            c.wown(v, "vector")
            c.step("vector", v.memset(tsr[:, :], 0.0))
        for tsr in (s0f, s1f, s0b, tails):
            c.wown(v, "vector")
            c.step("vector", v.memset(tsr[:, :], 0.0))
        c.wown(v, "vector")
        c.step("vector", v.memset(zerot[:, :], 0.0))
        c.wown(v, "vector")
        c.step("vector", v.memset(onesr[:, :], 1.0))
        c.wown(v, "vector")
        c.step("vector", v.memset(onesrf[:, :], 1.0))
        c.wown(v, "vector")
        c.step("vector", v.memset(onescf[:, :], 1.0))
        c.wown(v, "vector")
        ev["consts"] = c.step("vector", v.memset(onescb[:, :], 1.0))
    c.on("vector", _pre_v)

    groups = [("q0", 0, qb, 0), ("q1", 128, qb, 1),
              ("k0", 256, kb, 0), ("k1", 384, kb, 1),
              ("v0", 512, vb, 0), ("v1", 640, vb, 1),
              ("v2", 768, vb, 2), ("v3", 896, vb, 3)]

    scur = [0]

    for hf in range(min(NPART, A1B_NPT)):
        for tb in range(NTB):
            def _ldz(e, hf=hf, tb=tb):
                c.w(e, ev.get(("zdone", hf, tb - 1)) if tb > 0 else
                    ev.get(("zdone", hf - 1, NTB - 1)))
                src = z_d[:, hf * HLF + tb * TBW: hf * HLF + (tb + 1) * TBW]
                ev[("z", hf, tb)] = c.dma(
                    e, "q_z", zbuf[:, :].rearrange("p (cc t) -> p cc t", t=TBW),
                    src.rearrange("(cc p) t -> p cc t", p=128))
            c.on("sync", _ldz)

            for gi, (gn, cbse, dbuf, dck) in enumerate(groups):
                bank = (pbC, pbD, pbE)[gi % 3]

                def _pmm(t, hf=hf, tb=tb, gi=gi, cbse=cbse, bank=bank):
                    if hf == 0 and tb == 0 and gi == 0:
                        c.w(t, ev["wall"])
                    if tb == 0:
                        c.w(t, ev.get(("scan_done", hf - 1)))
                    c.w(t, ev[("z", hf, tb)])
                    c.w(t, free.get(("pj", gi % 3)))
                    last = None
                    for cc in range(16):
                        last = t.matmul(
                            bank[:, 0:TBW],
                            wallb[:, cc * 1028 + cbse: cc * 1028 + cbse + 128],
                            zbuf[:, cc * TBW:(cc + 1) * TBW],
                            start=(cc == 0), stop=(cc == 15))
                    ev[("pmm", hf, tb, gi)] = c.step("tensor", last)
                c.on("tensor", _pmm)

                def _pev(v, hf=hf, tb=tb, gi=gi, dbuf=dbuf, dck=dck,
                         bank=bank):
                    c.wown(v, "vector")
                    c.w(v, ev[("pmm", hf, tb, gi)])
                    e = c.step("vector", v.tensor_copy(
                        dbuf[:, dck * (HLF + 4) + 4 + tb * TBW:
                             dck * (HLF + 4) + 4 + (tb + 1) * TBW],
                        bank[:, 0:TBW]))
                    ev[("pev", hf, tb, gi)] = e
                    free[("pj", gi % 3)] = e
                c.on("vector", _pev)

            def _abmm(t, hf=hf, tb=tb):
                if tb == 0:
                    c.w(t, ev.get(("scan_done", hf - 1)))
                c.w(t, ev[("z", hf, tb)])
                c.w(t, free.get(("pab",)))
                c.w(t, ev["identf"])
                last = None
                for cc in range(16):
                    last = t.matmul(
                        pbB[0:4, 0:TBW],
                        wallb[:, cc * 1028 + 1024: cc * 1028 + 1028],
                        zbuf[:, cc * TBW:(cc + 1) * TBW],
                        start=(cc == 0), stop=(cc == 15))
                ev[("abmm", hf, tb)] = c.step("tensor", last)
            c.on("tensor", _abmm)

            def _abst(v, hf=hf, tb=tb):
                c.wown(v, "vector")
                c.w(v, ev[("abmm", hf, tb)])
                ev[("abst", hf, tb)] = c.step("vector", v.tensor_copy(
                    abstg[:, :], pbB[0:4, 0:TBW]))
                free[("pab",)] = ev[("abst", hf, tb)]
            c.on("vector", _abst)

            for abi in range(4):
                def _abx(t, hf=hf, tb=tb, abi=abi):
                    c.w(t, ev[("abst", hf, tb)])
                    c.w(t, free.get(("pabx",)))
                    ev[("abx", hf, tb, abi)] = c.step("tensor", t.matmul(
                        pbA[0:1, 0:TBW], identf[0:4, abi:abi + 1],
                        abstg[:, :], start=True, stop=True))
                c.on("tensor", _abx)

                def _abe(s, hf=hf, tb=tb, abi=abi):
                    c.wown(s, "scalar")
                    c.w(s, ev[("abx", hf, tb, abi)])
                    h2 = abi % 2
                    if abi < 2:
                        c.step("scalar", s.activation(
                            nrow[0:1, 0:TBW], pbA[0:1, 0:TBW], AF.Sigmoid))
                        c.wown(s, "scalar")
                        e = c.step("scalar", s.activation(
                            r_LB[32 * h2:32 * h2 + 1, tb * TBW:(tb + 1) * TBW],
                            nrow[0:1, 0:TBW], AF.Ln))
                    else:
                        e = c.step("scalar", s.activation(
                            r_b[32 * h2:32 * h2 + 1, tb * TBW:(tb + 1) * TBW],
                            pbA[0:1, 0:TBW], AF.Sigmoid))
                    ev[("abe", hf, tb, abi)] = e
                    free[("pabx",)] = e
                c.on("scalar", _abe)

            def _zdone(v, hf=hf, tb=tb):
                ev[("abev", hf, tb)] = ev[("abe", hf, tb, 3)]
                ev[("zdone", hf, tb)] = ev[("abe", hf, tb, 3)]
            _zdone(None)

        def _lrows(v, hf=hf):
            c.wown(v, "vector")
            c.w(v, ev[("abev", hf, NTB - 1)])
            for h2 in range(2):
                d0 = bass.AP(onescf.ap().tensor, 32 * h2,
                             [[1, 1], [0, HLF]])
                c.step("vector", v.tensor_tensor_scan(
                    r_L[32 * h2:32 * h2 + 1, :],
                    d0, r_LB[32 * h2:32 * h2 + 1, :], 0.0,
                    ALU.mult, ALU.add))
                c.wown(v, "vector")
            ev[("lrows", hf)] = ("dv", c.cnt["dv"])
        c.on("vector", _lrows)

        def _lnb(s, hf=hf):
            c.wown(s, "scalar")
            c.w(s, ev[("lrows", hf)])
            for h2 in range(2):
                c.step("scalar", s.activation(
                    r_LB[32 * h2:32 * h2 + 1, :],
                    r_b[32 * h2:32 * h2 + 1, :], AF.Ln))
                c.wown(s, "scalar")
            ev[("lnb", hf)] = ("ac", c.cnt["ac"])
        c.on("scalar", _lnb)

        def _lbadd(v, hf=hf):
            c.wown(v, "vector")
            c.w(v, ev[("lnb", hf)])
            for h2 in range(2):
                c.step("vector", v.tensor_tensor(
                    r_LB[32 * h2:32 * h2 + 1, :],
                    r_LB[32 * h2:32 * h2 + 1, :],
                    r_L[32 * h2:32 * h2 + 1, :], ALU.add))
                c.wown(v, "vector")
            ev[("lb", hf)] = ("dv", c.cnt["dv"])
        c.on("vector", _lbadd)

        def _bnc(g, hf=hf):
            c.w(g, ev[("lb", hf)])
            srcs = {0: r_L, 1: r_LB, 2: r_b}
            for h2 in range(2):
                for ri in range(3):
                    c.w(g, ev.get(("tcp", hf - 1, h2, ri)))
                    ev[("bw", hf, h2, ri)] = c.dma(
                        g, f"q_s{h2}{ri}", scr_d[2 * ri + h2],
                        srcs[ri][32 * h2:32 * h2 + 1, :])
        c.on("gpsimd", _bnc)

        def _rrd(g, hf=hf):
            for h2 in range(2):
                for ri in range(2):
                    k = 2 * ri + h2
                    c.w(g, ev[("bw", hf, h2, ri)])
                    c.w(g, ev.get(("rrfree", hf - 1)))
                    ev[("rrow", hf, k)] = c.dma(
                        g, f"q_rr{k}", rrowf[0:1, k * HLF:(k + 1) * HLF],
                        scr_d[k])
        c.on("gpsimd", _rrd)

        for h2 in range(2):
            for ri, dstt in ((0, lc_tm), (1, lbc_tm), (2, bc_tm)):
                def _brd(g, hf=hf, h2=h2, ri=ri):
                    c.w(g, ev[("bw", hf, h2, ri)])
                    c.w(g, ev.get(("btr_last",)) or ("pe", 0))
                    ev[("br", hf, h2, ri)] = c.dma(
                        g, f"q_r{h2}{ri}", scT[0:NCHH_, :],
                        scr_d[2 * ri + h2].rearrange("(ch p) -> ch p", p=64))
                c.on("gpsimd", _brd)

                def _btr(t, hf=hf, h2=h2, ri=ri):
                    c.w(t, ev[("br", hf, h2, ri)])
                    c.w(t, ev["identf"])
                    if ("btrf",) in ev:
                        c.w(t, ("dv", ev[("btrf",)]))
                    ev[("btr", hf, h2, ri)] = c.step("tensor", t.matmul(
                        pbB[64 * h2:64 * (h2 + 1), 0:NCHH_],
                        scT[0:NCHH_, :], identf[0:NCHH_, 0:NCHH_],
                        start=True, stop=True))
                    ev[("btr_last",)] = ev[("btr", hf, h2, ri)]
                c.on("tensor", _btr)

                def _bev(v, hf=hf, h2=h2, ri=ri, dstt=dstt):
                    c.wown(v, "vector")
                    c.w(v, ev[("btr", hf, h2, ri)])
                    e = c.step("vector", v.tensor_copy(
                        dstt[64 * h2:64 * (h2 + 1), :],
                        pbB[64 * h2:64 * (h2 + 1), 0:NCHH_]))
                    ev[("tcp", hf, h2, ri)] = e
                    ev[("btrf",)] = c.cnt["dv"]
                c.on("vector", _bev)

        def _rowstage(s, hf=hf):
            c.wown(s, "scalar")
            for h2 in range(2):
                for ri in range(3):
                    c.w(s, ev[("tcp", hf, h2, ri)])
            lcs = bass.AP(r_L.ap().tensor, 63, [[HLF, 1], [64, NCHH_]])
            lcs1 = bass.AP(r_L.ap().tensor, 32 * HLF + 63,
                           [[HLF, 1], [64, NCHH_]])
            c.step("scalar", s.copy(nrow[0:1, 0:NCHH_], lcs))
            c.wown(s, "scalar")
            ev[("lcrow", hf)] = c.step(
                "scalar", s.copy(nrow[0:1, NCHH_:2 * NCHH_], lcs1))
        c.on("scalar", _rowstage)

        def _rowprep(v, hf=hf):
            c.wown(v, "vector")
            c.w(v, ev[("lcrow", hf)])
            c.step("vector", v.tensor_copy(
                nrow[0:1, 2 * NCHH_ + 1:3 * NCHH_],
                nrow[0:1, 0:NCHH_ - 1]))
            c.wown(v, "vector")
            c.step("vector", v.memset(nrow[0:1, 2 * NCHH_:2 * NCHH_ + 1], 0.0))
            c.wown(v, "vector")
            c.step("vector", v.tensor_copy(
                nrow[0:1, 3 * NCHH_ + 1:4 * NCHH_],
                nrow[0:1, NCHH_:2 * NCHH_ - 1]))
            c.wown(v, "vector")
            c.step("vector", v.memset(nrow[0:1, 3 * NCHH_:3 * NCHH_ + 1], 0.0))
            c.wown(v, "vector")
            c.step("vector", v.tensor_tensor(
                nrow[0:1, 4 * NCHH_:6 * NCHH_], nrow[0:1, 0:2 * NCHH_],
                nrow[0:1, 2 * NCHH_:4 * NCHH_], ALU.subtract))
            c.wown(v, "vector")
            ev[("rowprep", hf)] = ("dv", c.cnt["dv"])
        c.on("vector", _rowprep)

        def _rowexp(s, hf=hf):
            c.wown(s, "scalar")
            c.w(s, ev[("rowprep", hf)])
            ev[("rowexp", hf)] = c.step("scalar", s.activation(
                nrow[0:1, 4 * NCHH_:6 * NCHH_],
                nrow[0:1, 4 * NCHH_:6 * NCHH_], AF.Exp))
        c.on("scalar", _rowexp)

        def _bcmm(t, hf=hf):
            c.w(t, ev[("rowexp", hf)])
            c.w(t, ev["consts"])
            t.matmul(pbB[0:64, 0:NCHH_], onesrf[0:1, 0:64],
                     nrow[0:1, 0:NCHH_], start=True, stop=True)
            t.matmul(pbB[64:128, 0:NCHH_], onesrf[0:1, 0:64],
                     nrow[0:1, NCHH_:2 * NCHH_], start=True, stop=True)
            t.matmul(pbB[0:64, NCHH_:2 * NCHH_], onesrf[0:1, 0:64],
                     nrow[0:1, 2 * NCHH_:3 * NCHH_], start=True, stop=True)
            t.matmul(pbB[64:128, NCHH_:2 * NCHH_], onesrf[0:1, 0:64],
                     nrow[0:1, 3 * NCHH_:4 * NCHH_], start=True, stop=True)
            t.matmul(pbB[:, 2 * NCHH_:3 * NCHH_], onesrf[0:1, :],
                     nrow[0:1, 4 * NCHH_:5 * NCHH_], start=True, stop=True)
            last = t.matmul(pbB[:, 3 * NCHH_:4 * NCHH_], onesrf[0:1, :],
                            nrow[0:1, 5 * NCHH_:6 * NCHH_],
                            start=True, stop=True)
            ev[("bcmm", hf)] = c.step("tensor", last)
        c.on("tensor", _bcmm)

        def _tiles(v, hf=hf):
            c.wown(v, "vector")
            c.w(v, ev[("bcmm", hf)])
            c.step("vector", v.tensor_copy(lcp_tm[:, :], pbB[:, 0:NCHH_]))
            c.wown(v, "vector")
            c.step("vector", v.tensor_copy(
                lp_tm[:, :], pbB[:, NCHH_:2 * NCHH_]))
            c.wown(v, "vector")
            c.step("vector", v.tensor_copy(
                gcf_tm[:, :], pbB[:, 2 * NCHH_:4 * NCHH_]))
            c.wown(v, "vector")
            c.step("vector", v.tensor_scalar(nl_tm[:, :], lc_tm[:, :], -1.0,
                                             None, ALU.mult))
            c.wown(v, "vector")
            c.step("vector", v.tensor_tensor(g_tm[:, :], lc_tm[:, :],
                                             lp_tm[:, :], ALU.subtract))
            c.wown(v, "vector")
            c.step("vector", v.tensor_tensor(gbn_tm[:, :], lbc_tm[:, :],
                                             lp_tm[:, :], ALU.subtract))
            c.wown(v, "vector")
            c.step("vector", v.tensor_tensor(esc_tm[:, :], lcp_tm[:, :],
                                             lc_tm[:, :], ALU.subtract))
            c.wown(v, "vector")
            ev[("tiles1", hf)] = ("dv", c.cnt["dv"])
        c.on("vector", _tiles)

        def _tilex(s, hf=hf):
            c.wown(s, "scalar")
            c.w(s, ev[("tiles1", hf)])
            c.step("scalar", s.activation(g_tm[:, :], g_tm[:, :], AF.Exp))
            c.wown(s, "scalar")
            c.step("scalar", s.activation(gbn_tm[:, :], gbn_tm[:, :], AF.Exp))
            c.wown(s, "scalar")
            ev[("tilex", hf)] = c.step("scalar", s.activation(
                esc_tm[:, :], esc_tm[:, :], AF.Exp))
        c.on("scalar", _tilex)

        def _tilen(v, hf=hf):
            c.wown(v, "vector")
            c.w(v, ev[("tilex", hf)])
            c.step("vector", v.tensor_scalar(gbn_tm[:, :], gbn_tm[:, :], -1.0,
                                             None, ALU.mult))
            c.wown(v, "vector")
            ev[("ctiles", hf)] = ("dv", c.cnt["dv"])
        c.on("vector", _tilen)

        def _tail(v, hf=hf):
            c.wown(v, "vector")
            c.w(v, ev[("pev", hf, NTB - 1, 7)])
            for ck, (gn, cbse, dbuf, dck) in enumerate(groups):
                off = dck * (HLF + 4) if dbuf is not qb or True else 0
                base = {0: qb, 1: qb, 2: kb, 3: kb}.get(ck)
                c.step("vector", v.tensor_copy(
                    tails[:, ck * 4: ck * 4 + 3],
                    (qb if ck < 2 else kb if ck < 4 else vb)[
                        :, (ck % 2 if ck < 4 else ck - 4) * (HLF + 4) + 4 +
                        HLF - 3: (ck % 2 if ck < 4 else ck - 4) * (HLF + 4) +
                        4 + HLF]))
                c.wown(v, "vector")
            ev[("tails", hf)] = ("dv", c.cnt["dv"])
        c.on("vector", _tail)

        if hf > 0:
            def _pads(v, hf=hf):
                c.wown(v, "vector")
                for ck in range(8):
                    buf = qb if ck < 2 else kb if ck < 4 else vb
                    dck = ck % 2 if ck < 4 else ck - 4
                    c.step("vector", v.tensor_copy(
                        buf[:, dck * (HLF + 4) + 1: dck * (HLF + 4) + 4],
                        tails[:, 32 + ck * 4: 32 + ck * 4 + 3]))
                    c.wown(v, "vector")
                ev[("pads", hf)] = ("dv", c.cnt["dv"])
            c.on("vector", _pads)

        def _tailrot(v, hf=hf):
            c.wown(v, "vector")
            c.w(v, ev[("tails", hf)])
            c.step("vector", v.tensor_copy(tails[:, 32:64], tails[:, 0:32]))
            c.wown(v, "vector")
            ev[("tailrot", hf)] = ("dv", c.cnt["dv"])
        c.on("vector", _tailrot)

        for ck in range(8):
            buf = qb if ck < 2 else kb if ck < 4 else vb
            dck = ck % 2 if ck < 4 else ck - 4
            bof = dck * (HLF + 4)
            for sp in range(NSP):
                def _conv(v, hf=hf, ck=ck, buf=buf, bof=bof, sp=sp):
                    c.wown(v, "vector")
                    c.w(v, ev[("tailrot", hf)])
                    c.w(v, ev["cw"])
                    c.w(v, ev.get(("silu_last",)))
                    if hf > 0:
                        c.w(v, ev[("pads", hf)])
                    vw0 = buf[:, bof + 1 + sp * CSP: bof + 1 + sp * CSP + CSP]
                    c.step("vector", v.tensor_scalar(
                        tmpc[:, :], vw0, cwb[:, ck * 4: ck * 4 + 1], None,
                        ALU.mult))
                    for m in range(1, 4):
                        c.wown(v, "vector")
                        vwm = buf[:, bof + 1 + m + sp * CSP:
                                  bof + 1 + m + sp * CSP + CSP]
                        c.step("vector", v.scalar_tensor_tensor(
                            tmpc[:, :], vwm, cwb[:, ck * 4 + m: ck * 4 + m + 1],
                            tmpc[:, :], ALU.mult, ALU.add))
                    c.wown(v, "vector")
                    ev[("conv", hf, ck, sp)] = ("dv", c.cnt["dv"])
                c.on("vector", _conv)

                def _silu(s, hf=hf, ck=ck, buf=buf, bof=bof, sp=sp):
                    c.wown(s, "scalar")
                    c.w(s, ev[("conv", hf, ck, sp)])
                    c.w(s, ev["cb"])
                    e = c.step("scalar", s.activation(
                        buf[:, bof + 4 + sp * CSP: bof + 4 + (sp + 1) * CSP],
                        tmpc[:, :], AF.Silu, bias=cbb[:, ck:ck + 1]))
                    ev[("silu", hf, ck, sp)] = e
                    ev[("silu_last",)] = e
                c.on("scalar", _silu)
        ev[("convall", hf)] = ("ac", c.cnt["ac"])

        for ck in range(4):
            if A1B_PHASES < 3:
                break
            buf = qb if ck < 2 else kb
            bof = (ck % 2) * (HLF + 4)
            for tb in range(NTB):
                def _sq(s, hf=hf, ck=ck, buf=buf, bof=bof, tb=tb):
                    c.wown(s, "scalar")
                    c.w(s, ev[("convall", hf)])
                    ev[("sq", hf, ck, tb)] = c.step("scalar", s.activation(
                        sqw[:, :], buf[:, bof + 4 + tb * TBW:
                                       bof + 4 + (tb + 1) * TBW], AF.Square))
                c.on("scalar", _sq)

                def _nmm(t, hf=hf, ck=ck, tb=tb):
                    c.w(t, ev[("sq", hf, ck, tb)])
                    c.w(t, ev["consts"])
                    c.w(t, free.get(("pnm",)))
                    ev[("nmm", hf, ck, tb)] = c.step("tensor", t.matmul(
                        pbA[0:1, 0:TBW], onescb[:, :], sqw[:, :],
                        start=True, stop=True))
                c.on("tensor", _nmm)

                def _nrm(v, hf=hf, ck=ck, tb=tb):
                    c.wown(v, "vector")
                    c.w(v, ev[("nmm", hf, ck, tb)])
                    c.step("vector", v.tensor_scalar(
                        nrow[0:1, 0:TBW], pbA[0:1, 0:TBW], 1e-24, None,
                        ALU.max))
                    c.wown(v, "vector")
                    ev[("nr1", hf, ck, tb)] = c.step("vector", v.engine_nop()) \
                        if False else ("dv", c.cnt["dv"])
                    free[("pnm",)] = ("dv", c.cnt["dv"])
                c.on("vector", _nrm)

                def _nsq(s, hf=hf, ck=ck, tb=tb):
                    c.wown(s, "scalar")
                    c.w(s, ev[("nr1", hf, ck, tb)])
                    ev[("nsq", hf, ck, tb)] = c.step("scalar", s.activation(
                        nrow[0:1, TBW:2 * TBW], nrow[0:1, 0:TBW], AF.Sqrt))
                c.on("scalar", _nsq)

                def _nrc(v, hf=hf, ck=ck, tb=tb):
                    c.wown(v, "vector")
                    c.w(v, ev[("nsq", hf, ck, tb)])
                    ev[("nrc", hf, ck, tb)] = c.step("vector", v.reciprocal(
                        nrow[0:1, 0:TBW], nrow[0:1, TBW:2 * TBW]))
                c.on("vector", _nrc)

                def _nbc(t, hf=hf, ck=ck, tb=tb):
                    c.w(t, ev[("nrc", hf, ck, tb)])
                    c.w(t, free.get(("pnb",)))
                    ev[("nbc", hf, ck, tb)] = c.step("tensor", t.matmul(
                        pbB[:, 0:TBW], onesrf[0:1, :], nrow[0:1, 0:TBW],
                        start=True, stop=True))
                c.on("tensor", _nbc)

                def _napp(v, hf=hf, ck=ck, buf=buf, bof=bof, tb=tb):
                    c.wown(v, "vector")
                    c.w(v, ev[("nbc", hf, ck, tb)])
                    sl = buf[:, bof + 4 + tb * TBW: bof + 4 + (tb + 1) * TBW]
                    e = c.step("vector", v.tensor_tensor(
                        sl, sl, pbB[:, 0:TBW], ALU.mult))
                    ev[("napp", hf, ck, tb)] = e
                    free[("pnb",)] = e
                c.on("vector", _napp)
        ev[("normall", hf)] = ("dv", c.cnt["dv"])

        for ck in range(4):
            bof = ck * (HLF + 4)
            for tb2 in range(HLF // 128):
                def _vtr(t, hf=hf, ck=ck, bof=bof, tb2=tb2):
                    if ck == 0 and tb2 == 0:
                        c.w(t, ev[("convall", hf)])
                        c.w(t, ev["ident"])
                    if ("ptf", tb2 % 2) in ev:
                        c.w(t, ("dv", ev[("ptf", tb2 % 2)]))
                    ev[("vtr", hf, ck, tb2)] = c.step("tensor", t.transpose(
                        pt[tb2 % 2][:, 0:128],
                        vb[:, bof + 4 + tb2 * 128: bof + 4 + (tb2 + 1) * 128],
                        identb[:, :]))
                c.on("tensor", _vtr)

                def _vev(v, hf=hf, ck=ck, bof=bof, tb2=tb2):
                    c.wown(v, "vector")
                    c.w(v, ev[("vtr", hf, ck, tb2)])
                    c.step("vector", v.tensor_copy(
                        vb[:, bof + 4 + tb2 * 128: bof + 4 + (tb2 + 1) * 128],
                        pt[tb2 % 2][:, 0:128]))
                    ev[("ptf", tb2 % 2)] = c.cnt["dv"]
                c.on("vector", _vev)
        ev[("vtr_done", hf)] = ("dv", c.cnt["dv"])

        if A1B_PHASES >= 4:
            for ch in range(min(NCHH, A1B_NCH)):
                t0 = ch * 64
                emit_chunk(c, ev, free, locals(), hf, ch, t0, scur)
                scur[0] ^= 1
            lastch = min(NCHH, A1B_NCH) - 1
            ev[("scan_done", hf)] = ev[("ochunk", hf, lastch)]
            ev[("rrfree", hf)] = (ev.get(("dM_mm", hf, lastch))
                                  or ev.get(("dA_mm", hf, lastch))
                                  or ev[("ochunk", hf, lastch)])

        def _sto(e, hf=hf):
            if A1B_PHASES >= 4:
                c.w(e, ev[("ochunk", hf, min(NCHH, A1B_NCH) - 1)])
            else:
                c.w(e, ("dv", c.cnt["dv"]))
                c.w(e, ("ac", c.cnt["ac"]))
            dst = o_d[:, hf * HLF:(hf + 1) * HLF].rearrange(
                "(ec p) t -> p ec t", p=128)
            ev[("ost", hf)] = c.dma(
                e, "q_o", dst,
                ob[:, :].rearrange("p (ec t) -> p ec t", t=HLF))
            c.w(e, ev[("ost", hf)])
        c.on("sync", _sto)

    def _fin(e):
        c.w(e, ev[("ost", min(NPART, A1B_NPT) - 1)])
    c.on("sync", _fin)

    c.close()
    for h in reversed(hold):
        h.__exit__(None, None, None)
    return nc


def _attn1_consts():
    eyem = np.zeros((128, 128), np.float32)
    for p in range(128):
        eyem[p, p % 64] = 1.0
        eyem[p, 64 + p % 64] = 1.0
    r = np.arange(128)
    blk = (r[:, None] // 64) == (r[None, :] // 64)
    low = r[:, None] > r[None, :]
    upi = r[:, None] <= r[None, :]
    mskA = np.where(blk & low, 0.0, 1e30).astype(np.float32)
    mskAT = np.where(blk & (~low) & (~np.eye(128, dtype=bool)), 0.0,
                     -1e30).astype(np.float32)
    mskM = np.where(blk & upi, 0.0, -1e30).astype(np.float32)
    return eyem, mskA, mskAT, mskM


def _run_attn1(x_full, inp):
    nc_a = _get("attn1a", build_attn1a)
    nc_b = _get("attn1b", build_attn1b)
    nc_c = _get("attn1c", build_attn1c)
    shards = x_full.reshape(NCORES, TPC, D)
    in_a = [{"x": np.ascontiguousarray(shards[i]), "ident": _IDENT}
            for i in range(NCORES)]
    ra = run_bass_kernel_spmd(nc_a, in_a, core_ids=list(range(NCORES)))
    z4 = np.concatenate([r["z4"] for r in ra.results], 1)
    z4 = np.ascontiguousarray(z4)

    eyem, mskA, mskAT, mskM = _attn1_consts()
    Wq, Wk, Wv = inp["L1_attn_Wq"], inp["L1_attn_Wk"], inp["L1_attn_Wv"]
    Wa, Wb = inp["L1_attn_Wa"], inp["L1_attn_Wb"]
    qcw, qcb = inp["L1_attn_qcw"], inp["L1_attn_qcb"]
    kcw, kcb = inp["L1_attn_kcw"], inp["L1_attn_kcb"]
    vcw, vcb = inp["L1_attn_vcw"], inp["L1_attn_vcb"]
    in_b = []
    for i in range(NCORES):
        h0 = 2 * i
        wall = np.concatenate(
            [Wq[:, h0 * 128:(h0 + 2) * 128], Wk[:, h0 * 128:(h0 + 2) * 128],
             Wv[:, h0 * 256:(h0 + 2) * 256], Wa[:, h0:h0 + 2],
             Wb[:, h0:h0 + 2]], axis=1)
        cwh = np.concatenate(
            [qcw[h0 * 128:(h0 + 2) * 128], kcw[h0 * 128:(h0 + 2) * 128],
             vcw[h0 * 256:(h0 + 2) * 256]], axis=0)
        cwh = np.ascontiguousarray(cwh.reshape(8, 128, 4).transpose(1, 0, 2))
        cbh = np.concatenate(
            [qcb[h0 * 128:(h0 + 2) * 128], kcb[h0 * 128:(h0 + 2) * 128],
             vcb[h0 * 256:(h0 + 2) * 256]])
        cbh = np.ascontiguousarray(cbh.reshape(8, 128).T)
        id64 = np.ascontiguousarray(
            np.concatenate([np.eye(64, dtype=np.float32)] * 2, 0))
        in_b.append({"z4": z4, "wall": np.ascontiguousarray(wall),
                     "cwh": cwh, "cbh": cbh, "ident": _IDENT, "eyem": eyem,
                     "mskA": mskA, "mskAT": mskAT, "mskM": mskM,
                     "id64": id64})
    rb = run_bass_kernel_spmd(nc_b, in_b, core_ids=list(range(NCORES)))
    o_full = np.ascontiguousarray(
        np.concatenate([r["o"] for r in rb.results], 0))
    wo = np.ascontiguousarray(inp["L1_attn_Wo"])
    in_c = [{"x": np.ascontiguousarray(shards[i]),
             "ofm": np.ascontiguousarray(o_full[:, i * TPC:(i + 1) * TPC]),
             "wo": wo} for i in range(NCORES)]
    rc = run_bass_kernel_spmd(nc_c, in_c, core_ids=list(range(NCORES)))
    return np.concatenate([r["xo"] for r in rc.results], 0)


# version 05
# speedup vs baseline: 1.0031x; 1.0031x over previous
import numpy as np
import concourse.bass as bass
import concourse.mybir as mybir
from concourse.bass_utils import run_bass_kernel_spmd

F32 = mybir.dt.float32
BF16 = mybir.dt.bfloat16
AF = mybir.ActivationFunctionType
ALU = mybir.AluOpType
AX = mybir.AxisListType

NCORES = 8
T, D = 8192, 2048
TPC = T // NCORES
NTILE = TPC // 128
DI = 8192
LN_EPS = 1e-5

ENG_SEM = {"vector": "dv", "scalar": "ac", "gpsimd": "gp", "tensor": "pe"}


class Ctx:

    def __init__(self, nc):
        self.nc = nc
        self.hold = []
        self.blk = nc.Block()
        self.block = self.blk.__enter__()
        self.sems = {}
        self.cnt = {}
        for n in ["pe", "dv", "ac", "gp"]:
            self._mksem(n)

    def _mksem(self, n):
        h = self.nc.semaphore(n)
        self.sems[n] = h.__enter__()
        self.hold.append(h)
        self.cnt[n] = 0

    def qsem(self, n):
        if n not in self.sems:
            self._mksem(n)
        return n

    def close(self):
        self.blk.__exit__(None, None, None)
        for h in reversed(self.hold):
            h.__exit__(None, None, None)

    def on(self, engine, fn):
        getattr(self.block, engine)(fn)

    def w(self, eng, ev):
        if ev is None:
            return
        sem, val = ev
        if val > 0:
            eng.wait_ge(self.sems[sem], val)

    def wown(self, eng, engine_name):
        sem = ENG_SEM[engine_name]
        if self.cnt[sem] > 0:
            eng.wait_ge(self.sems[sem], self.cnt[sem])

    def step(self, engine_name, instr):
        sem = ENG_SEM[engine_name]
        instr.then_inc(self.sems[sem], 1)
        self.cnt[sem] += 1
        return (sem, self.cnt[sem])

    def dma(self, eng, slot_sem, out, in_):
        s = self.qsem(slot_sem)
        if self.cnt[s] > 0:
            eng.wait_ge(self.sems[s], self.cnt[s])
        i = eng.dma_start(out=out, in_=in_)
        i.then_inc(self.sems[s], 16)
        self.cnt[s] += 16
        return (s, self.cnt[s])


def sb(nc, hold, name, shape, dt=F32):
    h = nc.sbuf_tensor(name, shape, dt)
    t = h.__enter__()
    hold.append(h)
    return t


def psum(nc, hold, name, shape, dt=F32):
    h = nc.psum_tensor(name, shape, dt)
    t = h.__enter__()
    hold.append(h)
    return t



def build_ffn0():
    nc = bass.Bass("TRN2", target_bir_lowering=False)
    hold = []
    x_d = nc.dram_tensor("x", [TPC, D], F32, kind="ExternalInput")
    wg_d = nc.dram_tensor("wg", [D, DI], F32, kind="ExternalInput")
    wu_d = nc.dram_tensor("wu", [D, DI], F32, kind="ExternalInput")
    wd_d = nc.dram_tensor("wd", [DI, D], F32, kind="ExternalInput")
    id_d = nc.dram_tensor("ident", [128, 128], F32, kind="ExternalInput")
    xo_d = nc.dram_tensor("xo", [TPC, D], F32, kind="ExternalOutput")

    xs = sb(nc, hold, "xs", [128, 4 * D])
    ztmp = sb(nc, hold, "ztmp", [128, 2 * D], BF16)
    zfm = sb(nc, hold, "zfm", [128, 16 * 512], BF16)
    hsp = sb(nc, hold, "hsp", [128, 4 * DI], BF16)
    wgb = sb(nc, hold, "wgb", [128, 2 * 8192], BF16)
    wub = sb(nc, hold, "wub", [128, 8192], BF16)
    sg = sb(nc, hold, "sg", [128, 2 * 512])
    tmpa = sb(nc, hold, "tmpa", [128, 1024], BF16)
    tmpm = sb(nc, hold, "tmpm", [128, 1024], BF16)
    tmpm2 = sb(nc, hold, "tmpm2", [128, 1024], BF16)
    tmps = sb(nc, hold, "tmps", [128, 256], BF16)
    tmpk = sb(nc, hold, "tmpk", [128, 1024], BF16)
    bn6 = sb(nc, hold, "bn6", [128, 24])
    mv = sb(nc, hold, "mv", [128, 2])
    sd = sb(nc, hold, "sd", [128, 1])
    rstd = sb(nc, hold, "rstd", [128, 1])
    epst = sb(nc, hold, "epst", [128, 1])
    identb = sb(nc, hold, "identb", [128, 128], BF16)

    p = [psum(nc, hold, f"p{i}", [128, 512]) for i in range(4)]
    pt = [psum(nc, hold, f"pt{i}", [128, 1024], BF16) for i in range(2)]

    c = Ctx(nc)
    ev = {}
    free = {}

    def _pre(g):
        ev["ident"] = c.dma(g, "q_id", identb[:, :], id_d[:, :])
    c.on("gpsimd", _pre)

    def _pre_v(v):
        ev["eps"] = c.step("vector", v.memset(epst[:, :], LN_EPS))
    c.on("vector", _pre_v)

    NB = DI // 512
    NC64 = DI // 128
    NIG = DI // 1024
    NDB = D // 512
    for half in range(2):
        tiles = [half * 4 + i for i in range(4)]

        for idx, j in enumerate(tiles):
            def _ld(e, idx=idx, j=j):
                c.w(e, ev.get(("xstored", half - 1, idx)))
                ev[("xld", half, idx)] = c.dma(
                    e, f"q_x{idx}", xs[:, idx * D:(idx + 1) * D],
                    x_d[j * 128:(j + 1) * 128, :])
            c.on("sync", _ld)

        for idx, j in enumerate(tiles):
            xsl = xs[:, idx * D:(idx + 1) * D]
            zsl = ztmp[:, (idx % 2) * D:((idx % 2) + 1) * D]

            def _stats(v, xsl=xsl, idx=idx):
                c.wown(v, "vector")
                c.w(v, ev[("xld", half, idx)])
                for k4 in range(4):
                    c.step("vector", v.bn_stats(bn6[:, k4 * 6:(k4 + 1) * 6],
                                                xsl[:, k4 * 512:(k4 + 1) * 512]))
                c.wown(v, "vector")
                c.w(v, ev.get(("sd", half, idx - 1)) if idx else
                    ev.get(("sd", half - 1, 3)))
                ev[("mv", half, idx)] = c.step(
                    "vector", v.bn_aggr(mv[:, :], bn6[:, :]))
            c.on("vector", _stats)

            def _sqrt(s, idx=idx):
                c.wown(s, "scalar")
                c.w(s, ev[("mv", half, idx)])
                ev[("sd", half, idx)] = c.step(
                    "scalar",
                    s.activation(sd[:, :], mv[:, 1:2], AF.Sqrt, bias=epst[:, :]))
            c.on("scalar", _sqrt)

            def _apply(v, xsl=xsl, zsl=zsl, idx=idx):
                c.wown(v, "vector")
                c.w(v, ev[("sd", half, idx)])
                c.step("vector", v.reciprocal(rstd[:, :], sd[:, :]))
                c.wown(v, "vector")
                key = ("ztr_done", half, idx - 2) if idx >= 2 else \
                      ("ztr_done", half - 1, idx + 2)
                if key in ev:
                    c.w(v, ("pe", ev[key]))
                ev[("z", half, idx)] = c.step(
                    "vector",
                    v.tensor_scalar(zsl, xsl, mv[:, 0:1], rstd[:, :],
                                    ALU.subtract, ALU.mult))
            c.on("vector", _apply)

            for ch in range(16):
                def _trc(t, zsl=zsl, idx=idx, ch=ch):
                    if ch == 0:
                        c.w(t, ev[("z", half, idx)])
                        c.w(t, ev["ident"])
                    if ("pt_free", ch % 2) in ev:
                        c.w(t, ("dv", ev[("pt_free", ch % 2)]))
                    ev[("ztrp", half, idx, ch)] = c.step(
                        "tensor", t.transpose(
                            pt[ch % 2][:, 0:128],
                            zsl[:, ch * 128:(ch + 1) * 128], identb[:, :]))
                c.on("tensor", _trc)

                def _evc(v, idx=idx, ch=ch):
                    c.wown(v, "vector")
                    c.w(v, ev[("ztrp", half, idx, ch)])
                    c.step("vector", v.tensor_copy(
                        zfm[:, ch * 512 + idx * 128: ch * 512 + (idx + 1) * 128],
                        pt[ch % 2][:, 0:128]))
                    ev[("pt_free", ch % 2)] = c.cnt["dv"]
                c.on("vector", _evc)
            ev[("ztr_done", half, idx)] = c.cnt["pe"]
        ev[("zfm_ready", half)] = ("dv", c.cnt["dv"])

        for b in range(NB):
            def _ldw(g, b=b):
                c.w(g, free.get(("wgb_free", b % 2)))
                if b % 2 == 0:
                    c.w(g, free.get(("wdslot_free", 0)))
                    c.w(g, free.get(("wdslot_free", 1)))
                src = wg_d[:, b * 512:(b + 1) * 512].rearrange(
                    "(c p) i -> p c i", p=128)
                dst = wgb[:, (b % 2) * 8192:(b % 2 + 1) * 8192].rearrange(
                    "pp (c i) -> pp c i", i=512)
                ev[("wg", half, b)] = c.dma(g, f"q_wg{b % 2}", dst, src)
                c.w(g, free.get(("wub_free",)))
                src2 = wu_d[:, b * 512:(b + 1) * 512].rearrange(
                    "(c p) i -> p c i", p=128)
                ev[("wu", half, b)] = c.dma(
                    g, "q_wu", wub[:, :].rearrange("pp (c i) -> pp c i", i=512), src2)
            c.on("gpsimd", _ldw)

            for jj in range(4):
                gb = p[jj % 2]
                ub = p[2 + jj % 2]

                def _mm(t, b=b, jj=jj, gb=gb, ub=ub):
                    c.w(t, ev[("zfm_ready", half)])
                    c.w(t, ev[("wg", half, b)])
                    c.w(t, free.get(("pfree", jj % 2)))
                    last = None
                    for ch in range(16):
                        last = t.matmul(
                            gb[:, :],
                            zfm[:, ch * 512 + jj * 128: ch * 512 + (jj + 1) * 128],
                            wgb[:, (b % 2) * 8192 + ch * 512:
                                (b % 2) * 8192 + (ch + 1) * 512],
                            start=(ch == 0), stop=(ch == 15))
                    ev[("gmm", half, b, jj)] = c.step("tensor", last)
                    c.w(t, ev[("wu", half, b)])
                    c.w(t, free.get(("pfree", 2 + jj % 2)))
                    for ch in range(16):
                        last = t.matmul(
                            ub[:, :],
                            zfm[:, ch * 512 + jj * 128: ch * 512 + (jj + 1) * 128],
                            wub[:, ch * 512:(ch + 1) * 512],
                            start=(ch == 0), stop=(ch == 15))
                    ev[("umm", half, b, jj)] = c.step("tensor", last)
                    if jj == 3:
                        free[("wgb_free", b % 2)] = ev[("gmm", half, b, jj)]
                        free[("wub_free",)] = ev[("umm", half, b, jj)]
                c.on("tensor", _mm)

                def _silu(s, b=b, jj=jj, gb=gb):
                    c.wown(s, "scalar")
                    c.w(s, ev[("gmm", half, b, jj)])
                    c.w(s, free.get(("sg_free", jj % 2)))
                    ev[("sg", half, b, jj)] = c.step(
                        "scalar",
                        s.activation(sg[:, (jj % 2) * 512:(jj % 2 + 1) * 512],
                                     gb[:, :], AF.Silu))
                    free[("pfree", jj % 2)] = ev[("sg", half, b, jj)]
                c.on("scalar", _silu)

                def _h(v, b=b, jj=jj, ub=ub):
                    c.wown(v, "vector")
                    c.w(v, ev[("sg", half, b, jj)])
                    c.w(v, ev[("umm", half, b, jj)])
                    ev[("hev", half, b, jj)] = c.step(
                        "vector",
                        v.tensor_tensor(
                            hsp[:, jj * DI + b * 512: jj * DI + (b + 1) * 512],
                            sg[:, (jj % 2) * 512:(jj % 2 + 1) * 512],
                            ub[:, :], ALU.mult))
                    free[("pfree", 2 + jj % 2)] = ev[("hev", half, b, jj)]
                    free[("sg_free", jj % 2)] = ev[("hev", half, b, jj)]
                c.on("vector", _h)

        for jj in range(4):
            for s4 in range(NIG):
                base = jj * DI + s4 * 1024
                hspan = hsp[:, base: base + 1024]

                def _spa(s, jj=jj, s4=s4, hspan=hspan):
                    c.wown(s, "scalar")
                    c.w(s, ev[("hev", half, 2 * s4 + 1, jj)])
                    c.w(s, ev.get(("sp_aused",)))
                    ev[("spabs", half, jj, s4)] = c.step(
                        "scalar", s.activation(tmpa[:, :], hspan, AF.Abs))
                c.on("scalar", _spa)

                def _sp(v, jj=jj, s4=s4, hspan=hspan):
                    def st(mk):
                        c.wown(v, "vector")
                        return c.step("vector", mk())
                    c.wown(v, "vector")
                    c.w(v, ev[("spabs", half, jj, s4)])
                    a0 = tmpa[:, 0::4]; a1 = tmpa[:, 1::4]
                    a2 = tmpa[:, 2::4]; a3 = tmpa[:, 3::4]
                    st(lambda: v.tensor_tensor(tmpm[:, 0:256], a0, a1, ALU.max))
                    st(lambda: v.tensor_tensor(tmpm[:, 256:512], a2, a3, ALU.max))
                    st(lambda: v.tensor_tensor(tmpm[:, 512:768], a0, a1, ALU.min))
                    st(lambda: v.tensor_tensor(tmpm[:, 768:1024], a2, a3, ALU.min))
                    m01 = tmpm[:, 0:256]; m23 = tmpm[:, 256:512]
                    n01 = tmpm[:, 512:768]; n23 = tmpm[:, 768:1024]
                    st(lambda: v.tensor_tensor(tmpm2[:, 0:256], m01, m23, ALU.max))
                    st(lambda: v.tensor_tensor(tmpm2[:, 256:512], m01, m23, ALU.min))
                    st(lambda: v.tensor_tensor(tmpm2[:, 512:768], n01, n23, ALU.max))
                    mx = tmpm2[:, 0:256]; mn = tmpm2[:, 256:512]
                    nx = tmpm2[:, 512:768]
                    st(lambda: v.tensor_tensor(tmpm2[:, 768:1024], mx, nx, ALU.min))
                    st(lambda: v.tensor_tensor(tmps[:, :], mn, tmpm2[:, 768:1024], ALU.max))
                    sec_b = bass.AP(tmps.ap().tensor, 0,
                                    [[256, 128], [1, 256], [0, 4]])
                    av = tmpa[:, :].rearrange("p (g j) -> p g j", j=4)
                    kv = tmpk[:, :].rearrange("p (g j) -> p g j", j=4)
                    ev[("sp_aused",)] = st(
                        lambda: v.tensor_tensor(kv, av, sec_b, ALU.is_ge))
                    ev[("sp", half, jj, s4)] = st(
                        lambda: v.tensor_tensor(hspan, hspan, tmpk[:, :], ALU.mult))
                c.on("vector", _sp)

        for jj in range(4):
            for ch in range(NC64):
                def _trh(t, jj=jj, ch=ch):
                    if ch == 0:
                        c.w(t, ev[("sp", half, jj, DI // 1024 - 1)])
                    if ("pt_free", ch % 2) in ev:
                        c.w(t, ("dv", ev[("pt_free", ch % 2)]))
                    ev[("htrp", half, jj, ch)] = c.step(
                        "tensor", t.transpose(
                            pt[ch % 2][:, 0:128],
                            hsp[:, jj * DI + ch * 128: jj * DI + (ch + 1) * 128],
                            identb[:, :]))
                c.on("tensor", _trh)

                def _evh(v, jj=jj, ch=ch):
                    c.wown(v, "vector")
                    c.w(v, ev[("htrp", half, jj, ch)])
                    c.step("vector", v.tensor_copy(
                        hsp[:, jj * DI + ch * 128: jj * DI + (ch + 1) * 128],
                        pt[ch % 2][:, 0:128]))
                    ev[("pt_free", ch % 2)] = c.cnt["dv"]
                c.on("vector", _evh)
            ev[("htr_done", half, jj)] = ("dv", c.cnt["dv"])

        for db in range(NDB):
            for icg in range(NIG):
                def _ldwd(g, db=db, icg=icg):
                    k = db * NIG + icg
                    slot = k % 2
                    if ("wdslot_free", slot) in free:
                        c.w(g, free[("wdslot_free", slot)])
                    elif k < 2:
                        c.w(g, ev[("gmm", half, NB - 1, 3)])
                    src = wd_d[icg * 1024:(icg + 1) * 1024,
                               db * 512:(db + 1) * 512].rearrange(
                        "(c p) d -> p c d", p=128)
                    dst = wgb[:, slot * 4096:(slot + 1) * 4096].rearrange(
                        "pp (c d) -> pp c d", d=512)
                    ev[("wd", half, k)] = c.dma(g, f"q_wg{slot}", dst, src)
                c.on("gpsimd", _ldwd)

                def _dmm(t, db=db, icg=icg):
                    k = db * NIG + icg
                    slot = k % 2
                    c.w(t, ev[("wd", half, k)])
                    last = None
                    for jj in range(4):
                        if icg == 0:
                            c.w(t, ev[("htr_done", half, jj)])
                            c.w(t, free.get(("dp_free", jj)))
                            c.w(t, ev[("hev", half, NB - 1, 3)])
                            c.w(t, ev[("sg", half, NB - 1, 3)])
                        for ch in range(8):
                            last = t.matmul(
                                p[jj][:, :],
                                hsp[:, jj * DI + (icg * 8 + ch) * 128:
                                    jj * DI + (icg * 8 + ch + 1) * 128],
                                wgb[:, slot * 4096 + ch * 512:
                                    slot * 4096 + (ch + 1) * 512],
                                start=(icg == 0 and ch == 0),
                                stop=(icg == NIG - 1 and ch == 7))
                    ev[("dmm", half, k)] = c.step("tensor", last)
                    free[("wdslot_free", slot)] = ev[("dmm", half, k)]
                c.on("tensor", _dmm)

            for jj in range(4):
                def _dev(v, db=db, jj=jj):
                    c.wown(v, "vector")
                    c.w(v, ev[("dmm", half, db * NIG + NIG - 1)])
                    xsl = xs[:, jj * D + db * 512: jj * D + (db + 1) * 512]
                    ev[("resid", half, db, jj)] = c.step(
                        "vector",
                        v.scalar_tensor_tensor(xsl, p[jj][:, :], 0.5, xsl,
                                               ALU.mult, ALU.add))
                    free[("dp_free", jj)] = ev[("resid", half, db, jj)]
                    if db == NDB - 1:
                        free[("pfree", jj % 2)] = ev[("resid", half, db, jj)]
                        free[("pfree", 2 + jj % 2)] = ev[("resid", half, db, jj)]
                c.on("vector", _dev)

        for idx, j in enumerate(tiles):
            def _st(e, idx=idx, j=j):
                for db in range(NDB):
                    c.w(e, ev[("resid", half, db, idx)])
                ev[("xstored", half, idx)] = c.dma(
                    e, f"q_xst{idx}", xo_d[j * 128:(j + 1) * 128, :],
                    xs[:, idx * D:(idx + 1) * D])
            c.on("sync", _st)

    def _fin(e):
        for idx in range(4):
            c.w(e, ev[("xstored", 1, idx)])
    c.on("sync", _fin)

    c.close()
    for h in reversed(hold):
        h.__exit__(None, None, None)
    return nc



_IDENT = np.eye(128, dtype=np.float32)
_CACHE = {}


def _get(name, builder):
    if name not in _CACHE:
        _CACHE[name] = builder()
    return _CACHE[name]


def _run_ffn0(x_full, wg, wu, wd):
    nc = _get("ffn0", build_ffn0)
    shards = x_full.reshape(NCORES, TPC, D)
    in_maps = [{"x": np.ascontiguousarray(shards[i]), "wg": wg, "wu": wu,
                "wd": wd, "ident": _IDENT} for i in range(NCORES)]
    res = run_bass_kernel_spmd(nc, in_maps, core_ids=list(range(NCORES)))
    return np.concatenate([r["xo"] for r in res.results], 0)


def _rope_tables(core):
    NT = TPC // 128
    t = np.arange(core * TPC, (core + 1) * TPC, dtype=np.float64)
    invf = 1.0 / (10000.0 ** (np.arange(0, 128, 2, dtype=np.float64) / 128.0))
    ang = (t[:, None] * invf[None, :]).reshape(NT, 128, 64)
    tab = np.stack([np.cos(ang), np.sin(ang)], axis=1)
    return np.ascontiguousarray(tab.transpose(2, 0, 1, 3)).astype(np.float32)


def _run_attn0(x_full, inp):
    nc_a = _get("attn0a", build_attn0a)
    nc_b = _get("attn0b", build_attn0b)
    shards = x_full.reshape(NCORES, TPC, D)
    wq = np.ascontiguousarray(inp["L0_attn_Wq"].T)
    wk = np.ascontiguousarray(inp["L0_attn_Wk"].T)
    wv = np.ascontiguousarray(inp["L0_attn_Wv"].T)
    wg = np.ascontiguousarray(inp["L0_attn_Wg"].T)
    wo = np.ascontiguousarray(inp["L0_attn_Wo"].T)
    bias = np.ascontiguousarray(np.stack(
        [inp["L0_attn_bq"], inp["L0_attn_bk"], inp["L0_attn_bv"],
         inp["L0_attn_bg"]]))
    bo = np.ascontiguousarray(inp["L0_attn_bo"].reshape(1, D))
    in_a = [{"x": np.ascontiguousarray(shards[i]), "wq": wq, "wk": wk,
             "wv": wv, "wgt": wg, "bias": bias, "rope": _rope_tables(i),
             "ident": _IDENT} for i in range(NCORES)]
    ra = run_bass_kernel_spmd(nc_a, in_a, core_ids=list(range(NCORES)))
    zS = np.zeros((128, 16, 128), np.float32)
    zZ = np.zeros((128, 16), np.float32)
    in_b = []
    for i in range(NCORES):
        dsS = np.stack([ra.results[r]["ds"] if r < i else zS
                        for r in range(NCORES)]).transpose(2, 1, 0, 3)
        dsZ = np.stack([ra.results[r]["dz"] if r < i else zZ
                        for r in range(NCORES)]).transpose(1, 0, 2)
        in_b.append({"x": np.ascontiguousarray(shards[i]),
                     "qf": ra.results[i]["qf"], "kt": ra.results[i]["kt"],
                     "vt": ra.results[i]["vt"], "gt": ra.results[i]["gt"],
                     "dsS": np.ascontiguousarray(dsS),
                     "dsZ": np.ascontiguousarray(dsZ),
                     "wo": wo, "bo": bo, "ident": _IDENT})
    rb = run_bass_kernel_spmd(nc_b, in_b, core_ids=list(range(NCORES)))
    return np.concatenate([r["xo"] for r in rb.results], 0)


_SWT_CACHE = {}


def _run_ffn1(x_full, inp, want_z4=False):
    nc_a = _get("ffn1a", build_ffn1a)
    nc_b = (_get("ffn1bz", lambda: build_ffn1b(True)) if want_z4
            else _get("ffn1b", build_ffn1b))
    shards = x_full.reshape(NCORES, TPC, D)
    key = id(inp["L1_ffn_switch_w"])
    if key not in _SWT_CACHE:
        _SWT_CACHE.clear()
        sw = inp["L1_ffn_switch_w"].reshape(8, 2048, D)
        _SWT_CACHE[key] = [
            np.ascontiguousarray(sw[:, i * 256:(i + 1) * 256, :].transpose(0, 2, 1))
            for i in range(NCORES)]
    swt = _SWT_CACHE[key]
    xr = x_full[:2048].reshape(NCORES, 256, D)
    in_a = [{"xr": np.ascontiguousarray(xr[i]), "swt": swt[i],
             "ident": _IDENT} for i in range(NCORES)]
    ra = run_bass_kernel_spmd(nc_a, in_a, core_ids=list(range(NCORES)))
    pl8 = np.ascontiguousarray(
        np.stack([ra.results[i]["pl"][0] for i in range(NCORES)]))
    w1h = np.ascontiguousarray(inp["L1_ffn_w1"].transpose(2, 0, 1))
    w2h = np.ascontiguousarray(inp["L1_ffn_w2"].transpose(2, 0, 1))
    b1h = np.ascontiguousarray(inp["L1_ffn_b1"].T)
    b2h = np.ascontiguousarray(inp["L1_ffn_b2"].T)
    w3t = np.ascontiguousarray(inp["L1_ffn_w3"].T)
    b3 = np.ascontiguousarray(inp["L1_ffn_b3"].reshape(1, D))
    swb = np.ascontiguousarray(inp["L1_ffn_switch_b"].reshape(1, 8))
    in_b = [{"x": np.ascontiguousarray(shards[i]), "pl8": pl8, "swb": swb,
             "w1h": w1h, "b1h": b1h, "w2h": w2h, "b2h": b2h, "w3t": w3t,
             "b3": b3, "ident": _IDENT} for i in range(NCORES)]
    rb = run_bass_kernel_spmd(nc_b, in_b, core_ids=list(range(NCORES)))
    xo = np.concatenate([r["xo"] for r in rb.results], 0)
    if want_z4:
        z4 = np.ascontiguousarray(
            np.concatenate([r["z4o"] for r in rb.results], 1))
        return xo, z4
    return xo


def kernel(**inp):
    x = np.ascontiguousarray(
        inp["L0_attn_x"].reshape(T, D)).astype(np.float32, copy=False)
    x1 = _run_ffn0(x, inp["L0_ffn_w_gate"], inp["L0_ffn_w_up"],
                   inp["L0_ffn_w_down"])
    x2 = _run_attn0(x1, inp)
    x3 = _run_ffn0(x2, inp["L0_ffn_w_gate"], inp["L0_ffn_w_up"],
                   inp["L0_ffn_w_down"])
    x4, z4 = _run_ffn1(x3, inp, want_z4=True)
    x5 = _run_attn1(x4, inp, z4=z4)
    x6 = _run_ffn1(x5, inp)
    return x6.reshape(1, T, D)



A0A_PHASES = 9


def build_attn0a():
    nc = bass.Bass("TRN2", target_bir_lowering=False)
    hold = []
    NT = TPC // 128
    NCH = TPC // 64
    x_d = nc.dram_tensor("x", [TPC, D], F32, kind="ExternalInput")
    wq_d = nc.dram_tensor("wq", [D, D], F32, kind="ExternalInput")
    wk_d = nc.dram_tensor("wk", [D, D], F32, kind="ExternalInput")
    wv_d = nc.dram_tensor("wv", [D, D], F32, kind="ExternalInput")
    wgt_d = nc.dram_tensor("wgt", [D, D], F32, kind="ExternalInput")
    bias_d = nc.dram_tensor("bias", [4, D], F32, kind="ExternalInput")
    rope_d = nc.dram_tensor("rope", [128, NT, 2, 64], F32, kind="ExternalInput")
    id_d = nc.dram_tensor("ident", [128, 128], F32, kind="ExternalInput")
    qf_d = nc.dram_tensor("qf", [D, TPC], BF16, kind="ExternalOutput")
    kt_d = nc.dram_tensor("kt", [TPC, D], BF16, kind="ExternalOutput")
    vt_d = nc.dram_tensor("vt", [TPC, D], BF16, kind="ExternalOutput")
    gt_d = nc.dram_tensor("gt", [TPC, D], BF16, kind="ExternalOutput")
    ds_d = nc.dram_tensor("ds", [128, 16, 128], F32, kind="ExternalOutput")
    dz_d = nc.dram_tensor("dz", [128, 16], F32, kind="ExternalOutput")

    xs = sb(nc, hold, "xs", [128, D])
    ztmp = sb(nc, hold, "ztmp", [128, D], BF16)
    zfm = sb(nc, hold, "zfm", [128, 16 * TPC], BF16)
    wb = sb(nc, hold, "wb", [128, 2 * 8192], BF16)
    qk = sb(nc, hold, "qk", [128, NT * D], BF16)
    vtm = sb(nc, hold, "vtm", [128, NT * D], BF16)
    gsl = sb(nc, hold, "gsl", [128, (TPC // 128) * D], BF16)
    ropes = sb(nc, hold, "ropes", [128, NT * 2 * 64])
    tr1 = sb(nc, hold, "tr1", [128, 1024], BF16)
    tr2 = sb(nc, hold, "tr2", [128, 1024], BF16)
    te1 = sb(nc, hold, "te1", [128, 2048], BF16)
    te2 = sb(nc, hold, "te2", [128, 2048], BF16)
    biasb = sb(nc, hold, "biasb", [1, D], BF16)
    onesr = sb(nc, hold, "onesr", [1, 128], BF16)
    onesc = sb(nc, hold, "onesc", [128, 1], BF16)
    dsout = sb(nc, hold, "dsout", [128, 16 * 128])
    dzout = sb(nc, hold, "dzout", [128, 16])
    bn6 = sb(nc, hold, "bn6", [128, 24])
    mv = sb(nc, hold, "mv", [128, 2])
    sd = sb(nc, hold, "sd", [128, 1])
    rstd = sb(nc, hold, "rstd", [128, 1])
    epst = sb(nc, hold, "epst", [128, 1])
    identb = sb(nc, hold, "identb", [128, 128], BF16)

    pp = [psum(nc, hold, f"pp{i}", [128, 512]) for i in range(6)]
    pt = [psum(nc, hold, f"pt{i}", [128, 1024], BF16) for i in range(2)]

    c = Ctx(nc)
    ev = {}
    free = {}
    PPELEM = NT * 2 * 64
    QKPP = NT * D

    def _pre(g):
        ev["ident"] = c.dma(g, "q_id", identb[:, :], id_d[:, :])
        pass
        ev["rope"] = c.dma(
            g, "q_r",
            ropes[:, :].rearrange("p (tt s i) -> p tt s i", s=2, i=64),
            rope_d[:, :, :, :])
    c.on("gpsimd", _pre)

    def _pre_v(v):
        c.step("vector", v.memset(epst[:, :], LN_EPS))
        c.wown(v, "vector")
        c.step("vector", v.memset(onesr[:, :], 1.0))
        c.wown(v, "vector")
        ev["ones"] = c.step("vector", v.memset(onesc[:, :], 1.0))
    c.on("vector", _pre_v)

    for idx in range(NT):
        def _ld(e, idx=idx):
            if idx > 0:
                c.w(e, ev[("z", idx - 1)])
            ev[("xld", idx)] = c.dma(
                e, "q_x", xs[:, :], x_d[idx * 128:(idx + 1) * 128, :])
        c.on("sync", _ld)

        def _stats(v, idx=idx):
            c.wown(v, "vector")
            c.w(v, ev[("xld", idx)])
            for k4 in range(4):
                c.step("vector", v.bn_stats(bn6[:, k4 * 6:(k4 + 1) * 6],
                                            xs[:, k4 * 512:(k4 + 1) * 512]))
            c.wown(v, "vector")
            c.w(v, ev.get(("sd", idx - 1)))
            ev[("mv", idx)] = c.step("vector", v.bn_aggr(mv[:, :], bn6[:, :]))
        c.on("vector", _stats)

        def _sqrt(s, idx=idx):
            c.wown(s, "scalar")
            c.w(s, ev[("mv", idx)])
            ev[("sd", idx)] = c.step(
                "scalar", s.activation(sd[:, :], mv[:, 1:2], AF.Sqrt,
                                       bias=epst[:, :]))
        c.on("scalar", _sqrt)

        def _apply(v, idx=idx):
            c.wown(v, "vector")
            c.w(v, ev[("sd", idx)])
            c.step("vector", v.reciprocal(rstd[:, :], sd[:, :]))
            c.wown(v, "vector")
            if ("ztr_done", idx - 1) in ev:
                c.w(v, ("pe", ev[("ztr_done", idx - 1)]))
            ev[("z", idx)] = c.step(
                "vector", v.tensor_scalar(ztmp[:, :], xs[:, :], mv[:, 0:1],
                                          rstd[:, :], ALU.subtract, ALU.mult))
        c.on("vector", _apply)

        for ch in range(16):
            def _trc(t, idx=idx, ch=ch):
                if ch == 0:
                    c.w(t, ev[("z", idx)])
                    c.w(t, ev["ident"])
                if ("pt_free", ch % 2) in ev:
                    c.w(t, ("dv", ev[("pt_free", ch % 2)]))
                ev[("ztrp", idx, ch)] = c.step(
                    "tensor", t.transpose(pt[ch % 2][:, 0:128],
                                          ztmp[:, ch * 128:(ch + 1) * 128],
                                          identb[:, :]))
            c.on("tensor", _trc)

            def _evc(v, idx=idx, ch=ch):
                c.wown(v, "vector")
                c.w(v, ev[("ztrp", idx, ch)])
                c.step("vector", v.tensor_copy(
                    zfm[:, ch * TPC + idx * 128: ch * TPC + (idx + 1) * 128],
                    pt[ch % 2][:, 0:128]))
                ev[("pt_free", ch % 2)] = c.cnt["dv"]
            c.on("vector", _evc)
        ev[("ztr_done", idx)] = c.cnt["pe"]
    ev["zfm_ready"] = ("dv", c.cnt["dv"])
    ev["consts"] = ("dv", 3)

    def emit_ds():
        for h in range(16):
            bank = pp[h % 4]
            off = (h // 4) * 128
            for gidx, base in enumerate((0, 64)):
                def _mmg(t, h=h, bank=bank, off=off, gidx=gidx, base=base):
                    if h == 0 and gidx == 0:
                        c.w(t, free[("ppb", 0)])
                        c.w(t, free[("ppb", 1)])
                        c.w(t, ev[("elu", "k", NT - 1)])
                        c.w(t, ev["ones"])
                    if gidx == 0 and ("dsevB", h - 4) in ev:
                        c.w(t, ev[("dsevB", h - 4)])
                    if gidx == 1:
                        c.w(t, ev[("dsevA", h)])
                    last = None
                    for cc8 in range(NCH // 2):
                        lhs = qk[base:base + 64,
                                 cc8 * D + h * 128: cc8 * D + (h + 1) * 128]
                        rhs = vtm[base:base + 64,
                                  cc8 * D + h * 128: cc8 * D + (h + 1) * 128]
                        last = t.matmul(bank[:, off:off + 128], lhs, rhs,
                                        start=(cc8 == 0),
                                        stop=(cc8 == NCH // 2 - 1))
                    ev[("dsmmS", h, gidx)] = c.step("tensor", last)
                    zc = 4 * h + 2 * gidx
                    for cc8 in range(NCH // 2):
                        lhs = qk[base:base + 64,
                                 cc8 * D + h * 128: cc8 * D + (h + 1) * 128]
                        last = t.matmul(pp[4][:, zc:zc + 1], lhs,
                                        onesc[base:base + 64, :],
                                        start=(cc8 == 0),
                                        stop=(cc8 == NCH // 2 - 1))
                    ev[("dsmm", h, gidx)] = c.step("tensor", last)
                c.on("tensor", _mmg)

                if gidx == 0:
                    def _evA(v, h=h, bank=bank, off=off):
                        c.wown(v, "vector")
                        c.w(v, ev[("dsmm", h, 0)])
                        ev[("dsevA", h)] = c.step("vector", v.tensor_copy(
                            dsout[:, h * 128:(h + 1) * 128],
                            bank[:, off:off + 128]))
                    c.on("vector", _evA)
                else:
                    def _evB(v, h=h, bank=bank, off=off):
                        c.wown(v, "vector")
                        c.w(v, ev[("dsmm", h, 1)])
                        ev[("dsevB", h)] = c.step("vector", v.tensor_tensor(
                            dsout[:, h * 128:(h + 1) * 128],
                            dsout[:, h * 128:(h + 1) * 128],
                            bank[:, off:off + 128], ALU.add))
                    c.on("vector", _evB)

        def _evz(v):
            c.wown(v, "vector")
            c.w(v, ev[("dsmm", 15, 1)])
            c.step("vector", v.tensor_copy(dzout[:, :], pp[4][:, 0:64:4]))
            c.wown(v, "vector")
            ev["dzev"] = c.step("vector", v.tensor_tensor(
                dzout[:, :], dzout[:, :], pp[4][:, 2:64:4], ALU.add))
        c.on("vector", _evz)
        free[("ppb", 0)] = ("dv", c.cnt["dv"])
        free[("ppb", 1)] = ("dv", c.cnt["dv"])

        def _dsst(e):
            c.w(e, ev["dzev"])
            ev["dsst"] = c.dma(e, "q_ds", ds_d[:, :, :],
                               dsout[:, :].rearrange("p (h e) -> p h e", e=128))
            ev["dzst"] = c.dma(e, "q_dz", dz_d[:, :], dzout[:, :])
            c.w(e, ev["dsst"])
            c.w(e, ev["dzst"])
        c.on("sync", _dsst)

    projs = [("v", wv_d, 2), ("g", wgt_d, 3), ("k", wk_d, 1), ("q", wq_d, 0)]
    PHN = {"v": 1, "g": 2, "k": 3, "q": 5}
    for pname, w_d, bi in projs:
        if A0A_PHASES < PHN[pname]:
            continue
        def _ldb(g, pname=pname, bi=bi):
            if ("pmm_last",) in ev:
                c.w(g, ev[("pmm_last",)])
            ev[("biasld", pname)] = c.dma(g, "q_bias", biasb[0:1, :],
                                          bias_d[bi:bi + 1, :])
        c.on("gpsimd", _ldb)
        for jb in range(4):
            def _ldw(g, pname=pname, w_d=w_d, jb=jb):
                slot = jb % 2
                c.w(g, free.get(("wb", slot)))
                src = w_d[:, jb * 512:(jb + 1) * 512].rearrange(
                    "(c p) j -> p c j", p=128)
                dst = wb[:, slot * 8192:(slot + 1) * 8192].rearrange(
                    "pp (c j) -> pp c j", j=512)
                ev[("w", pname, jb)] = c.dma(g, f"q_w{slot}", dst, src)
            c.on("gpsimd", _ldw)

            for tt in range(NT):
                bank = pp[tt % 2]

                def _mm(t, pname=pname, jb=jb, tt=tt, bank=bank, bi=bi):
                    if jb == 0 and tt == 0:
                        if pname == "v":
                            c.w(t, ev["zfm_ready"])
                            c.w(t, ev["consts"])
                        c.w(t, ev[("biasld", pname)])
                    c.w(t, ev[("w", pname, jb)])
                    c.w(t, free.get(("ppb", tt % 2)))
                    slot = jb % 2
                    for ch in range(16):
                        t.matmul(
                            bank[:, :],
                            zfm[:, ch * TPC + tt * 128: ch * TPC + (tt + 1) * 128],
                            wb[:, slot * 8192 + ch * 512:
                               slot * 8192 + (ch + 1) * 512],
                            start=(ch == 0), stop=False)
                    last = t.matmul(
                        bank[:, :], onesr[:, 0:128],
                        biasb[0:1, jb * 512:(jb + 1) * 512],
                        start=False, stop=True)
                    ev[("pmm", pname, jb, tt)] = c.step("tensor", last)
                    ev[("pmm_last",)] = ev[("pmm", pname, jb, tt)]
                    if tt == NT - 1:
                        free[("wb", jb % 2)] = ev[("pmm", pname, jb, tt)]
                c.on("tensor", _mm)

                if pname == "v":
                    def _ev(v, jb=jb, tt=tt, bank=bank):
                        c.wown(v, "vector")
                        c.w(v, ev[("pmm", "v", jb, tt)])
                        e = c.step("vector", v.tensor_copy(
                            vtm[:, tt * D + jb * 512: tt * D + (jb + 1) * 512],
                            bank[:, :]))
                        free[("ppb", tt % 2)] = e
                    c.on("vector", _ev)
                elif pname == "g":
                    def _ev(s, jb=jb, tt=tt, bank=bank):
                        c.wown(s, "scalar")
                        c.w(s, ev[("pmm", "g", jb, tt)])
                        e = c.step("scalar", s.activation(
                            gsl[:, tt * D + jb * 512: tt * D + (jb + 1) * 512],
                            bank[:, :], AF.Sigmoid))
                        ev[("pev", "g", jb, tt)] = e
                        free[("ppb", tt % 2)] = e
                    c.on("scalar", _ev)
                    if jb == 3:
                        def _stg(e, tt=tt):
                            c.w(e, ev[("pev", "g", 3, tt)])
                            ev[("gst", tt)] = c.dma(
                                e, f"q_g{tt % 2}",
                                gt_d[tt * 128:(tt + 1) * 128, :],
                                gsl[:, tt * D:(tt + 1) * D])
                        c.on("sync", _stg)
                else:
                    def _ev(v, pname=pname, jb=jb, tt=tt, bank=bank):
                        c.wown(v, "vector")
                        c.w(v, ev[("pmm", pname, jb, tt)])
                        if pname == "q" and jb == 0:
                            c.w(v, ev[("kst", tt)])
                        e = c.step("vector", v.tensor_copy(
                            qk[:, tt * D + jb * 512: tt * D + (jb + 1) * 512],
                            bank[:, :]))
                        ev[("pev", pname, jb, tt)] = e
                        free[("ppb", tt % 2)] = e
                    c.on("vector", _ev)

        if pname == "v":
            for tt in range(NT):
                def _stv(e, tt=tt):
                    c.w(e, free[("ppb", tt % 2)])
                    c.dma(e, f"q_v{tt % 2}", vt_d[tt * 128:(tt + 1) * 128, :],
                          vtm[:, tt * D:(tt + 1) * D])
                c.on("sync", _stv)
        if pname in ("q", "k"):
            for tt in range(NT):
                def _rope(v, pname=pname, tt=tt):
                    c.wown(v, "vector")
                    c.w(v, ev[("pev", pname, 3, tt)])
                    c.w(v, ev["rope"])
                    prev = (("eexp", pname, tt - 1) if tt > 0 else
                            (("eexp", "q", NT - 1) if pname == "k" else None))
                    if prev and prev in ev:
                        c.w(v, ev[prev])
                    base = tt * D
                    x1 = bass.AP(qk.ap().tensor, base,
                                 [[QKPP, 128], [128, 16], [2, 64]])
                    x2 = bass.AP(qk.ap().tensor, base + 1,
                                 [[QKPP, 128], [128, 16], [2, 64]])
                    cosb = bass.AP(ropes.ap().tensor, tt * 128,
                                   [[PPELEM, 128], [0, 16], [1, 64]])
                    sinb = bass.AP(ropes.ap().tensor, tt * 128 + 64,
                                   [[PPELEM, 128], [0, 16], [1, 64]])
                    t1v = tr1[:, :].rearrange("p (h i) -> p h i", i=64)
                    t2v = tr2[:, :].rearrange("p (h i) -> p h i", i=64)

                    def st(mk):
                        c.wown(v, "vector")
                        return c.step("vector", mk())
                    c.step("vector", v.tensor_tensor(t1v, x1, sinb, ALU.mult))
                    st(lambda: v.tensor_tensor(t2v, x2, sinb, ALU.mult))
                    st(lambda: v.tensor_tensor(x1, x1, cosb, ALU.mult))
                    st(lambda: v.tensor_tensor(x2, x2, cosb, ALU.mult))
                    st(lambda: v.tensor_tensor(x1, x1, t2v, ALU.subtract))
                    st(lambda: v.tensor_tensor(x2, x2, t1v, ALU.add))
                    row = qk[:, base:base + D]
                    ev[("ropem", pname, tt)] = st(
                        lambda: v.tensor_scalar(te1[:, :], row, 0.0, None,
                                                ALU.min))
                c.on("vector", _rope)

                def _eexp(s, pname=pname, tt=tt):
                    c.wown(s, "scalar")
                    c.w(s, ev[("ropem", pname, tt)])
                    ev[("eexp", pname, tt)] = c.step(
                        "scalar", s.activation(te2[:, :], te1[:, :], AF.Exp))
                c.on("scalar", _eexp)

                def _efin(v, pname=pname, tt=tt):
                    c.wown(v, "vector")
                    c.w(v, ev[("eexp", pname, tt)])
                    row = qk[:, tt * D:(tt + 1) * D]
                    c.step("vector", v.tensor_scalar(row, row, 0.0, None,
                                                     ALU.max))
                    c.wown(v, "vector")
                    ev[("elu", pname, tt)] = c.step(
                        "vector", v.tensor_tensor(row, row, te2[:, :], ALU.add))
                c.on("vector", _efin)

                if pname == "q":
                    for ch in range(16):
                        def _trq(t, tt=tt, ch=ch):
                            if ch == 0:
                                c.w(t, ev[("elu", "q", tt)])
                            if ("pt_free", ch % 2) in ev:
                                c.w(t, ("dv", ev[("pt_free", ch % 2)]))
                            ev[("qtr", tt, ch)] = c.step(
                                "tensor", t.transpose(
                                    pt[ch % 2][:, 0:128],
                                    qk[:, tt * D + ch * 128:
                                       tt * D + (ch + 1) * 128],
                                    identb[:, :]))
                        c.on("tensor", _trq)

                        def _evq(v, tt=tt, ch=ch):
                            c.wown(v, "vector")
                            c.w(v, ev[("qtr", tt, ch)])
                            c.step("vector", v.tensor_copy(
                                zfm[:, ch * TPC + tt * 128:
                                    ch * TPC + (tt + 1) * 128],
                                pt[ch % 2][:, 0:128]))
                            ev[("pt_free", ch % 2)] = c.cnt["dv"]
                            if ch == 15:
                                ev[("qfst_done", tt)] = ("dv", c.cnt["dv"])
                        c.on("vector", _evq)

                    if tt == NT - 1:
                        def _stq(e):
                            c.w(e, ev[("qfst_done", NT - 1)])
                            ev["qfstored"] = c.dma(
                                e, "q_qf",
                                qf_d.rearrange("(cc p) t -> p cc t", p=128),
                                zfm[:, :].rearrange("p (cc t) -> p cc t",
                                                    t=TPC))
                        c.on("sync", _stq)
                else:
                    def _stk(e, tt=tt):
                        c.w(e, ev[("elu", "k", tt)])
                        ev[("kst", tt)] = c.dma(
                            e, f"q_k{tt % 2}",
                            kt_d[tt * 128:(tt + 1) * 128, :],
                            qk[:, tt * D:(tt + 1) * D])
                    c.on("sync", _stk)
        if pname == "k" and A0A_PHASES >= 4:
            emit_ds()


    def _fin2(e):
        if "qfstored" in ev:
            c.w(e, ev["qfstored"])
        for tt in range(NT):
            if ("kst", tt) in ev:
                c.w(e, ev[("kst", tt)])
            if ("gst", tt) in ev:
                c.w(e, ev[("gst", tt)])
    c.on("sync", _fin2)

    c.close()
    for h in reversed(hold):
        h.__exit__(None, None, None)
    return nc



def build_attn0b():
    nc = bass.Bass("TRN2", target_bir_lowering=False)
    hold = []
    NT = TPC // 128
    NCH = TPC // 64
    x_d = nc.dram_tensor("x", [TPC, D], F32, kind="ExternalInput")
    qf_d = nc.dram_tensor("qf", [D, TPC], BF16, kind="ExternalInput")
    kt_d = nc.dram_tensor("kt", [TPC, D], BF16, kind="ExternalInput")
    vt_d = nc.dram_tensor("vt", [TPC, D], BF16, kind="ExternalInput")
    gt_d = nc.dram_tensor("gt", [TPC, D], BF16, kind="ExternalInput")
    dsS_d = nc.dram_tensor("dsS", [16, 128, 8, 128], F32, kind="ExternalInput")
    dsZ_d = nc.dram_tensor("dsZ", [128, 8, 16], F32, kind="ExternalInput")
    wo_d = nc.dram_tensor("wo", [D, D], F32, kind="ExternalInput")
    bo_d = nc.dram_tensor("bo", [1, D], F32, kind="ExternalInput")
    id_d = nc.dram_tensor("ident", [128, 128], F32, kind="ExternalInput")
    xo_d = nc.dram_tensor("xo", [TPC, D], F32, kind="ExternalOutput")

    qfm = sb(nc, hold, "qfm", [128, 16 * TPC], BF16)
    ktm = sb(nc, hold, "ktm", [128, NT * D], BF16)
    vtm = sb(nc, hold, "vtm", [128, NT * D], BF16)
    gtm = sb(nc, hold, "gtm", [128, NT * D], BF16)
    otm = sb(nc, hold, "otm", [128, NT * D], BF16)
    ssf = sb(nc, hold, "ssf", [128, 16 * 129])
    ssb = sb(nc, hold, "ssb", [128, 16 * 129], BF16)
    dsl = sb(nc, hold, "dsl", [128, 2 * 8 * 128])
    dzl = sb(nc, hold, "dzl", [128, 8 * 16])
    wob = sb(nc, hold, "wob", [128, 2 * 512], BF16)
    bob = sb(nc, hold, "bob", [1, D], BF16)
    onesr = sb(nc, hold, "onesr", [1, 128], BF16)
    onesc = sb(nc, hold, "onesc", [128, 1], BF16)
    xsl = sb(nc, hold, "xsl", [128, 2 * 512])
    dent = sb(nc, hold, "dent", [128, 2])
    identb = sb(nc, hold, "identb", [128, 128], BF16)

    pp = [psum(nc, hold, f"pp{i}", [128, 512]) for i in range(6)]
    pt = [psum(nc, hold, f"pt{i}", [128, 1024], BF16) for i in range(2)]

    c = Ctx(nc)
    ev = {}
    free = {}

    def _pre(g):
        ev["ident"] = c.dma(g, "q_id", identb[:, :], id_d[:, :])
        ev["bo"] = c.dma(g, "q_bo", bob[:, :], bo_d[:, :])
        ev["qfm"] = c.dma(g, "q_qf", qfm[:, :].rearrange(
            "p (cc t) -> p cc t", t=TPC),
            qf_d.rearrange("(cc p) t -> p cc t", p=128))
        ev["ktm"] = c.dma(g, "q_kt", ktm[:, :].rearrange(
            "p (tt j) -> p tt j", j=D),
            kt_d.rearrange("(tt p) j -> p tt j", p=128))
        ev["vtm"] = c.dma(g, "q_vt", vtm[:, :].rearrange(
            "p (tt j) -> p tt j", j=D),
            vt_d.rearrange("(tt p) j -> p tt j", p=128))
        ev["gtm"] = c.dma(g, "q_gt", gtm[:, :].rearrange(
            "p (tt j) -> p tt j", j=D),
            gt_d.rearrange("(tt p) j -> p tt j", p=128))
    c.on("gpsimd", _pre)

    def _pre_v(v):
        c.step("vector", v.memset(onesr[:, :], 1.0))
        c.wown(v, "vector")
        ev["ones"] = c.step("vector", v.memset(onesc[:, :], 1.0))
    c.on("vector", _pre_v)

    for h in range(16):
        def _ldh(g, h=h):
            c.w(g, ev.get(("pfdone", h - 2)))
            ev[("dslh", h)] = c.dma(
                g, f"q_dsl{h % 2}",
                dsl[:, (h % 2) * 1024:(h % 2 + 1) * 1024].rearrange(
                    "p (r e) -> p r e", e=128),
                dsS_d[h])
        c.on("gpsimd", _ldh)

        def _sumh(v, h=h):
            c.wown(v, "vector")
            c.w(v, ev[("dslh", h)])
            base = (h % 2) * 1024
            c.step("vector", v.tensor_tensor(
                ssf[:, h * 129: h * 129 + 128], dsl[:, base:base + 128],
                dsl[:, base + 128:base + 256], ALU.add))
            for r in range(2, 8):
                c.wown(v, "vector")
                e = c.step("vector", v.tensor_tensor(
                    ssf[:, h * 129: h * 129 + 128],
                    ssf[:, h * 129: h * 129 + 128],
                    dsl[:, base + r * 128: base + (r + 1) * 128], ALU.add))
            ev[("pfdone", h)] = e
        c.on("vector", _sumh)

    def _ldz(g):
        c.w(g, ev.get(("pfdone", 14)))
        c.w(g, ev.get(("pfdone", 15)))
        ev["dzl"] = c.dma(g, "q_dzl",
                          dzl[:, :].rearrange("p (r hh) -> p r hh", hh=16),
                          dsZ_d[:, :, :])
    c.on("gpsimd", _ldz)

    def _sumz(v):
        c.wown(v, "vector")
        c.w(v, ev["dzl"])
        stg = dsl[:, 0:16]
        c.step("vector", v.tensor_tensor(stg, dzl[:, 0:16], dzl[:, 16:32],
                                         ALU.add))
        for r in range(2, 8):
            c.wown(v, "vector")
            c.step("vector", v.tensor_tensor(
                stg, stg, dzl[:, r * 16:(r + 1) * 16], ALU.add))
        c.wown(v, "vector")
        zb = bass.AP(ssf.ap().tensor, 128, [[16 * 129, 128], [129, 16]])
        ev["zpf"] = c.step("vector", v.tensor_copy(zb, stg))
    c.on("vector", _sumz)

    for cch in range(NCH):
        trow = (cch % 2) * 64
        ttj = cch // 2

        def _smm(t, cch=cch, trow=trow, ttj=ttj):
            if cch == 0:
                c.w(t, ev["ktm"])
                c.w(t, ev["vtm"])
                c.w(t, ev["ones"])
            else:
                c.w(t, ev[("sadd", cch - 1)])
            for h in range(16):
                sbank = pp[h % 4]
                soff = (h // 4) * 128
                lhs = ktm[trow:trow + 64,
                          ttj * D + h * 128: ttj * D + (h + 1) * 128]
                rhs = vtm[trow:trow + 64,
                          ttj * D + h * 128: ttj * D + (h + 1) * 128]
                t.matmul(sbank[:, soff:soff + 128], lhs, rhs,
                         start=True, stop=True)
                last = t.matmul(pp[4][:, 2 * h:2 * h + 1], lhs,
                                onesc[trow:trow + 64, :],
                                start=True, stop=True)
            ev[("smm", cch)] = c.step("tensor", last)
        c.on("tensor", _smm)

        def _sadd(v, cch=cch):
            c.wown(v, "vector")
            c.w(v, ev[("smm", cch)])
            if cch == 0:
                for h in range(16):
                    c.w(v, ev[("pfdone", h)])
                c.w(v, ev["zpf"])
            for h in range(16):
                sbank = pp[h % 4]
                soff = (h // 4) * 128
                c.step("vector", v.tensor_tensor(
                    ssf[:, h * 129: h * 129 + 128],
                    ssf[:, h * 129: h * 129 + 128],
                    sbank[:, soff:soff + 128], ALU.add))
                c.wown(v, "vector")
            zb = bass.AP(ssf.ap().tensor, 128, [[16 * 129, 128], [129, 16]])
            e = c.step("vector", v.tensor_tensor(zb, zb, pp[4][:, 0:32:2],
                                                 ALU.add))
            ev[("sadd", cch)] = e
        c.on("vector", _sadd)

        def _scast(s, cch=cch):
            c.wown(s, "scalar")
            c.w(s, ev[("sadd", cch)])
            if cch > 0:
                c.w(s, ev[("nmm", cch - 1, 15)])
            ev[("scast", cch)] = c.step(
                "scalar", s.copy(ssb[:, :], ssf[:, :]))
        c.on("scalar", _scast)

        for h in range(16):
            nb = pp[5]

            def _nmm(t, cch=cch, h=h, nb=nb, trow=trow):
                if cch == 0 and h == 0:
                    c.w(t, ev["qfm"])
                c.w(t, ev[("scast", cch)])
                c.w(t, free.get(("nb", 0)))
                lhs = qfm[:, h * TPC + cch * 64: h * TPC + (cch + 1) * 64]
                last = t.matmul(nb[trow:trow + 64, 0:129], lhs,
                                ssb[:, h * 129:(h + 1) * 129],
                                start=True, stop=True)
                ev[("nmm", cch, h)] = c.step("tensor", last)
            c.on("tensor", _nmm)

            def _nev(v, cch=cch, h=h, nb=nb, trow=trow, ttj=ttj):
                c.wown(v, "vector")
                c.w(v, ev[("nmm", cch, h)])
                if cch == 0 and h == 0:
                    c.w(v, ev["gtm"])
                c.step("vector", v.tensor_scalar(
                    dent[trow:trow + 64, 0:1], nb[trow:trow + 64, 128:129],
                    1e-6, None, ALU.add))
                c.wown(v, "vector")
                c.step("vector", v.reciprocal(dent[trow:trow + 64, 1:2],
                                              dent[trow:trow + 64, 0:1]))
                c.wown(v, "vector")
                osl = otm[trow:trow + 64,
                          ttj * D + h * 128: ttj * D + (h + 1) * 128]
                gsl = gtm[trow:trow + 64,
                          ttj * D + h * 128: ttj * D + (h + 1) * 128]
                e = c.step("vector", v.scalar_tensor_tensor(
                    osl, nb[trow:trow + 64, 0:128], dent[trow:trow + 64, 1:2],
                    gsl, ALU.mult, ALU.mult))
                ev[("oev", cch, h)] = e
                free[("nb", 0)] = e
            c.on("vector", _nev)

    for tt in range(NT):
        for ch in range(16):
            def _tro(t, tt=tt, ch=ch):
                if tt == 0 and ch == 0:
                    c.w(t, ev[("oev", NCH - 1, 15)])
                    c.w(t, ev["ident"])
                if ("pt_free", ch % 2) in ev:
                    c.w(t, ("dv", ev[("pt_free", ch % 2)]))
                ev[("otr", tt, ch)] = c.step(
                    "tensor", t.transpose(
                        pt[ch % 2][:, 0:128],
                        otm[:, tt * D + ch * 128: tt * D + (ch + 1) * 128],
                        identb[:, :]))
            c.on("tensor", _tro)

            def _evo(v, tt=tt, ch=ch):
                c.wown(v, "vector")
                c.w(v, ev[("otr", tt, ch)])
                c.step("vector", v.tensor_copy(
                    qfm[:, ch * TPC + tt * 128: ch * TPC + (tt + 1) * 128],
                    pt[ch % 2][:, 0:128]))
                ev[("pt_free", ch % 2)] = c.cnt["dv"]
            c.on("vector", _evo)
    ev["ofm_done"] = ("dv", c.cnt["dv"])

    NGRP = (NT + 3) // 4
    for half in range(NGRP):
        tts = list(range(half * 4, min(NT, half * 4 + 4)))
        for db in range(4):
            for ch in range(16):
                def _ldwo(g, half=half, db=db, ch=ch):
                    k = (half * 4 + db) * 16 + ch
                    slot = k % 2
                    c.w(g, free.get(("wob", slot)))
                    src = wo_d[ch * 128:(ch + 1) * 128,
                               db * 512:(db + 1) * 512]
                    ev[("wo", half, db, ch)] = c.dma(
                        g, f"q_wo{slot}", wob[:, slot * 512:(slot + 1) * 512],
                        src)
                c.on("gpsimd", _ldwo)

                def _wmm(t, half=half, db=db, ch=ch):
                    k = (half * 4 + db) * 16 + ch
                    slot = k % 2
                    c.w(t, ev[("wo", half, db, ch)])
                    if ch == 0:
                        c.w(t, ev["ofm_done"])
                        c.w(t, ev["bo"])
                        for i4 in range(4):
                            c.w(t, free.get(("wop", i4)))
                    last = None
                    for i4, tt in enumerate(tts):
                        last = t.matmul(
                            pp[i4][:, :],
                            qfm[:, ch * TPC + tt * 128: ch * TPC + (tt + 1) * 128],
                            wob[:, slot * 512:(slot + 1) * 512],
                            start=(ch == 0), stop=False)
                        if ch == 15:
                            last = t.matmul(
                                pp[i4][:, :], onesr[:, 0:128],
                                bob[0:1, db * 512:(db + 1) * 512],
                                start=False, stop=True)
                    ev[("wmm", half, db, ch)] = c.step("tensor", last)
                    free[("wob", slot)] = ev[("wmm", half, db, ch)]
                c.on("tensor", _wmm)

            for i4, tt in enumerate(tts):
                def _ldx(e, half=half, db=db, i4=i4, tt=tt):
                    c.w(e, free.get(("xsl", i4 % 2)))
                    ev[("xld", half, db, i4)] = c.dma(
                        e, f"q_x{i4 % 2}", xsl[:, (i4 % 2) * 512:(i4 % 2 + 1) * 512],
                        x_d[tt * 128:(tt + 1) * 128, db * 512:(db + 1) * 512])
                c.on("sync", _ldx)

                def _res(v, half=half, db=db, i4=i4, tt=tt):
                    c.wown(v, "vector")
                    c.w(v, ev[("wmm", half, db, 15)])
                    c.w(v, ev[("xld", half, db, i4)])
                    e = c.step("vector", v.tensor_tensor(
                        xsl[:, (i4 % 2) * 512:(i4 % 2 + 1) * 512],
                        xsl[:, (i4 % 2) * 512:(i4 % 2 + 1) * 512],
                        pp[i4][:, :], ALU.add))
                    ev[("res", half, db, i4)] = e
                    free[("wop", i4)] = e
                c.on("vector", _res)

                def _sto(e, half=half, db=db, i4=i4, tt=tt):
                    c.w(e, ev[("res", half, db, i4)])
                    ee = c.dma(e, f"q_xo{i4 % 2}",
                               xo_d[tt * 128:(tt + 1) * 128,
                                    db * 512:(db + 1) * 512],
                               xsl[:, (i4 % 2) * 512:(i4 % 2 + 1) * 512])
                    free[("xsl", i4 % 2)] = ee
                    ev[("sto", half, db, i4)] = ee
                c.on("sync", _sto)

    def _fin(e):
        for i4 in range(len(list(range((NGRP - 1) * 4, min(NT, NGRP * 4))))):
            c.w(e, ev[("sto", NGRP - 1, 3, i4)])
    c.on("sync", _fin)

    c.close()
    for h in reversed(hold):
        h.__exit__(None, None, None)
    return nc



def build_ffn1a():
    nc = bass.Bass("TRN2", target_bir_lowering=False)
    hold = []
    xr_d = nc.dram_tensor("xr", [256, D], F32, kind="ExternalInput")
    swt_d = nc.dram_tensor("swt", [8, D, 256], F32, kind="ExternalInput")
    id_d = nc.dram_tensor("ident", [128, 128], F32, kind="ExternalInput")
    pl_d = nc.dram_tensor("pl", [1, 8], F32, kind="ExternalOutput")

    xs = sb(nc, hold, "xs", [128, D])
    ztmp = sb(nc, hold, "ztmp", [128, D], BF16)
    zr = sb(nc, hold, "zr", [128, 16 * 256], BF16)
    swb = sb(nc, hold, "swb", [128, 2 * 16 * 256], BF16)
    prt = sb(nc, hold, "prt", [128, 8])
    dum = sb(nc, hold, "dum", [128, 256])
    acc1 = sb(nc, hold, "acc1", [128, 2])
    onesc = sb(nc, hold, "onesc", [128, 1])
    row = sb(nc, hold, "row", [1, 128])
    lg = sb(nc, hold, "lg", [1, 8])
    bn6 = sb(nc, hold, "bn6", [128, 24])
    mv = sb(nc, hold, "mv", [128, 2])
    sd = sb(nc, hold, "sd", [128, 1])
    rstd = sb(nc, hold, "rstd", [128, 1])
    epst = sb(nc, hold, "epst", [128, 1])
    identb = sb(nc, hold, "identb", [128, 128], BF16)

    pp = [psum(nc, hold, f"pp{i}", [128, 512]) for i in range(2)]
    pt = [psum(nc, hold, f"pt{i}", [128, 1024], BF16) for i in range(2)]

    c = Ctx(nc)
    ev = {}

    def _pre(g):
        ev["ident"] = c.dma(g, "q_id", identb[:, :], id_d[:, :])
    c.on("gpsimd", _pre)

    def _pre_v(v):
        c.step("vector", v.memset(epst[:, :], LN_EPS))
        c.wown(v, "vector")
        ev["ones"] = c.step("vector", v.memset(onesc[:, :], 1.0))
    c.on("vector", _pre_v)

    for idx in range(2):
        def _ld(e, idx=idx):
            if idx > 0:
                c.w(e, ev[("z", idx - 1)])
            ev[("xld", idx)] = c.dma(e, "q_x", xs[:, :],
                                     xr_d[idx * 128:(idx + 1) * 128, :])
        c.on("sync", _ld)

        def _stats(v, idx=idx):
            c.wown(v, "vector")
            c.w(v, ev[("xld", idx)])
            for k4 in range(4):
                c.step("vector", v.bn_stats(bn6[:, k4 * 6:(k4 + 1) * 6],
                                            xs[:, k4 * 512:(k4 + 1) * 512]))
            c.wown(v, "vector")
            c.w(v, ev.get(("sd", idx - 1)))
            ev[("mv", idx)] = c.step("vector", v.bn_aggr(mv[:, :], bn6[:, :]))
        c.on("vector", _stats)

        def _sqrt(s, idx=idx):
            c.wown(s, "scalar")
            c.w(s, ev[("mv", idx)])
            ev[("sd", idx)] = c.step(
                "scalar", s.activation(sd[:, :], mv[:, 1:2], AF.Sqrt,
                                       bias=epst[:, :]))
        c.on("scalar", _sqrt)

        def _apply(v, idx=idx):
            c.wown(v, "vector")
            c.w(v, ev[("sd", idx)])
            c.step("vector", v.reciprocal(rstd[:, :], sd[:, :]))
            c.wown(v, "vector")
            if ("ztr_done", idx - 1) in ev:
                c.w(v, ("pe", ev[("ztr_done", idx - 1)]))
            ev[("z", idx)] = c.step(
                "vector", v.tensor_scalar(ztmp[:, :], xs[:, :], mv[:, 0:1],
                                          rstd[:, :], ALU.subtract, ALU.mult))
        c.on("vector", _apply)

        for ch in range(16):
            def _trc(t, idx=idx, ch=ch):
                if ch == 0:
                    c.w(t, ev[("z", idx)])
                    c.w(t, ev["ident"])
                if ("pt_free", ch % 2) in ev:
                    c.w(t, ("dv", ev[("pt_free", ch % 2)]))
                ev[("ztrp", idx, ch)] = c.step(
                    "tensor", t.transpose(pt[ch % 2][:, 0:128],
                                          ztmp[:, ch * 128:(ch + 1) * 128],
                                          identb[:, :]))
            c.on("tensor", _trc)

            def _evc(v, idx=idx, ch=ch):
                c.wown(v, "vector")
                c.w(v, ev[("ztrp", idx, ch)])
                c.step("vector", v.tensor_copy(
                    zr[:, ch * 256 + idx * 128: ch * 256 + (idx + 1) * 128],
                    pt[ch % 2][:, 0:128]))
                ev[("pt_free", ch % 2)] = c.cnt["dv"]
            c.on("vector", _evc)
        ev[("ztr_done", idx)] = c.cnt["pe"]

    for e8 in range(8):
        def _ldsw(g, e8=e8):
            if ("swused", e8 - 2) in ev:
                c.w(g, ev[("swused", e8 - 2)])
            src = swt_d[e8].rearrange("(cc p) t -> p cc t", p=128)
            dst = swb[:, (e8 % 2) * 4096:(e8 % 2 + 1) * 4096].rearrange(
                "pp (cc t) -> pp cc t", t=256)
            ev[("sw", e8)] = c.dma(g, f"q_sw{e8 % 2}", dst, src)
        c.on("gpsimd", _ldsw)

        def _ttr(v, e8=e8):
            c.wown(v, "vector")
            c.w(v, ev[("sw", e8)])
            c.w(v, ("dv", ev[("pt_free", 0)]))
            c.w(v, ("dv", ev[("pt_free", 1)]))
            for cc in range(16):
                c.wown(v, "vector")
                c.step("vector", v.tensor_tensor(
                    dum[:, :], zr[:, cc * 256:(cc + 1) * 256],
                    swb[:, (e8 % 2) * 4096 + cc * 256:
                        (e8 % 2) * 4096 + (cc + 1) * 256], ALU.mult))
                c.wown(v, "vector")
                if cc == 0:
                    c.step("vector", v.tensor_reduce(
                        acc1[:, 0:1], dum[:, :], AX.X, ALU.add))
                else:
                    c.step("vector", v.tensor_reduce(
                        acc1[:, 1:2], dum[:, :], AX.X, ALU.add))
                    c.wown(v, "vector")
                    c.step("vector", v.tensor_tensor(
                        acc1[:, 0:1], acc1[:, 0:1], acc1[:, 1:2], ALU.add))
            c.wown(v, "vector")
            e = c.step("vector", v.tensor_copy(prt[:, e8:e8 + 1], acc1[:, 0:1]))
            ev[("swused", e8)] = e
        c.on("vector", _ttr)

    def _sum(t):
        c.w(t, ev[("swused", 7)])
        c.w(t, ev["ones"])
        ev["summ"] = c.step("tensor", t.matmul(pp[0][0:1, 0:8], onesc[:, :],
                                               prt[:, :], start=True, stop=True))
    c.on("tensor", _sum)

    def _fin(v):
        c.wown(v, "vector")
        c.w(v, ev["summ"])
        ev["lg"] = c.step("vector", v.tensor_copy(lg[0:1, :], pp[0][0:1, 0:8]))
    c.on("vector", _fin)

    def _st(e):
        c.w(e, ev["lg"])
        ee = c.dma(e, "q_pl", pl_d[:, :], lg[0:1, :])
        c.w(e, ee)
    c.on("sync", _st)

    c.close()
    for h in reversed(hold):
        h.__exit__(None, None, None)
    return nc



def build_ffn1b(with_z4=False):
    nc = bass.Bass("TRN2", target_bir_lowering=False)
    hold = []
    NT = TPC // 128
    x_d = nc.dram_tensor("x", [TPC, D], F32, kind="ExternalInput")
    pl8_d = nc.dram_tensor("pl8", [8, 8], F32, kind="ExternalInput")
    swb_d = nc.dram_tensor("swb", [1, 8], F32, kind="ExternalInput")
    w1_d = nc.dram_tensor("w1h", [128, 8, 128], F32, kind="ExternalInput")
    b1_d = nc.dram_tensor("b1h", [128, 8], F32, kind="ExternalInput")
    w2_d = nc.dram_tensor("w2h", [128, 8, 128], F32, kind="ExternalInput")
    b2_d = nc.dram_tensor("b2h", [128, 8], F32, kind="ExternalInput")
    w3_d = nc.dram_tensor("w3t", [D, D], F32, kind="ExternalInput")
    b3_d = nc.dram_tensor("b3", [1, D], F32, kind="ExternalInput")
    id_d = nc.dram_tensor("ident", [128, 128], F32, kind="ExternalInput")
    xo_d = nc.dram_tensor("xo", [TPC, D], F32, kind="ExternalOutput")
    if with_z4:
        z4o_d = nc.dram_tensor("z4o", [D, TPC], BF16, kind="ExternalOutput")

    xs = sb(nc, hold, "xs", [128, D])
    ztmp = sb(nc, hold, "ztmp", [128, D], BF16)
    zfm = sb(nc, hold, "zfm", [128, 16 * TPC], BF16)
    w1b = sb(nc, hold, "w1b", [128, 8 * 128], BF16)
    w2b = sb(nc, hold, "w2b", [128, 8 * 128], BF16)
    b1b = sb(nc, hold, "b1b", [128, 8])
    b2b = sb(nc, hold, "b2b", [128, 8])
    w1s = sb(nc, hold, "w1s", [128, 2 * 128], BF16)
    w2s = sb(nc, hold, "w2s", [128, 2 * 128], BF16)
    b1s = sb(nc, hold, "b1s", [128, 2])
    b2s = sb(nc, hold, "b2s", [128, 3])
    selB = sb(nc, hold, "selB", [128, 32])
    pl8s = sb(nc, hold, "pl8s", [8, 8])
    dum8 = sb(nc, hold, "dum8", [128, 8])
    lgr = sb(nc, hold, "lgr", [1, 48])
    onesc = sb(nc, hold, "onesc", [128, 1], BF16)
    onesr = sb(nc, hold, "onesr", [1, 128], BF16)
    ones8 = sb(nc, hold, "ones8", [8, 1])
    onesrf = sb(nc, hold, "onesrf", [1, 128])
    hbuf = sb(nc, hold, "hbuf", [128, 2 * 512], BF16)
    w3b = sb(nc, hold, "w3b", [128, 2 * 512], BF16)
    b3b = sb(nc, hold, "b3b", [1, D], BF16)
    xsl = sb(nc, hold, "xsl", [128, 2 * 512])
    bn6 = sb(nc, hold, "bn6", [128, 24])
    mv = sb(nc, hold, "mv", [128, 2])
    sd = sb(nc, hold, "sd", [128, 1])
    rstd = sb(nc, hold, "rstd", [128, 1])
    epst = sb(nc, hold, "epst", [128, 1])
    identb = sb(nc, hold, "identb", [128, 128], BF16)

    pw = [psum(nc, hold, f"pw{i}", [128, 512]) for i in range(6)]
    pt = [psum(nc, hold, f"pt{i}", [128, 1024], BF16) for i in range(2)]

    c = Ctx(nc)
    ev = {}
    free = {}

    def _pre(g):
        ev["ident"] = c.dma(g, "q_id", identb[:, :], id_d[:, :])
        ev["pl8"] = c.dma(g, "q_pl", pl8s[:, :], pl8_d[:, :])
        ev["swb"] = c.dma(g, "q_swb", lgr[0:1, 40:48], swb_d[:, :])
        ev["w1"] = c.dma(g, "q_w1", w1b[:, :].rearrange(
            "p (e f) -> p e f", f=128), w1_d[:, :, :])
        ev["w2"] = c.dma(g, "q_w2", w2b[:, :].rearrange(
            "p (e f) -> p e f", f=128), w2_d[:, :, :])
        ev["b1"] = c.dma(g, "q_b1", b1b[:, :], b1_d[:, :])
        ev["b2"] = c.dma(g, "q_b2", b2b[:, :], b2_d[:, :])
        ev["b3"] = c.dma(g, "q_b3", b3b[:, :], b3_d[:, :])
    c.on("gpsimd", _pre)

    def _pre_v(v):
        c.step("vector", v.memset(epst[:, :], LN_EPS))
        c.wown(v, "vector")
        c.step("vector", v.memset(onesc[:, :], 1.0))
        c.wown(v, "vector")
        c.step("vector", v.memset(onesr[:, :], 1.0))
        c.wown(v, "vector")
        c.step("vector", v.memset(onesrf[:, :], 1.0))
        c.wown(v, "vector")
        ev["ones"] = c.step("vector", v.memset(ones8[:, :], 1.0))
    c.on("vector", _pre_v)

    def _lmm(t):
        c.w(t, ev["pl8"])
        c.w(t, ev["ones"])
        ev["lmm"] = c.step("tensor", t.matmul(
            pw[0][0:1, 0:8], ones8[:, :], pl8s[:, :], start=True, stop=True))
    c.on("tensor", _lmm)

    def _route(v):
        LG, M1, NM1, PU, S1, RS, PR, SEL1, MSK, M2, SEL2, G1, G2, C2A, C2B, DU = (
            0, 8, 9, 10, 18, 19, 20, 28, 36, 44, 45, 46, 47, 28, 36, 40)
        c.wown(v, "vector")
        c.w(v, ev["lmm"])
        c.w(v, ev["swb"])
        c.step("vector", v.tensor_tensor(lgr[0:1, 0:8], pw[0][0:1, 0:8],
                                         lgr[0:1, 40:48], ALU.add))
        c.wown(v, "vector")
        c.step("vector", v.tensor_reduce(lgr[0:1, 8:9], lgr[0:1, 0:8], AX.X, ALU.max))
        c.wown(v, "vector")
        c.step("vector", v.tensor_scalar(lgr[0:1, 9:10], lgr[0:1, 8:9], -1.0,
                                         None, ALU.mult))
        c.wown(v, "vector")
        ev["r1"] = c.cnt["dv"]
    c.on("vector", _route)

    def _rexp(s):
        c.wown(s, "scalar")
        c.w(s, ("dv", ev["r1"]))
        ev["rexp"] = c.step("scalar", s.activation(
            lgr[0:1, 10:18], lgr[0:1, 0:8], AF.Exp, bias=lgr[0:1, 9:10]))
    c.on("scalar", _rexp)

    def _route2(v):
        c.wown(v, "vector")
        c.w(v, ev["rexp"])
        c.step("vector", v.tensor_reduce(lgr[0:1, 18:19], lgr[0:1, 10:18],
                                         AX.X, ALU.add))
        c.wown(v, "vector")
        c.step("vector", v.reciprocal(lgr[0:1, 19:20], lgr[0:1, 18:19]))
        c.wown(v, "vector")
        c.step("vector", v.tensor_scalar(lgr[0:1, 20:28], lgr[0:1, 10:18],
                                         lgr[0:1, 19:20], None, ALU.mult))
        c.wown(v, "vector")
        c.step("vector", v.tensor_scalar(lgr[0:1, 28:36], lgr[0:1, 0:8],
                                         lgr[0:1, 8:9], None, ALU.is_ge))
        c.wown(v, "vector")
        c.step("vector", v.scalar_tensor_tensor(
            lgr[0:1, 36:44], lgr[0:1, 28:36], -1e30, lgr[0:1, 0:8],
            ALU.mult, ALU.add))
        c.wown(v, "vector")
        c.step("vector", v.tensor_reduce(lgr[0:1, 44:45], lgr[0:1, 36:44],
                                         AX.X, ALU.max))
        c.wown(v, "vector")
        c.step("vector", v.tensor_scalar(lgr[0:1, 36:44], lgr[0:1, 36:44],
                                         lgr[0:1, 44:45], None, ALU.is_ge))
        c.wown(v, "vector")
        c.step("vector", v.tensor_tensor(lgr[0:1, 10:18], lgr[0:1, 20:28],
                                         lgr[0:1, 28:36], ALU.mult))
        c.wown(v, "vector")
        c.step("vector", v.tensor_reduce(lgr[0:1, 45:46], lgr[0:1, 10:18],
                                         AX.X, ALU.add))
        c.wown(v, "vector")
        c.step("vector", v.tensor_tensor(lgr[0:1, 10:18], lgr[0:1, 20:28],
                                         lgr[0:1, 36:44], ALU.mult))
        c.wown(v, "vector")
        c.step("vector", v.tensor_reduce(lgr[0:1, 46:47], lgr[0:1, 10:18],
                                         AX.X, ALU.add))
        c.wown(v, "vector")
        c.step("vector", v.tensor_scalar(lgr[0:1, 10:18], lgr[0:1, 28:36],
                                         lgr[0:1, 45:46], None, ALU.mult))
        c.wown(v, "vector")
        ev["route"] = c.step("vector", v.tensor_scalar(
            lgr[0:1, 18:26], lgr[0:1, 36:44], lgr[0:1, 46:47], None, ALU.mult))
    c.on("vector", _route2)

    def _bcast(t):
        c.w(t, ev["route"])
        last = t.matmul(pw[1][:, 0:8], onesrf[:, :], lgr[0:1, 28:36],
                        start=True, stop=True)
        last = t.matmul(pw[1][:, 8:16], onesrf[:, :], lgr[0:1, 10:18],
                        start=True, stop=True)
        last = t.matmul(pw[1][:, 16:24], onesrf[:, :], lgr[0:1, 36:44],
                        start=True, stop=True)
        last = t.matmul(pw[1][:, 24:32], onesrf[:, :], lgr[0:1, 18:26],
                        start=True, stop=True)
        ev["bcast"] = c.step("tensor", last)
    c.on("tensor", _bcast)

    def _selc(v):
        c.wown(v, "vector")
        c.w(v, ev["bcast"])
        ev["selB"] = c.step("vector", v.tensor_copy(selB[:, :], pw[1][:, 0:32]))
    c.on("vector", _selc)

    def _comb(v):
        c.wown(v, "vector")
        c.w(v, ev["selB"])
        c.w(v, ev["w1"])
        c.w(v, ev["w2"])
        c.w(v, ev["b1"])
        c.w(v, ev["b2"])
        for k2 in range(2):
            sc = 0 if k2 == 0 else 16
            cc = 8 if k2 == 0 else 24
            c.step("vector", v.tensor_scalar(
                w1s[:, k2 * 128:(k2 + 1) * 128], w1b[:, 0:128],
                selB[:, sc:sc + 1], None, ALU.mult))
            for e8 in range(1, 8):
                c.wown(v, "vector")
                c.step("vector", v.scalar_tensor_tensor(
                    w1s[:, k2 * 128:(k2 + 1) * 128],
                    w1b[:, e8 * 128:(e8 + 1) * 128], selB[:, sc + e8:sc + e8 + 1],
                    w1s[:, k2 * 128:(k2 + 1) * 128], ALU.mult, ALU.add))
            c.wown(v, "vector")
            c.step("vector", v.tensor_scalar(
                w2s[:, k2 * 128:(k2 + 1) * 128], w2b[:, 0:128],
                selB[:, cc:cc + 1], None, ALU.mult))
            for e8 in range(1, 8):
                c.wown(v, "vector")
                c.step("vector", v.scalar_tensor_tensor(
                    w2s[:, k2 * 128:(k2 + 1) * 128],
                    w2b[:, e8 * 128:(e8 + 1) * 128], selB[:, cc + e8:cc + e8 + 1],
                    w2s[:, k2 * 128:(k2 + 1) * 128], ALU.mult, ALU.add))
            c.wown(v, "vector")
            c.step("vector", v.tensor_tensor(dum8[:, :], b1b[:, :],
                                             selB[:, sc:sc + 8], ALU.mult))
            c.wown(v, "vector")
            c.step("vector", v.tensor_reduce(b1s[:, k2:k2 + 1], dum8[:, :],
                                             AX.X, ALU.add))
            c.wown(v, "vector")
            c.step("vector", v.tensor_tensor(dum8[:, :], b2b[:, :],
                                             selB[:, cc:cc + 8], ALU.mult))
            c.wown(v, "vector")
            c.step("vector", v.tensor_reduce(b2s[:, k2:k2 + 1], dum8[:, :],
                                             AX.X, ALU.add))
            c.wown(v, "vector")
        ev["wsel"] = c.step("vector", v.tensor_tensor(
            b2s[:, 2:3], b2s[:, 0:1], b2s[:, 1:2], ALU.add))
    c.on("vector", _comb)

    for idx in range(NT):
        def _ld(e, idx=idx):
            if idx > 0:
                c.w(e, ev[("z", idx - 1)])
            ev[("xld", idx)] = c.dma(e, "q_x", xs[:, :],
                                     x_d[idx * 128:(idx + 1) * 128, :])
        c.on("sync", _ld)

        def _stats(v, idx=idx):
            c.wown(v, "vector")
            c.w(v, ev[("xld", idx)])
            for k4 in range(4):
                c.step("vector", v.bn_stats(bn6[:, k4 * 6:(k4 + 1) * 6],
                                            xs[:, k4 * 512:(k4 + 1) * 512]))
            c.wown(v, "vector")
            c.w(v, ev.get(("sd", idx - 1)))
            ev[("mv", idx)] = c.step("vector", v.bn_aggr(mv[:, :], bn6[:, :]))
        c.on("vector", _stats)

        def _sqrt(s, idx=idx):
            c.wown(s, "scalar")
            c.w(s, ev[("mv", idx)])
            ev[("sd", idx)] = c.step(
                "scalar", s.activation(sd[:, :], mv[:, 1:2], AF.Sqrt,
                                       bias=epst[:, :]))
        c.on("scalar", _sqrt)

        def _apply(v, idx=idx):
            c.wown(v, "vector")
            c.w(v, ev[("sd", idx)])
            c.step("vector", v.reciprocal(rstd[:, :], sd[:, :]))
            c.wown(v, "vector")
            if ("ztr_done", idx - 1) in ev:
                c.w(v, ("pe", ev[("ztr_done", idx - 1)]))
            ev[("z", idx)] = c.step(
                "vector", v.tensor_scalar(ztmp[:, :], xs[:, :], mv[:, 0:1],
                                          rstd[:, :], ALU.subtract, ALU.mult))
        c.on("vector", _apply)

        for ch in range(16):
            def _trc(t, idx=idx, ch=ch):
                if ch == 0:
                    c.w(t, ev[("z", idx)])
                    c.w(t, ev["ident"])
                if ("pt_free", ch % 2) in ev:
                    c.w(t, ("dv", ev[("pt_free", ch % 2)]))
                ev[("ztrp", idx, ch)] = c.step(
                    "tensor", t.transpose(pt[ch % 2][:, 0:128],
                                          ztmp[:, ch * 128:(ch + 1) * 128],
                                          identb[:, :]))
            c.on("tensor", _trc)

            def _evc(v, idx=idx, ch=ch):
                c.wown(v, "vector")
                c.w(v, ev[("ztrp", idx, ch)])
                c.step("vector", v.tensor_copy(
                    zfm[:, ch * TPC + idx * 128: ch * TPC + (idx + 1) * 128],
                    pt[ch % 2][:, 0:128]))
                ev[("pt_free", ch % 2)] = c.cnt["dv"]
            c.on("vector", _evc)
        ev[("ztr_done", idx)] = c.cnt["pe"]
    ev["zfm_ready"] = ("dv", c.cnt["dv"])

    EW = min(512, TPC)
    NTH = TPC // EW
    for hh in range(16):
        for th in range(NTH):
            pa = pw[2 + (th % 2)]
            pb = pw[4 + (th % 2)]
            for k2 in range(2):
                def _hmm(t, hh=hh, th=th, k2=k2, pa=pa):
                    if hh == 0 and th == 0 and k2 == 0:
                        c.w(t, ev["zfm_ready"])
                        c.w(t, ev["wsel"])
                    c.w(t, free.get(("pa", th % 2)))
                    last = t.matmul(
                        pa[:, 0:EW], w1s[:, k2 * 128:(k2 + 1) * 128],
                        zfm[:, hh * TPC + th * EW: hh * TPC + (th + 1) * EW],
                        start=True, stop=True)
                    ev[("h1", hh, th, k2)] = c.step("tensor", last)
                c.on("tensor", _hmm)

                def _gelu(s, hh=hh, th=th, k2=k2, pa=pa):
                    c.wown(s, "scalar")
                    c.w(s, ev[("h1", hh, th, k2)])
                    c.w(s, free.get(("hb", th % 2)))
                    e = c.step("scalar", s.activation(
                        hbuf[:, (th % 2) * 512:(th % 2) * 512 + EW],
                        pa[:, 0:EW], AF.Gelu, bias=b1s[:, k2:k2 + 1]))
                    ev[("hb", hh, th, k2)] = e
                    free[("pa", th % 2)] = e
                c.on("scalar", _gelu)

                def _omm(t, hh=hh, th=th, k2=k2, pb=pb):
                    c.w(t, ev[("hb", hh, th, k2)])
                    if k2 == 0:
                        c.w(t, free.get(("pb", th % 2)))
                    last = t.matmul(
                        pb[:, 0:EW], w2s[:, k2 * 128:(k2 + 1) * 128],
                        hbuf[:, (th % 2) * 512:(th % 2) * 512 + EW],
                        start=(k2 == 0), stop=(k2 == 1))
                    e2 = c.step("tensor", last)
                    ev[("omm", hh, th, k2)] = e2
                    free[("hb", th % 2)] = e2
                c.on("tensor", _omm)

            def _oev(v, hh=hh, th=th, pb=pb):
                c.wown(v, "vector")
                c.w(v, ev[("omm", hh, th, 1)])
                zsl = zfm[:, hh * TPC + th * EW: hh * TPC + (th + 1) * EW]
                e = c.step("vector", v.scalar_tensor_tensor(
                    zsl, pb[:, 0:EW], b2s[:, 2:3], zsl, ALU.add, ALU.add))
                ev[("zo", hh, th)] = e
                free[("pb", th % 2)] = e
            c.on("vector", _oev)
    ev["zo_done"] = ("dv", c.cnt["dv"])

    NGRP = (NT + 3) // 4
    for half in range(NGRP):
        tts = list(range(half * 4, min(NT, half * 4 + 4)))
        for db in range(4):
            for ch in range(16):
                def _ldw3(g, half=half, db=db, ch=ch):
                    k = (half * 4 + db) * 16 + ch
                    slot = k % 2
                    c.w(g, free.get(("w3b", slot)))
                    src = w3_d[ch * 128:(ch + 1) * 128, db * 512:(db + 1) * 512]
                    ev[("w3", half, db, ch)] = c.dma(
                        g, f"q_w3{slot}", w3b[:, slot * 512:(slot + 1) * 512],
                        src)
                c.on("gpsimd", _ldw3)

                def _wmm(t, half=half, db=db, ch=ch):
                    k = (half * 4 + db) * 16 + ch
                    slot = k % 2
                    c.w(t, ev[("w3", half, db, ch)])
                    if ch == 0:
                        c.w(t, ev["zo_done"])
                        c.w(t, ev["b3"])
                        for i4 in range(len(tts)):
                            c.w(t, free.get(("wop", i4)))
                    last = None
                    for i4, tt in enumerate(tts):
                        last = t.matmul(
                            pw[i4][:, :],
                            zfm[:, ch * TPC + tt * 128: ch * TPC + (tt + 1) * 128],
                            w3b[:, slot * 512:(slot + 1) * 512],
                            start=(ch == 0), stop=False)
                        if ch == 15:
                            last = t.matmul(
                                pw[i4][:, :], onesr[:, 0:128],
                                b3b[0:1, db * 512:(db + 1) * 512],
                                start=False, stop=True)
                    ev[("wmm", half, db, ch)] = c.step("tensor", last)
                    free[("w3b", slot)] = ev[("wmm", half, db, ch)]
                c.on("tensor", _wmm)

            for i4, tt in enumerate(tts):
                def _ldx(e, half=half, db=db, i4=i4, tt=tt):
                    c.w(e, free.get(("xsl", i4 % 2)))
                    ev[("xld2", half, db, i4)] = c.dma(
                        e, f"q_x{i4 % 2}",
                        xsl[:, (i4 % 2) * 512:(i4 % 2 + 1) * 512],
                        x_d[tt * 128:(tt + 1) * 128, db * 512:(db + 1) * 512])
                c.on("sync", _ldx)

                def _res(v, half=half, db=db, i4=i4, tt=tt):
                    c.wown(v, "vector")
                    c.w(v, ev[("wmm", half, db, 15)])
                    c.w(v, ev[("xld2", half, db, i4)])
                    e = c.step("vector", v.scalar_tensor_tensor(
                        xsl[:, (i4 % 2) * 512:(i4 % 2 + 1) * 512],
                        pw[i4][:, :], 0.5,
                        xsl[:, (i4 % 2) * 512:(i4 % 2 + 1) * 512],
                        ALU.mult, ALU.add))
                    ev[("res", half, db, i4)] = e
                    free[("wop", i4)] = e
                c.on("vector", _res)

                def _sto(e, half=half, db=db, i4=i4, tt=tt):
                    c.w(e, ev[("res", half, db, i4)])
                    ee = c.dma(e, f"q_xo{i4 % 2}",
                               xo_d[tt * 128:(tt + 1) * 128,
                                    db * 512:(db + 1) * 512],
                               xsl[:, (i4 % 2) * 512:(i4 % 2 + 1) * 512])
                    free[("xsl", i4 % 2)] = ee
                    ev[("sto", half, db, i4)] = ee
                c.on("sync", _sto)

    if with_z4:
        for idx in range(NT):
            def _ldA(e, idx=idx):
                if idx > 0:
                    c.w(e, ev[("Az", idx - 1)])
                for db in range(4):
                    c.w(e, ev[("sto", idx // 4, db, idx % 4)])
                ev[("Axld", idx)] = c.dma(e, "q_xA", xs[:, :],
                                          xo_d[idx * 128:(idx + 1) * 128, :])
            c.on("sync", _ldA)

            def _statsA(v, idx=idx):
                c.wown(v, "vector")
                c.w(v, ev[("Axld", idx)])
                for k4 in range(4):
                    c.step("vector", v.bn_stats(
                        bn6[:, k4 * 6:(k4 + 1) * 6],
                        xs[:, k4 * 512:(k4 + 1) * 512]))
                c.wown(v, "vector")
                c.w(v, ev.get(("Asd", idx - 1)))
                ev[("Amv", idx)] = c.step("vector",
                                          v.bn_aggr(mv[:, :], bn6[:, :]))
            c.on("vector", _statsA)

            def _sqrtA(s, idx=idx):
                c.wown(s, "scalar")
                c.w(s, ev[("Amv", idx)])
                ev[("Asd", idx)] = c.step(
                    "scalar", s.activation(sd[:, :], mv[:, 1:2], AF.Sqrt,
                                           bias=epst[:, :]))
            c.on("scalar", _sqrtA)

            def _applyA(v, idx=idx):
                c.wown(v, "vector")
                c.w(v, ev[("Asd", idx)])
                c.step("vector", v.reciprocal(rstd[:, :], sd[:, :]))
                c.wown(v, "vector")
                if ("AztrD", idx - 1) in ev:
                    c.w(v, ("pe", ev[("AztrD", idx - 1)]))
                ev[("Az", idx)] = c.step(
                    "vector", v.tensor_scalar(ztmp[:, :], xs[:, :],
                                              mv[:, 0:1], rstd[:, :],
                                              ALU.subtract, ALU.mult))
            c.on("vector", _applyA)

            for ch in range(16):
                def _trcA(t, idx=idx, ch=ch):
                    if ch == 0:
                        c.w(t, ev[("Az", idx)])
                        c.w(t, ev["ident"])
                    if ("AptF", ch % 2) in ev:
                        c.w(t, ("dv", ev[("AptF", ch % 2)]))
                    ev[("Aztrp", idx, ch)] = c.step(
                        "tensor", t.transpose(
                            pt[ch % 2][:, 0:128],
                            ztmp[:, ch * 128:(ch + 1) * 128],
                            identb[:, :]))
                c.on("tensor", _trcA)

                def _evcA(v, idx=idx, ch=ch):
                    c.wown(v, "vector")
                    c.w(v, ev[("Aztrp", idx, ch)])
                    c.step("vector", v.tensor_copy(
                        zfm[:, ch * TPC + idx * 128:
                            ch * TPC + (idx + 1) * 128],
                        pt[ch % 2][:, 0:128]))
                    ev[("AptF", ch % 2)] = c.cnt["dv"]
                c.on("vector", _evcA)
            ev[("AztrD", idx)] = c.cnt["pe"]

        def _stA(e):
            c.w(e, ("dv", c.cnt["dv"]))
            ee = c.dma(e, "q_zA",
                       z4o_d.rearrange("(cc p) t -> p cc t", p=128),
                       zfm[:, :].rearrange("p (cc t) -> p cc t", t=TPC))
            c.w(e, ee)
        c.on("sync", _stA)

    def _fin(e):
        nlast = len(list(range((NGRP - 1) * 4, min(NT, NGRP * 4))))
        for i4 in range(nlast):
            c.w(e, ev[("sto", NGRP - 1, 3, i4)])
    c.on("sync", _fin)

    c.close()
    for h in reversed(hold):
        h.__exit__(None, None, None)
    return nc



def build_attn1a():
    nc = bass.Bass("TRN2", target_bir_lowering=False)
    hold = []
    NT = TPC // 128
    x_d = nc.dram_tensor("x", [TPC, D], F32, kind="ExternalInput")
    id_d = nc.dram_tensor("ident", [128, 128], F32, kind="ExternalInput")
    z_d = nc.dram_tensor("z4", [D, TPC], BF16, kind="ExternalOutput")

    xs = sb(nc, hold, "xs", [128, D])
    ztmp = sb(nc, hold, "ztmp", [128, D], BF16)
    zfm = sb(nc, hold, "zfm", [128, 16 * TPC], BF16)
    bn6 = sb(nc, hold, "bn6", [128, 24])
    mv = sb(nc, hold, "mv", [128, 2])
    sd = sb(nc, hold, "sd", [128, 1])
    rstd = sb(nc, hold, "rstd", [128, 1])
    epst = sb(nc, hold, "epst", [128, 1])
    identb = sb(nc, hold, "identb", [128, 128], BF16)
    pt = [psum(nc, hold, f"pt{i}", [128, 1024], BF16) for i in range(2)]

    c = Ctx(nc)
    ev = {}

    def _pre(g):
        ev["ident"] = c.dma(g, "q_id", identb[:, :], id_d[:, :])
    c.on("gpsimd", _pre)

    def _pre_v(v):
        ev["eps"] = c.step("vector", v.memset(epst[:, :], LN_EPS))
    c.on("vector", _pre_v)

    for idx in range(NT):
        def _ld(e, idx=idx):
            if idx > 0:
                c.w(e, ev[("z", idx - 1)])
            ev[("xld", idx)] = c.dma(e, "q_x", xs[:, :],
                                     x_d[idx * 128:(idx + 1) * 128, :])
        c.on("sync", _ld)

        def _stats(v, idx=idx):
            c.wown(v, "vector")
            c.w(v, ev[("xld", idx)])
            for k4 in range(4):
                c.step("vector", v.bn_stats(bn6[:, k4 * 6:(k4 + 1) * 6],
                                            xs[:, k4 * 512:(k4 + 1) * 512]))
            c.wown(v, "vector")
            c.w(v, ev.get(("sd", idx - 1)))
            ev[("mv", idx)] = c.step("vector", v.bn_aggr(mv[:, :], bn6[:, :]))
        c.on("vector", _stats)

        def _sqrt(s, idx=idx):
            c.wown(s, "scalar")
            c.w(s, ev[("mv", idx)])
            ev[("sd", idx)] = c.step(
                "scalar", s.activation(sd[:, :], mv[:, 1:2], AF.Sqrt,
                                       bias=epst[:, :]))
        c.on("scalar", _sqrt)

        def _apply(v, idx=idx):
            c.wown(v, "vector")
            c.w(v, ev[("sd", idx)])
            c.step("vector", v.reciprocal(rstd[:, :], sd[:, :]))
            c.wown(v, "vector")
            if ("ztr_done", idx - 1) in ev:
                c.w(v, ("pe", ev[("ztr_done", idx - 1)]))
            ev[("z", idx)] = c.step(
                "vector", v.tensor_scalar(ztmp[:, :], xs[:, :], mv[:, 0:1],
                                          rstd[:, :], ALU.subtract, ALU.mult))
        c.on("vector", _apply)

        for ch in range(16):
            def _trc(t, idx=idx, ch=ch):
                if ch == 0:
                    c.w(t, ev[("z", idx)])
                    c.w(t, ev["ident"])
                if ("pt_free", ch % 2) in ev:
                    c.w(t, ("dv", ev[("pt_free", ch % 2)]))
                ev[("ztrp", idx, ch)] = c.step(
                    "tensor", t.transpose(pt[ch % 2][:, 0:128],
                                          ztmp[:, ch * 128:(ch + 1) * 128],
                                          identb[:, :]))
            c.on("tensor", _trc)

            def _evc(v, idx=idx, ch=ch):
                c.wown(v, "vector")
                c.w(v, ev[("ztrp", idx, ch)])
                c.step("vector", v.tensor_copy(
                    zfm[:, ch * TPC + idx * 128: ch * TPC + (idx + 1) * 128],
                    pt[ch % 2][:, 0:128]))
                ev[("pt_free", ch % 2)] = c.cnt["dv"]
            c.on("vector", _evc)
        ev[("ztr_done", idx)] = c.cnt["pe"]

    def _st(e):
        c.w(e, ("dv", c.cnt["dv"]))
        ee = c.dma(e, "q_z", z_d.rearrange("(cc p) t -> p cc t", p=128),
                   zfm[:, :].rearrange("p (cc t) -> p cc t", t=TPC))
        c.w(e, ee)
    c.on("sync", _st)

    c.close()
    for h in reversed(hold):
        h.__exit__(None, None, None)
    return nc



def build_attn1c():
    nc = bass.Bass("TRN2", target_bir_lowering=False)
    hold = []
    NT = TPC // 128
    x_d = nc.dram_tensor("x", [TPC, D], F32, kind="ExternalInput")
    o_d = nc.dram_tensor("ofm", [4096, TPC], BF16, kind="ExternalInput")
    wo_d = nc.dram_tensor("wo", [4096, D], F32, kind="ExternalInput")
    xo_d = nc.dram_tensor("xo", [TPC, D], F32, kind="ExternalOutput")

    ofm = sb(nc, hold, "ofms", [128, 32 * TPC], BF16)
    wob = sb(nc, hold, "wob", [128, 2 * 512], BF16)
    xsl = sb(nc, hold, "xsl", [128, 2 * 512])

    pw = [psum(nc, hold, f"pw{i}", [128, 512]) for i in range(4)]

    c = Ctx(nc)
    ev = {}
    free = {}

    def _pre(g):
        ev["ofm"] = c.dma(g, "q_o", ofm[:, :].rearrange(
            "p (cc t) -> p cc t", t=TPC),
            o_d.rearrange("(cc p) t -> p cc t", p=128))
    c.on("gpsimd", _pre)

    NGRP = (NT + 3) // 4
    for half in range(NGRP):
        tts = list(range(half * 4, min(NT, half * 4 + 4)))
        for db in range(4):
            for ch in range(32):
                def _ldwo(g, half=half, db=db, ch=ch):
                    k = (half * 4 + db) * 32 + ch
                    slot = k % 2
                    c.w(g, free.get(("wob", slot)))
                    src = wo_d[ch * 128:(ch + 1) * 128, db * 512:(db + 1) * 512]
                    ev[("wo", half, db, ch)] = c.dma(
                        g, f"q_wo{slot}", wob[:, slot * 512:(slot + 1) * 512],
                        src)
                c.on("gpsimd", _ldwo)

                def _wmm(t, half=half, db=db, ch=ch):
                    k = (half * 4 + db) * 32 + ch
                    slot = k % 2
                    c.w(t, ev[("wo", half, db, ch)])
                    if ch == 0:
                        c.w(t, ev["ofm"])
                        for i4 in range(len(tts)):
                            c.w(t, free.get(("wop", i4)))
                    last = None
                    for i4, tt in enumerate(tts):
                        last = t.matmul(
                            pw[i4][:, :],
                            ofm[:, ch * TPC + tt * 128: ch * TPC + (tt + 1) * 128],
                            wob[:, slot * 512:(slot + 1) * 512],
                            start=(ch == 0), stop=(ch == 31))
                    ev[("wmm", half, db, ch)] = c.step("tensor", last)
                    free[("wob", slot)] = ev[("wmm", half, db, ch)]
                c.on("tensor", _wmm)

            for i4, tt in enumerate(tts):
                def _ldx(e, half=half, db=db, i4=i4, tt=tt):
                    c.w(e, free.get(("xsl", i4 % 2)))
                    ev[("xld", half, db, i4)] = c.dma(
                        e, f"q_x{i4 % 2}",
                        xsl[:, (i4 % 2) * 512:(i4 % 2 + 1) * 512],
                        x_d[tt * 128:(tt + 1) * 128, db * 512:(db + 1) * 512])
                c.on("sync", _ldx)

                def _res(v, half=half, db=db, i4=i4, tt=tt):
                    c.wown(v, "vector")
                    c.w(v, ev[("wmm", half, db, 31)])
                    c.w(v, ev[("xld", half, db, i4)])
                    e = c.step("vector", v.tensor_tensor(
                        xsl[:, (i4 % 2) * 512:(i4 % 2 + 1) * 512],
                        xsl[:, (i4 % 2) * 512:(i4 % 2 + 1) * 512],
                        pw[i4][:, :], ALU.add))
                    ev[("res", half, db, i4)] = e
                    free[("wop", i4)] = e
                c.on("vector", _res)

                def _sto(e, half=half, db=db, i4=i4, tt=tt):
                    c.w(e, ev[("res", half, db, i4)])
                    ee = c.dma(e, f"q_xo{i4 % 2}",
                               xo_d[tt * 128:(tt + 1) * 128,
                                    db * 512:(db + 1) * 512],
                               xsl[:, (i4 % 2) * 512:(i4 % 2 + 1) * 512])
                    free[("xsl", i4 % 2)] = ee
                    ev[("sto", half, db, i4)] = ee
                c.on("sync", _sto)

    def _fin(e):
        nlast = len(list(range((NGRP - 1) * 4, min(NT, NGRP * 4))))
        for i4 in range(nlast):
            c.w(e, ev[("sto", NGRP - 1, 3, i4)])
    c.on("sync", _fin)

    c.close()
    for h in reversed(hold):
        h.__exit__(None, None, None)
    return nc



def emit_chunk(c, ev, free, env, hf, ch, t0, scur):
    ALU_ = ALU
    qb = env["qb"]; kb = env["kb"]; vb = env["vb"]; ob = env["ob"]
    r_b = env["r_b"]; r_L = env["r_L"]; r_LB = env["r_LB"]
    s0f = env["s0f"]; s1f = env["s1f"]; s0b = env["s0b"]
    atp = env["atp"]
    xp_ = env["xp_"]; g_ = env["g_"]
    mt = env["mt"]; dec = env["dec"]; decf = env["decf"]; rhs_s = env["rhs_s"]
    mskPair = env["mskPair"]
    us = env["us"]; usc = env["usc"]; t1f = env["t1f"]; t2f = env["t2f"]
    ktms = env["ktms"]; otile = env["otile"]; colsb = env["colsb"]
    rrowf = env["rrowf"]
    onesrf = env["onesrf"]; onescf = env["onescf"]
    zerot = env["zerot"]; identb = env["identb"]; identf = env["identf"]
    eyemb = env["eyemb"]; mskMb = env["mskMb"]
    pbA = env["pbA"]; pbB = env["pbB"]; pbC = env["pbC"]; pbD = env["pbD"]
    pbE = env["pbE"]; pt = env["pt"]
    HLF = env["HLF"]
    BOF = HLF + 4
    trow = (ch % 2) * 64
    tt2 = ch // 2
    sc = s0f if scur[0] == 0 else s1f
    sn = s1f if scur[0] == 0 else s0f

    def kslice(h):
        return kb[:, h * BOF + 4 + t0: h * BOF + 4 + t0 + 64]

    def qslice(h):
        return qb[:, h * BOF + 4 + t0: h * BOF + 4 + t0 + 64]

    lc_tm = env["lc_tm"]; lbc_tm = env["lbc_tm"]; bc_tm = env["bc_tm"]
    nl_tm = env["nl_tm"]; g_tm = env["g_tm"]; gbn_tm = env["gbn_tm"]
    esc_tm = env["esc_tm"]; gcf_tm = env["gcf_tm"]
    NCHH_ = env["NCHH_"]
    id64b = env["id64b"]
    LBcol = lbc_tm[:, ch:ch + 1]
    nLcol = nl_tm[:, ch:ch + 1]
    Bcol = bc_tm[:, ch:ch + 1]
    Gcol = g_tm[:, ch:ch + 1]
    GBNcol = gbn_tm[:, ch:ch + 1]
    ESCcol = esc_tm[:, ch:ch + 1]

    def _chst(t):
        c.w(t, ev.get(("ochunk", hf, ch - 1)))
        if ch == 0:
            c.w(t, ev.get(("ost", hf - 1)))
            c.w(t, ev[("ctiles", hf)])
            c.w(t, ev[("normall", hf)])
            c.w(t, ev[("vtr_done", hf)])
        ev[("cols", hf, ch)] = c.step("tensor", t.nop())
    c.on("tensor", _chst)

    def _kraw(t):
        c.w(t, ev[("cols", hf, ch)])
        for rep in range(2):
            for h in range(2):
                for h2 in range(2):
                    last = t.matmul(
                        pbA[h * 64:(h + 1) * 64,
                            rep * 128 + h2 * 64: rep * 128 + h2 * 64 + 64],
                        kslice(h), kslice(h2), start=True, stop=True)
        ev[("kraw", hf, ch)] = c.step("tensor", last)
    c.on("tensor", _kraw)
    if A1B_SUB <= 1:
        ev[("ochunk", hf, ch)] = ev[("kraw", hf, ch)]
        return

    def _decmm(t, which, msk, rof, evk, wait_ev):
        mk = {"A": "mskA", "AT": "mskAT", "M": "mskM"}[which]

        def fn(t2):
            c.w(t2, ev[wait_ev])
            c.w(t2, ev[mk])
            for kk in range(4):
                c.w(t2, ev[("rrow", hf, kk)])
            last = None
            for h in range(2):
                rsl = rrowf[0:1, rof + h * HLF + t0: rof + h * HLF + t0 + 64]
                last = t2.matmul(pbC[:, h * 64:(h + 1) * 64],
                                 onesrf[0:1, :], rsl,
                                 start=True, stop=True)
            ev[(evk[0] + "_mm",) + evk[1:]] = c.step("tensor", last)
        return fn, mk

    def _dAAT(t):
        c.w(t, ev[("kraw", hf, ch)])
        c.w(t, ev["mskA"])
        c.w(t, ev["mskAT"])
        for kk in range(4):
            c.w(t, ev[("rrow", hf, kk)])
        last = None
        for rof, cb in ((0, 0), (2 * HLF, 128)):
            for h in range(2):
                rsl = rrowf[0:1, rof + h * HLF + t0:
                            rof + h * HLF + t0 + 64]
                last = t.matmul(pbC[:, cb + h * 64: cb + h * 64 + 64],
                                onesrf[0:1, :], rsl, start=True, stop=True)
        ev[("dAAT_mm", hf, ch)] = c.step("tensor", last)
    c.on("tensor", _dAAT)

    def _dmask(v):
        c.wown(v, "vector")
        c.w(v, ev[("dAAT_mm", hf, ch)])
        ev[("dAAT", hf, ch)] = c.step("vector", v.tensor_tensor(
            decf[:, 0:256], pbC[:, 0:256], mskPair[:, :], ALU_.add))
    c.on("vector", _dmask)

    def _dAact(s):
        c.wown(s, "scalar")
        c.w(s, ev[("dAAT", hf, ch)])
        ev[("dAa", hf, ch)] = c.step("scalar", s.activation(
            dec[:, 0:128], decf[:, 0:128], AF.Exp, bias=LBcol, scale=-1.0))
        c.wown(s, "scalar")
        ev[("dATa", hf, ch)] = c.step("scalar", s.activation(
            dec[:, 128:256], decf[:, 128:256], AF.Exp, bias=nLcol))
    c.on("scalar", _dAact)

    def _Am(v):
        c.wown(v, "vector")
        c.w(v, ev[("dAa", hf, ch)])
        ev[("Am", hf, ch)] = c.step("vector", v.tensor_tensor(
            atp[:, 0:128], dec[:, 0:128], pbA[:, 0:128], ALU_.mult))
    c.on("vector", _Am)
    if A1B_SUB <= 2:
        ev[("ochunk", hf, ch)] = ev[("Am", hf, ch)]
        return

    def _ATm(v):
        c.wown(v, "vector")
        c.w(v, ev[("dATa", hf, ch)])
        ev[("ATm", hf, ch)] = c.step("vector", v.tensor_tensor(
            atp[:, 128:256], dec[:, 128:256], pbA[:, 128:256], ALU_.mult))
    c.on("vector", _ATm)
    if A1B_SUB <= 3:
        ev[("ochunk", hf, ch)] = ev[("ATm", hf, ch)]
        return

    def _kq(t):
        c.w(t, ev[("ATm", hf, ch)])
        for h in range(2):
            for h2 in range(2):
                last = t.matmul(
                    pbA[h * 64:(h + 1) * 64, h2 * 64:h2 * 64 + 64],
                    kslice(h), qslice(h2), start=True, stop=True)
        ev[("kq", hf, ch)] = c.step("tensor", last)
    c.on("tensor", _kq)

    fnM, _ = _decmm(None, "M", mskMb, 0, ("dM", hf, ch), ("kq", hf, ch))
    c.on("tensor", fnM)

    def _dMmask(v):
        c.wown(v, "vector")
        c.w(v, ev[("dM_mm", hf, ch)])
        ev[("dM", hf, ch)] = c.step("vector", v.tensor_tensor(
            decf[:, 0:128], pbC[:, 0:128], mskMb[:, :], ALU_.add))
    c.on("vector", _dMmask)

    def _dMact(s):
        c.wown(s, "scalar")
        c.w(s, ev[("dM", hf, ch)])
        ev[("dMa", hf, ch)] = c.step("scalar", s.activation(
            dec[:, 0:128], decf[:, 0:128], AF.Exp, bias=nLcol))
    c.on("scalar", _dMact)

    def _Mm(v):
        c.wown(v, "vector")
        c.w(v, ev[("dMa", hf, ch)])
        ev[("Mm", hf, ch)] = c.step("vector", v.tensor_tensor(
            mt[:, :], dec[:, 0:128], pbA[:, 0:128], ALU_.mult))
    c.on("vector", _Mm)
    if A1B_SUB <= 4:
        ev[("ochunk", hf, ch)] = ev[("Mm", hf, ch)]
        return

    def _g0(v):
        c.wown(v, "vector")
        c.w(v, ev[("ATm", hf, ch)])
        ev[("g", hf, ch, 0)] = c.step("vector", v.tensor_tensor(
            g_[0][:, :], identb[:, :], atp[:, 128:256], ALU_.subtract))
    c.on("vector", _g0)

    for lvl in range(1, 6):
        if lvl == 1:
            xcur, xtcur = atp[:, 0:128], atp[:, 128:256]
        else:
            xcur = xp_[(lvl - 1) % 2][:, 0:128]
            xtcur = xp_[(lvl - 1) % 2][:, 128:256]
        xnew = xp_[lvl % 2][:, 0:128]
        gcur = g_[(lvl - 1) % 2]
        gnew = g_[lvl % 2]
        ew = 128 if lvl == 5 else 256

        def _sq(t, lvl=lvl, xcur=xcur, xtcur=xtcur, ew=ew):
            c.w(t, ev[("Mm", hf, ch)] if lvl == 1 else
                ev[("gev", hf, ch, lvl - 1)])
            last = t.matmul(pbC[:, 0:128], xtcur, xcur,
                            start=True, stop=True)
            if ew == 256:
                last = t.matmul(pbC[:, 128:256], xcur, xtcur,
                                start=True, stop=True)
            ev[("sqm", hf, ch, lvl)] = c.step("tensor", last)
        c.on("tensor", _sq)

        def _sqe(v, lvl=lvl, ew=ew):
            c.wown(v, "vector")
            c.w(v, ev[("sqm", hf, ch, lvl)])
            c.step("vector", v.tensor_copy(xp_[lvl % 2][:, 0:ew],
                                           pbC[:, 0:ew]))
            ev[("sqe", hf, ch, lvl)] = ("dv", c.cnt["dv"])
        c.on("vector", _sqe)

        def _gp(t, lvl=lvl, xnew=xnew, gcur=gcur):
            c.w(t, ev[("sqe", hf, ch, lvl)])
            c.w(t, ev[("g", hf, ch, lvl - 1)] if lvl == 1 else
                ev[("gev", hf, ch, lvl - 1)])
            t.matmul(pbD[:, 0:128], xnew, gcur[:, :],
                     start=True, stop=False)
            last = t.matmul(pbD[:, 0:128], identb[:, :], gcur[:, :],
                            start=False, stop=True)
            ev[("gpm", hf, ch, lvl)] = c.step("tensor", last)
        c.on("tensor", _gp)

        def _gpe(v, lvl=lvl, gnew=gnew):
            c.wown(v, "vector")
            c.w(v, ev[("gpm", hf, ch, lvl)])
            c.step("vector", v.tensor_copy(gnew[:, :], pbD[:, 0:128]))
            ev[("gev", hf, ch, lvl)] = ("dv", c.cnt["dv"])
        c.on("vector", _gpe)
    gfin = g_[5 % 2]
    if A1B_SUB <= 5:
        ev[("ochunk", hf, ch)] = ev[("gev", hf, ch, 5)]
        return

    def _ks0(t):
        c.w(t, ev[("gev", hf, ch, 5)])
        for h in range(2):
            last = t.matmul(pbC[h * 64:(h + 1) * 64, 0:256], kslice(h),
                            s0b[:, h * 256:(h + 1) * 256],
                            start=True, stop=True)
        ev[("ks0", hf, ch)] = c.step("tensor", last)
    c.on("tensor", _ks0)
    if A1B_SUB <= 6:
        ev[("ochunk", hf, ch)] = ev[("ks0", hf, ch)]
        return

    def _vadd(t):
        c.w(t, ev[("ks0", hf, ch)])
        c.w(t, ev["eyem"])
        vsl = vb[trow:trow + 64, :].rearrange("p (ck t) -> p ck t", t=HLF + 4)
        for h in range(2):
            vap = vsl[:, 2 * h:2 * h + 2, 4 + tt2 * 128: 4 + (tt2 + 1) * 128]
            last = t.matmul(pbD[h * 64:(h + 1) * 64, 0:256],
                            eyemb[trow:trow + 64, h * 64:(h + 1) * 64], vap,
                            start=True, stop=True)
        ev[("vadd", hf, ch)] = c.step("tensor", last)
    c.on("tensor", _vadd)

    def _rhs(v):
        c.wown(v, "vector")
        c.w(v, ev[("vadd", hf, ch)])
        c.step("vector", v.tensor_scalar(t1f[:, :], pbC[:, 0:256],
                                         GBNcol, None, ALU_.mult))
        c.wown(v, "vector")
        ev[("rhs", hf, ch)] = c.step("vector", v.scalar_tensor_tensor(
            rhs_s[:, :], pbD[:, 0:256], Bcol, t1f[:, :],
            ALU_.mult, ALU_.add))
    c.on("vector", _rhs)
    if A1B_SUB <= 7:
        ev[("ochunk", hf, ch)] = ev[("rhs", hf, ch)]
        return

    def _umm(t):
        c.w(t, ev[("rhs", hf, ch)])
        ev[("umm", hf, ch)] = c.step("tensor", t.matmul(
            pbC[:, 0:256], gfin[:, :], rhs_s[:, :], start=True, stop=True))
    c.on("tensor", _umm)

    def _uev(v):
        c.wown(v, "vector")
        c.w(v, ev[("umm", hf, ch)])
        ev[("uev", hf, ch)] = c.step("vector", v.tensor_copy(
            us[:, :], pbC[:, 0:256]))
    c.on("vector", _uev)
    if A1B_SUB <= 8:
        ev[("ochunk", hf, ch)] = ev[("uev", hf, ch)]
        return

    def _qs0(t):
        c.w(t, ev[("uev", hf, ch)])
        for h in range(2):
            t.matmul(pbA[h * 64:(h + 1) * 64, 0:256], qslice(h),
                     s0b[:, h * 256:(h + 1) * 256], start=True, stop=True)
        last = t.matmul(pbD[:, 0:256], mt[:, :], us[:, :],
                        start=True, stop=True)
        ev[("qs0", hf, ch)] = c.step("tensor", last)
    c.on("tensor", _qs0)

    def _oev(v):
        c.wown(v, "vector")
        c.w(v, ev[("qs0", hf, ch)])
        c.step("vector", v.tensor_copy(t2f[:, :], pbD[:, 0:256]))
        c.wown(v, "vector")
        c.step("vector", v.scalar_tensor_tensor(
            otile[:, :], pbA[:, 0:256], Gcol, t2f[:, :],
            ALU_.mult, ALU_.add))
        c.wown(v, "vector")
        ev[("otile", hf, ch)] = c.step("vector", v.tensor_scalar(
            usc[:, :], us[:, :], ESCcol, None, ALU_.mult))
    c.on("vector", _oev)
    if A1B_SUB <= 9:
        ev[("ochunk", hf, ch)] = ev[("otile", hf, ch)]
        return

    def _ktr(t):
        c.w(t, ev[("otile", hf, ch)])
        for h in range(2):
            last = t.transpose(pt[h][0:64, 0:128].tensor[h * 64:h * 64 + 64,
                                                         0:128]
                               if False else pt[0][h * 64:h * 64 + 64, 0:128],
                               kslice(h), identb[:, :])
        ev[("ktr", hf, ch)] = c.step("tensor", last)
    c.on("tensor", _ktr)

    def _kte(v):
        c.wown(v, "vector")
        c.w(v, ev[("ktr", hf, ch)])
        ev[("kte", hf, ch)] = c.step("vector", v.tensor_copy(
            ktms[:, :], pt[0][:, 0:128]))
    c.on("vector", _kte)
    if A1B_SUB <= 10:
        ev[("ochunk", hf, ch)] = ev[("kte", hf, ch)]
        return

    def _smm(t):
        c.w(t, ev[("kte", hf, ch)])
        e0 = c.step("tensor", t.matmul(
            pbE[:, 0:256], ktms[0:64, :], usc[0:64, :],
            start=True, stop=True))
        c.w(t, e0)
        ev[("smm", hf, ch)] = c.step("tensor", t.matmul(
            pbE[:, 256:512], ktms[64:128, :], usc[64:128, :],
            start=True, stop=True))
    c.on("tensor", _smm)

    def _sev(v):
        c.wown(v, "vector")
        c.w(v, ev[("smm", hf, ch)])
        for h in range(2):
            c.step("vector", v.scalar_tensor_tensor(
                sn[:, h * 256:(h + 1) * 256], sc[:, h * 256:(h + 1) * 256],
                gcf_tm[:, h * NCHH_ + ch: h * NCHH_ + ch + 1],
                pbE[:, h * 256:(h + 1) * 256], ALU_.mult, ALU_.add))
            c.wown(v, "vector")
        ev[("sevv", hf, ch)] = ("dv", c.cnt["dv"])
    c.on("vector", _sev)

    def _sevc(s):
        c.wown(s, "scalar")
        c.w(s, ev[("sevv", hf, ch)])
        ev[("sev", hf, ch)] = c.step("scalar", s.copy(s0b[:, :], sn[:, :]))
    c.on("scalar", _sevc)
    if A1B_SUB <= 11:
        ev[("ochunk", hf, ch)] = ev[("sev", hf, ch)]
        return

    def _otr(t):
        c.w(t, ev[("sev", hf, ch)])
        c.w(t, ev["id64"])
        if ("optf",) in ev:
            c.w(t, ("dv", ev[("optf",)]))
        e01 = None
        for ec in range(4):
            h, es = ec // 2, ec % 2
            if ec == 2:
                c.w(t, e01)
            last = t.matmul(
                pbD[0:128, ec * 64:(ec + 1) * 64],
                otile[h * 64:(h + 1) * 64, es * 128:(es + 1) * 128],
                id64b[h * 64:(h + 1) * 64, 0:64],
                start=True, stop=True)
            if ec == 1:
                e01 = c.step("tensor", last)
        ev[("otr", hf, ch)] = c.step("tensor", last)
    c.on("tensor", _otr)

    def _ote(v):
        c.wown(v, "vector")
        c.w(v, ev[("otr", hf, ch)])
        dst = ob[:, :].rearrange("p (ec t) -> p ec t", t=HLF)[:, :,
                                                             t0:t0 + 64]
        srcv = pbD[:, 0:256].rearrange("p (ec t) -> p ec t", t=64)
        c.step("vector", v.tensor_copy(dst, srcv))
        ev[("optf",)] = c.cnt["dv"]
    c.on("vector", _ote)
    ev[("ochunk", hf, ch)] = ("dv", c.cnt["dv"])


ATT1_T = T


A1B_PHASES = 9
A1B_NCH = 10 ** 9
A1B_SUB = 99
A1B_NPT = 10 ** 9


def build_attn1b():
    nc = bass.Bass("TRN2", target_bir_lowering=False)
    hold = []
    AT = ATT1_T
    NPART = 4
    HLF = AT // NPART
    TBW = min(512, HLF)
    NTB = HLF // TBW
    CSP = min(2048, HLF)
    NSP = HLF // CSP
    NCHH = HLF // 64

    z_d = nc.dram_tensor("z4", [D, AT], BF16, kind="ExternalInput")
    wall_d = nc.dram_tensor("wall", [D, 1028], F32, kind="ExternalInput")
    cw_d = nc.dram_tensor("cwh", [128, 8, 4], F32, kind="ExternalInput")
    cb_d = nc.dram_tensor("cbh", [128, 8], F32, kind="ExternalInput")
    id_d = nc.dram_tensor("ident", [128, 128], F32, kind="ExternalInput")
    eye_d = nc.dram_tensor("eyem", [128, 128], F32, kind="ExternalInput")
    mskA_d = nc.dram_tensor("mskA", [128, 128], F32, kind="ExternalInput")
    mskAT_d = nc.dram_tensor("mskAT", [128, 128], F32, kind="ExternalInput")
    mskM_d = nc.dram_tensor("mskM", [128, 128], F32, kind="ExternalInput")
    id64_d = nc.dram_tensor("id64", [128, 64], F32, kind="ExternalInput")
    o_d = nc.dram_tensor("o", [512, AT], BF16, kind="ExternalOutput")
    scr_d = nc.dram_tensor("scr", [6, HLF], F32)

    wallb = sb(nc, hold, "wallb", [128, 16 * 1028], BF16)
    cwb = sb(nc, hold, "cwb", [128, 8 * 4])
    cbb = sb(nc, hold, "cbb", [128, 8])
    zbuf = sb(nc, hold, "zbuf", [128, 16 * TBW], BF16)
    qb = sb(nc, hold, "qb", [128, 2 * (HLF + 4)], BF16)
    kb = sb(nc, hold, "kb", [128, 2 * (HLF + 4)], BF16)
    vb = sb(nc, hold, "vb", [128, 4 * (HLF + 4)], BF16)
    ob = sb(nc, hold, "ob", [128, 4 * HLF], BF16)
    r_b = sb(nc, hold, "r_b", [64, HLF], BF16)
    r_L = sb(nc, hold, "r_L", [64, HLF])
    r_LB = sb(nc, hold, "r_LB", [64, HLF])
    s0f = sb(nc, hold, "s0f", [128, 512])
    s1f = sb(nc, hold, "s1f", [128, 512])
    s0b = sb(nc, hold, "s0b", [128, 512], BF16)
    atp = sb(nc, hold, "atp", [128, 256], BF16)
    xp_ = [sb(nc, hold, f"xp{i}", [128, 256], BF16) for i in range(2)]
    g_ = [sb(nc, hold, f"g{i}", [128, 128], BF16) for i in range(2)]
    mt = sb(nc, hold, "mt", [128, 128], BF16)
    dec = sb(nc, hold, "dec", [128, 256], BF16)
    decf = sb(nc, hold, "decf", [128, 256])
    rhs_s = sb(nc, hold, "rhs_s", [128, 256], BF16)
    us = sb(nc, hold, "us", [128, 256], BF16)
    usc = sb(nc, hold, "usc", [128, 256], BF16)
    t1f = sb(nc, hold, "t1f", [128, 256])
    t2f = sb(nc, hold, "t2f", [128, 256])
    ktms = sb(nc, hold, "ktms", [128, 128], BF16)
    otile = sb(nc, hold, "otile", [128, 256], BF16)
    colsb = sb(nc, hold, "colsb", [128, 20])
    NCHH_ = HLF // 64
    lc_tm = sb(nc, hold, "lc_tm", [128, NCHH_])
    lbc_tm = sb(nc, hold, "lbc_tm", [128, NCHH_])
    bc_tm = sb(nc, hold, "bc_tm", [128, NCHH_])
    nl_tm = sb(nc, hold, "nl_tm", [128, NCHH_])
    g_tm = sb(nc, hold, "g_tm", [128, NCHH_])
    gbn_tm = sb(nc, hold, "gbn_tm", [128, NCHH_])
    esc_tm = sb(nc, hold, "esc_tm", [128, NCHH_])
    gcf_tm = sb(nc, hold, "gcf_tm", [128, 2 * NCHH_])
    lp_tm = sb(nc, hold, "lp_tm", [128, NCHH_])
    lcp_tm = sb(nc, hold, "lcp_tm", [128, NCHH_])
    id64b = sb(nc, hold, "id64b", [128, 64], BF16)
    scT = sb(nc, hold, "scT", [32, 64]) if HLF // 64 <= 32 else sb(nc, hold, "scT", [HLF // 64, 64])
    rrowf = sb(nc, hold, "rrowf", [1, 4 * HLF])
    tails = sb(nc, hold, "tails", [128, 64], BF16)
    tmpc = sb(nc, hold, "tmpc", [128, CSP], BF16)
    sqw = sb(nc, hold, "sqw", [128, TBW], BF16)
    abstg = sb(nc, hold, "abstg", [4, TBW])
    nrow = sb(nc, hold, "nrow", [1, 2 * TBW]) if False else sb(nc, hold, "nrow", [1, 2 * min(TBW, 512)])
    onesr = sb(nc, hold, "onesr", [1, 128], BF16)
    onesrf = sb(nc, hold, "onesrf", [33, 128])
    onescf = sb(nc, hold, "onescf", [128, 1])
    onescb = sb(nc, hold, "onescb", [128, 1], BF16)
    zerot = sb(nc, hold, "zerot", [33, 2])
    identb = sb(nc, hold, "identb", [128, 128], BF16)
    identf = sb(nc, hold, "identf", [128, 128])
    eyemb = sb(nc, hold, "eyemb", [128, 128], BF16)
    mskPair = sb(nc, hold, "mskPair", [128, 256])
    mskMb = sb(nc, hold, "mskMb", [128, 128])

    pbA = psum(nc, hold, "pbA", [128, 512])
    pbB = psum(nc, hold, "pbB", [128, 512])
    pbC = psum(nc, hold, "pbC", [128, 512])
    pbD = psum(nc, hold, "pbD", [128, 512])
    pbE = psum(nc, hold, "pbE", [128, 512])
    pt = [psum(nc, hold, f"pt{i}", [128, 1024], BF16) for i in range(2)]

    c = Ctx(nc)
    ev = {}
    free = {}

    def _pre(g):
        ev["wall"] = c.dma(g, "q_w", wallb[:, :].rearrange(
            "p (cc j) -> p cc j", j=1028),
            wall_d.rearrange("(cc p) j -> p cc j", p=128))
        ev["cw"] = c.dma(g, "q_cw", cwb[:, :].rearrange(
            "p (k m) -> p k m", m=4), cw_d[:, :, :])
        ev["cb"] = c.dma(g, "q_cb", cbb[:, :], cb_d[:, :])
        ev["ident"] = c.dma(g, "q_id", identb[:, :], id_d[:, :])
        ev["identf"] = c.dma(g, "q_idf", identf[:, :], id_d[:, :])
        ev["eyem"] = c.dma(g, "q_ey", eyemb[:, :], eye_d[:, :])
        ev["mskA"] = c.dma(g, "q_mA", mskPair[:, 0:128], mskA_d[:, :])
        ev["mskAT"] = c.dma(g, "q_mAT", mskPair[:, 128:256], mskAT_d[:, :])
        ev["mskM"] = c.dma(g, "q_mM", mskMb[:, :], mskM_d[:, :])
        ev["id64"] = c.dma(g, "q_i64", id64b[:, :], id64_d[:, :])
    c.on("gpsimd", _pre)

    def _pre_v(v):
        for tsr in (qb, kb, vb):
            c.wown(v, "vector")
            c.step("vector", v.memset(tsr[:, :], 0.0))
        for tsr in (s0f, s1f, s0b, tails):
            c.wown(v, "vector")
            c.step("vector", v.memset(tsr[:, :], 0.0))
        c.wown(v, "vector")
        c.step("vector", v.memset(zerot[:, :], 0.0))
        c.wown(v, "vector")
        c.step("vector", v.memset(onesr[:, :], 1.0))
        c.wown(v, "vector")
        c.step("vector", v.memset(onesrf[:, :], 1.0))
        c.wown(v, "vector")
        c.step("vector", v.memset(onescf[:, :], 1.0))
        c.wown(v, "vector")
        ev["consts"] = c.step("vector", v.memset(onescb[:, :], 1.0))
    c.on("vector", _pre_v)

    groups = [("q0", 0, qb, 0), ("q1", 128, qb, 1),
              ("k0", 256, kb, 0), ("k1", 384, kb, 1),
              ("v0", 512, vb, 0), ("v1", 640, vb, 1),
              ("v2", 768, vb, 2), ("v3", 896, vb, 3)]

    scur = [0]

    for hf in range(min(NPART, A1B_NPT)):
        for tb in range(NTB):
            def _ldz(e, hf=hf, tb=tb):
                c.w(e, ev.get(("zdone", hf, tb - 1)) if tb > 0 else
                    ev.get(("zdone", hf - 1, NTB - 1)))
                src = z_d[:, hf * HLF + tb * TBW: hf * HLF + (tb + 1) * TBW]
                ev[("z", hf, tb)] = c.dma(
                    e, "q_z", zbuf[:, :].rearrange("p (cc t) -> p cc t", t=TBW),
                    src.rearrange("(cc p) t -> p cc t", p=128))
            c.on("sync", _ldz)

            for gi, (gn, cbse, dbuf, dck) in enumerate(groups):
                bank = (pbC, pbD, pbE)[gi % 3]

                def _pmm(t, hf=hf, tb=tb, gi=gi, cbse=cbse, bank=bank):
                    if hf == 0 and tb == 0 and gi == 0:
                        c.w(t, ev["wall"])
                    if tb == 0:
                        c.w(t, ev.get(("scan_done", hf - 1)))
                    c.w(t, ev[("z", hf, tb)])
                    c.w(t, free.get(("pj", gi % 3)))
                    last = None
                    for cc in range(16):
                        last = t.matmul(
                            bank[:, 0:TBW],
                            wallb[:, cc * 1028 + cbse: cc * 1028 + cbse + 128],
                            zbuf[:, cc * TBW:(cc + 1) * TBW],
                            start=(cc == 0), stop=(cc == 15))
                    ev[("pmm", hf, tb, gi)] = c.step("tensor", last)
                c.on("tensor", _pmm)

                def _pev(v, hf=hf, tb=tb, gi=gi, dbuf=dbuf, dck=dck,
                         bank=bank):
                    c.wown(v, "vector")
                    c.w(v, ev[("pmm", hf, tb, gi)])
                    e = c.step("vector", v.tensor_copy(
                        dbuf[:, dck * (HLF + 4) + 4 + tb * TBW:
                             dck * (HLF + 4) + 4 + (tb + 1) * TBW],
                        bank[:, 0:TBW]))
                    ev[("pev", hf, tb, gi)] = e
                    free[("pj", gi % 3)] = e
                c.on("vector", _pev)

            def _abmm(t, hf=hf, tb=tb):
                if tb == 0:
                    c.w(t, ev.get(("scan_done", hf - 1)))
                c.w(t, ev[("z", hf, tb)])
                c.w(t, free.get(("pab",)))
                c.w(t, ev["identf"])
                last = None
                for cc in range(16):
                    last = t.matmul(
                        pbB[0:4, 0:TBW],
                        wallb[:, cc * 1028 + 1024: cc * 1028 + 1028],
                        zbuf[:, cc * TBW:(cc + 1) * TBW],
                        start=(cc == 0), stop=(cc == 15))
                ev[("abmm", hf, tb)] = c.step("tensor", last)
            c.on("tensor", _abmm)

            def _abst(v, hf=hf, tb=tb):
                c.wown(v, "vector")
                c.w(v, ev[("abmm", hf, tb)])
                ev[("abst", hf, tb)] = c.step("vector", v.tensor_copy(
                    abstg[:, :], pbB[0:4, 0:TBW]))
                free[("pab",)] = ev[("abst", hf, tb)]
            c.on("vector", _abst)

            for abi in range(4):
                def _abx(t, hf=hf, tb=tb, abi=abi):
                    c.w(t, ev[("abst", hf, tb)])
                    c.w(t, free.get(("pabx",)))
                    ev[("abx", hf, tb, abi)] = c.step("tensor", t.matmul(
                        pbA[0:1, 0:TBW], identf[0:4, abi:abi + 1],
                        abstg[:, :], start=True, stop=True))
                c.on("tensor", _abx)

                def _abe(s, hf=hf, tb=tb, abi=abi):
                    c.wown(s, "scalar")
                    c.w(s, ev[("abx", hf, tb, abi)])
                    h2 = abi % 2
                    if abi < 2:
                        c.step("scalar", s.activation(
                            nrow[0:1, 0:TBW], pbA[0:1, 0:TBW], AF.Sigmoid))
                        c.wown(s, "scalar")
                        e = c.step("scalar", s.activation(
                            r_LB[32 * h2:32 * h2 + 1, tb * TBW:(tb + 1) * TBW],
                            nrow[0:1, 0:TBW], AF.Ln))
                    else:
                        e = c.step("scalar", s.activation(
                            r_b[32 * h2:32 * h2 + 1, tb * TBW:(tb + 1) * TBW],
                            pbA[0:1, 0:TBW], AF.Sigmoid))
                    ev[("abe", hf, tb, abi)] = e
                    free[("pabx",)] = e
                c.on("scalar", _abe)

            def _zdone(v, hf=hf, tb=tb):
                ev[("abev", hf, tb)] = ev[("abe", hf, tb, 3)]
                ev[("zdone", hf, tb)] = ev[("abe", hf, tb, 3)]
            _zdone(None)

        def _lrows(v, hf=hf):
            c.wown(v, "vector")
            c.w(v, ev[("abev", hf, NTB - 1)])
            for h2 in range(2):
                d0 = bass.AP(onescf.ap().tensor, 32 * h2,
                             [[1, 1], [0, HLF]])
                c.step("vector", v.tensor_tensor_scan(
                    r_L[32 * h2:32 * h2 + 1, :],
                    d0, r_LB[32 * h2:32 * h2 + 1, :], 0.0,
                    ALU.mult, ALU.add))
                c.wown(v, "vector")
            ev[("lrows", hf)] = ("dv", c.cnt["dv"])
        c.on("vector", _lrows)

        def _lnb(s, hf=hf):
            c.wown(s, "scalar")
            c.w(s, ev[("lrows", hf)])
            for h2 in range(2):
                c.step("scalar", s.activation(
                    r_LB[32 * h2:32 * h2 + 1, :],
                    r_b[32 * h2:32 * h2 + 1, :], AF.Ln))
                c.wown(s, "scalar")
            ev[("lnb", hf)] = ("ac", c.cnt["ac"])
        c.on("scalar", _lnb)

        def _lbadd(v, hf=hf):
            c.wown(v, "vector")
            c.w(v, ev[("lnb", hf)])
            for h2 in range(2):
                c.step("vector", v.tensor_tensor(
                    r_LB[32 * h2:32 * h2 + 1, :],
                    r_LB[32 * h2:32 * h2 + 1, :],
                    r_L[32 * h2:32 * h2 + 1, :], ALU.add))
                c.wown(v, "vector")
            ev[("lb", hf)] = ("dv", c.cnt["dv"])
        c.on("vector", _lbadd)

        def _bnc(g, hf=hf):
            c.w(g, ev[("lb", hf)])
            srcs = {0: r_L, 1: r_LB, 2: r_b}
            for h2 in range(2):
                for ri in range(3):
                    c.w(g, ev.get(("tcp", hf - 1, h2, ri)))
                    ev[("bw", hf, h2, ri)] = c.dma(
                        g, f"q_s{h2}{ri}", scr_d[2 * ri + h2],
                        srcs[ri][32 * h2:32 * h2 + 1, :])
        c.on("gpsimd", _bnc)

        def _rrd(g, hf=hf):
            for h2 in range(2):
                for ri in range(2):
                    k = 2 * ri + h2
                    c.w(g, ev[("bw", hf, h2, ri)])
                    c.w(g, ev.get(("rrfree", hf - 1)))
                    ev[("rrow", hf, k)] = c.dma(
                        g, f"q_rr{k}", rrowf[0:1, k * HLF:(k + 1) * HLF],
                        scr_d[k])
        c.on("gpsimd", _rrd)

        for h2 in range(2):
            for ri, dstt in ((0, lc_tm), (1, lbc_tm), (2, bc_tm)):
                def _brd(g, hf=hf, h2=h2, ri=ri):
                    c.w(g, ev[("bw", hf, h2, ri)])
                    c.w(g, ev.get(("btr_last",)) or ("pe", 0))
                    ev[("br", hf, h2, ri)] = c.dma(
                        g, f"q_r{h2}{ri}", scT[0:NCHH_, :],
                        scr_d[2 * ri + h2].rearrange("(ch p) -> ch p", p=64))
                c.on("gpsimd", _brd)

                def _btr(t, hf=hf, h2=h2, ri=ri):
                    c.w(t, ev[("br", hf, h2, ri)])
                    c.w(t, ev["identf"])
                    if ("btrf",) in ev:
                        c.w(t, ("dv", ev[("btrf",)]))
                    ev[("btr", hf, h2, ri)] = c.step("tensor", t.matmul(
                        pbB[64 * h2:64 * (h2 + 1), 0:NCHH_],
                        scT[0:NCHH_, :], identf[0:NCHH_, 0:NCHH_],
                        start=True, stop=True))
                    ev[("btr_last",)] = ev[("btr", hf, h2, ri)]
                c.on("tensor", _btr)

                def _bev(v, hf=hf, h2=h2, ri=ri, dstt=dstt):
                    c.wown(v, "vector")
                    c.w(v, ev[("btr", hf, h2, ri)])
                    e = c.step("vector", v.tensor_copy(
                        dstt[64 * h2:64 * (h2 + 1), :],
                        pbB[64 * h2:64 * (h2 + 1), 0:NCHH_]))
                    ev[("tcp", hf, h2, ri)] = e
                    ev[("btrf",)] = c.cnt["dv"]
                c.on("vector", _bev)

        def _rowstage(s, hf=hf):
            c.wown(s, "scalar")
            for h2 in range(2):
                for ri in range(3):
                    c.w(s, ev[("tcp", hf, h2, ri)])
            lcs = bass.AP(r_L.ap().tensor, 63, [[HLF, 1], [64, NCHH_]])
            lcs1 = bass.AP(r_L.ap().tensor, 32 * HLF + 63,
                           [[HLF, 1], [64, NCHH_]])
            c.step("scalar", s.copy(nrow[0:1, 0:NCHH_], lcs))
            c.wown(s, "scalar")
            ev[("lcrow", hf)] = c.step(
                "scalar", s.copy(nrow[0:1, NCHH_:2 * NCHH_], lcs1))
        c.on("scalar", _rowstage)

        def _rowprep(v, hf=hf):
            c.wown(v, "vector")
            c.w(v, ev[("lcrow", hf)])
            c.step("vector", v.tensor_copy(
                nrow[0:1, 2 * NCHH_ + 1:3 * NCHH_],
                nrow[0:1, 0:NCHH_ - 1]))
            c.wown(v, "vector")
            c.step("vector", v.memset(nrow[0:1, 2 * NCHH_:2 * NCHH_ + 1], 0.0))
            c.wown(v, "vector")
            c.step("vector", v.tensor_copy(
                nrow[0:1, 3 * NCHH_ + 1:4 * NCHH_],
                nrow[0:1, NCHH_:2 * NCHH_ - 1]))
            c.wown(v, "vector")
            c.step("vector", v.memset(nrow[0:1, 3 * NCHH_:3 * NCHH_ + 1], 0.0))
            c.wown(v, "vector")
            c.step("vector", v.tensor_tensor(
                nrow[0:1, 4 * NCHH_:6 * NCHH_], nrow[0:1, 0:2 * NCHH_],
                nrow[0:1, 2 * NCHH_:4 * NCHH_], ALU.subtract))
            c.wown(v, "vector")
            ev[("rowprep", hf)] = ("dv", c.cnt["dv"])
        c.on("vector", _rowprep)

        def _rowexp(s, hf=hf):
            c.wown(s, "scalar")
            c.w(s, ev[("rowprep", hf)])
            ev[("rowexp", hf)] = c.step("scalar", s.activation(
                nrow[0:1, 4 * NCHH_:6 * NCHH_],
                nrow[0:1, 4 * NCHH_:6 * NCHH_], AF.Exp))
        c.on("scalar", _rowexp)

        def _bcmm(t, hf=hf):
            c.w(t, ev[("rowexp", hf)])
            c.w(t, ev["consts"])
            t.matmul(pbB[0:64, 0:NCHH_], onesrf[0:1, 0:64],
                     nrow[0:1, 0:NCHH_], start=True, stop=True)
            t.matmul(pbB[64:128, 0:NCHH_], onesrf[0:1, 0:64],
                     nrow[0:1, NCHH_:2 * NCHH_], start=True, stop=True)
            t.matmul(pbB[0:64, NCHH_:2 * NCHH_], onesrf[0:1, 0:64],
                     nrow[0:1, 2 * NCHH_:3 * NCHH_], start=True, stop=True)
            t.matmul(pbB[64:128, NCHH_:2 * NCHH_], onesrf[0:1, 0:64],
                     nrow[0:1, 3 * NCHH_:4 * NCHH_], start=True, stop=True)
            t.matmul(pbB[:, 2 * NCHH_:3 * NCHH_], onesrf[0:1, :],
                     nrow[0:1, 4 * NCHH_:5 * NCHH_], start=True, stop=True)
            last = t.matmul(pbB[:, 3 * NCHH_:4 * NCHH_], onesrf[0:1, :],
                            nrow[0:1, 5 * NCHH_:6 * NCHH_],
                            start=True, stop=True)
            ev[("bcmm", hf)] = c.step("tensor", last)
        c.on("tensor", _bcmm)

        def _tiles(v, hf=hf):
            c.wown(v, "vector")
            c.w(v, ev[("bcmm", hf)])
            c.step("vector", v.tensor_copy(lcp_tm[:, :], pbB[:, 0:NCHH_]))
            c.wown(v, "vector")
            c.step("vector", v.tensor_copy(
                lp_tm[:, :], pbB[:, NCHH_:2 * NCHH_]))
            c.wown(v, "vector")
            c.step("vector", v.tensor_copy(
                gcf_tm[:, :], pbB[:, 2 * NCHH_:4 * NCHH_]))
            c.wown(v, "vector")
            c.step("vector", v.tensor_scalar(nl_tm[:, :], lc_tm[:, :], -1.0,
                                             None, ALU.mult))
            c.wown(v, "vector")
            c.step("vector", v.tensor_tensor(g_tm[:, :], lc_tm[:, :],
                                             lp_tm[:, :], ALU.subtract))
            c.wown(v, "vector")
            c.step("vector", v.tensor_tensor(gbn_tm[:, :], lbc_tm[:, :],
                                             lp_tm[:, :], ALU.subtract))
            c.wown(v, "vector")
            c.step("vector", v.tensor_tensor(esc_tm[:, :], lcp_tm[:, :],
                                             lc_tm[:, :], ALU.subtract))
            c.wown(v, "vector")
            ev[("tiles1", hf)] = ("dv", c.cnt["dv"])
        c.on("vector", _tiles)

        def _tilex(s, hf=hf):
            c.wown(s, "scalar")
            c.w(s, ev[("tiles1", hf)])
            c.step("scalar", s.activation(g_tm[:, :], g_tm[:, :], AF.Exp))
            c.wown(s, "scalar")
            c.step("scalar", s.activation(gbn_tm[:, :], gbn_tm[:, :], AF.Exp))
            c.wown(s, "scalar")
            ev[("tilex", hf)] = c.step("scalar", s.activation(
                esc_tm[:, :], esc_tm[:, :], AF.Exp))
        c.on("scalar", _tilex)

        def _tilen(v, hf=hf):
            c.wown(v, "vector")
            c.w(v, ev[("tilex", hf)])
            c.step("vector", v.tensor_scalar(gbn_tm[:, :], gbn_tm[:, :], -1.0,
                                             None, ALU.mult))
            c.wown(v, "vector")
            ev[("ctiles", hf)] = ("dv", c.cnt["dv"])
        c.on("vector", _tilen)

        def _tail(v, hf=hf):
            c.wown(v, "vector")
            c.w(v, ev[("pev", hf, NTB - 1, 7)])
            for ck, (gn, cbse, dbuf, dck) in enumerate(groups):
                off = dck * (HLF + 4) if dbuf is not qb or True else 0
                base = {0: qb, 1: qb, 2: kb, 3: kb}.get(ck)
                c.step("vector", v.tensor_copy(
                    tails[:, ck * 4: ck * 4 + 3],
                    (qb if ck < 2 else kb if ck < 4 else vb)[
                        :, (ck % 2 if ck < 4 else ck - 4) * (HLF + 4) + 4 +
                        HLF - 3: (ck % 2 if ck < 4 else ck - 4) * (HLF + 4) +
                        4 + HLF]))
                c.wown(v, "vector")
            ev[("tails", hf)] = ("dv", c.cnt["dv"])
        c.on("vector", _tail)

        if hf > 0:
            def _pads(v, hf=hf):
                c.wown(v, "vector")
                for ck in range(8):
                    buf = qb if ck < 2 else kb if ck < 4 else vb
                    dck = ck % 2 if ck < 4 else ck - 4
                    c.step("vector", v.tensor_copy(
                        buf[:, dck * (HLF + 4) + 1: dck * (HLF + 4) + 4],
                        tails[:, 32 + ck * 4: 32 + ck * 4 + 3]))
                    c.wown(v, "vector")
                ev[("pads", hf)] = ("dv", c.cnt["dv"])
            c.on("vector", _pads)

        def _tailrot(v, hf=hf):
            c.wown(v, "vector")
            c.w(v, ev[("tails", hf)])
            c.step("vector", v.tensor_copy(tails[:, 32:64], tails[:, 0:32]))
            c.wown(v, "vector")
            ev[("tailrot", hf)] = ("dv", c.cnt["dv"])
        c.on("vector", _tailrot)

        for ck in range(8):
            buf = qb if ck < 2 else kb if ck < 4 else vb
            dck = ck % 2 if ck < 4 else ck - 4
            bof = dck * (HLF + 4)
            for sp in range(NSP):
                def _conv(v, hf=hf, ck=ck, buf=buf, bof=bof, sp=sp):
                    c.wown(v, "vector")
                    c.w(v, ev[("tailrot", hf)])
                    c.w(v, ev["cw"])
                    c.w(v, ev.get(("silu_last",)))
                    if hf > 0:
                        c.w(v, ev[("pads", hf)])
                    vw0 = buf[:, bof + 1 + sp * CSP: bof + 1 + sp * CSP + CSP]
                    c.step("vector", v.tensor_scalar(
                        tmpc[:, :], vw0, cwb[:, ck * 4: ck * 4 + 1], None,
                        ALU.mult))
                    for m in range(1, 4):
                        c.wown(v, "vector")
                        vwm = buf[:, bof + 1 + m + sp * CSP:
                                  bof + 1 + m + sp * CSP + CSP]
                        c.step("vector", v.scalar_tensor_tensor(
                            tmpc[:, :], vwm, cwb[:, ck * 4 + m: ck * 4 + m + 1],
                            tmpc[:, :], ALU.mult, ALU.add))
                    c.wown(v, "vector")
                    ev[("conv", hf, ck, sp)] = ("dv", c.cnt["dv"])
                c.on("vector", _conv)

                def _silu(s, hf=hf, ck=ck, buf=buf, bof=bof, sp=sp):
                    c.wown(s, "scalar")
                    c.w(s, ev[("conv", hf, ck, sp)])
                    c.w(s, ev["cb"])
                    e = c.step("scalar", s.activation(
                        buf[:, bof + 4 + sp * CSP: bof + 4 + (sp + 1) * CSP],
                        tmpc[:, :], AF.Silu, bias=cbb[:, ck:ck + 1]))
                    ev[("silu", hf, ck, sp)] = e
                    ev[("silu_last",)] = e
                c.on("scalar", _silu)
        ev[("convall", hf)] = ("ac", c.cnt["ac"])

        for ck in range(4):
            if A1B_PHASES < 3:
                break
            buf = qb if ck < 2 else kb
            bof = (ck % 2) * (HLF + 4)
            for tb in range(NTB):
                def _sq(s, hf=hf, ck=ck, buf=buf, bof=bof, tb=tb):
                    c.wown(s, "scalar")
                    c.w(s, ev[("convall", hf)])
                    ev[("sq", hf, ck, tb)] = c.step("scalar", s.activation(
                        sqw[:, :], buf[:, bof + 4 + tb * TBW:
                                       bof + 4 + (tb + 1) * TBW], AF.Square))
                c.on("scalar", _sq)

                def _nmm(t, hf=hf, ck=ck, tb=tb):
                    c.w(t, ev[("sq", hf, ck, tb)])
                    c.w(t, ev["consts"])
                    c.w(t, free.get(("pnm",)))
                    ev[("nmm", hf, ck, tb)] = c.step("tensor", t.matmul(
                        pbA[0:1, 0:TBW], onescb[:, :], sqw[:, :],
                        start=True, stop=True))
                c.on("tensor", _nmm)

                def _nrm(v, hf=hf, ck=ck, tb=tb):
                    c.wown(v, "vector")
                    c.w(v, ev[("nmm", hf, ck, tb)])
                    c.step("vector", v.tensor_scalar(
                        nrow[0:1, 0:TBW], pbA[0:1, 0:TBW], 1e-24, None,
                        ALU.max))
                    c.wown(v, "vector")
                    ev[("nr1", hf, ck, tb)] = c.step("vector", v.engine_nop()) \
                        if False else ("dv", c.cnt["dv"])
                    free[("pnm",)] = ("dv", c.cnt["dv"])
                c.on("vector", _nrm)

                def _nsq(s, hf=hf, ck=ck, tb=tb):
                    c.wown(s, "scalar")
                    c.w(s, ev[("nr1", hf, ck, tb)])
                    ev[("nsq", hf, ck, tb)] = c.step("scalar", s.activation(
                        nrow[0:1, TBW:2 * TBW], nrow[0:1, 0:TBW], AF.Sqrt))
                c.on("scalar", _nsq)

                def _nrc(v, hf=hf, ck=ck, tb=tb):
                    c.wown(v, "vector")
                    c.w(v, ev[("nsq", hf, ck, tb)])
                    ev[("nrc", hf, ck, tb)] = c.step("vector", v.reciprocal(
                        nrow[0:1, 0:TBW], nrow[0:1, TBW:2 * TBW]))
                c.on("vector", _nrc)

                def _nbc(t, hf=hf, ck=ck, tb=tb):
                    c.w(t, ev[("nrc", hf, ck, tb)])
                    c.w(t, free.get(("pnb",)))
                    ev[("nbc", hf, ck, tb)] = c.step("tensor", t.matmul(
                        pbB[:, 0:TBW], onesrf[0:1, :], nrow[0:1, 0:TBW],
                        start=True, stop=True))
                c.on("tensor", _nbc)

                def _napp(v, hf=hf, ck=ck, buf=buf, bof=bof, tb=tb):
                    c.wown(v, "vector")
                    c.w(v, ev[("nbc", hf, ck, tb)])
                    sl = buf[:, bof + 4 + tb * TBW: bof + 4 + (tb + 1) * TBW]
                    e = c.step("vector", v.tensor_tensor(
                        sl, sl, pbB[:, 0:TBW], ALU.mult))
                    ev[("napp", hf, ck, tb)] = e
                    free[("pnb",)] = e
                c.on("vector", _napp)
        ev[("normall", hf)] = ("dv", c.cnt["dv"])

        for ck in range(4):
            bof = ck * (HLF + 4)
            for tb2 in range(HLF // 128):
                def _vtr(t, hf=hf, ck=ck, bof=bof, tb2=tb2):
                    if ck == 0 and tb2 == 0:
                        c.w(t, ev[("convall", hf)])
                        c.w(t, ev["ident"])
                    if ("ptf", tb2 % 2) in ev:
                        c.w(t, ("dv", ev[("ptf", tb2 % 2)]))
                    ev[("vtr", hf, ck, tb2)] = c.step("tensor", t.transpose(
                        pt[tb2 % 2][:, 0:128],
                        vb[:, bof + 4 + tb2 * 128: bof + 4 + (tb2 + 1) * 128],
                        identb[:, :]))
                c.on("tensor", _vtr)

                def _vev(v, hf=hf, ck=ck, bof=bof, tb2=tb2):
                    c.wown(v, "vector")
                    c.w(v, ev[("vtr", hf, ck, tb2)])
                    c.step("vector", v.tensor_copy(
                        vb[:, bof + 4 + tb2 * 128: bof + 4 + (tb2 + 1) * 128],
                        pt[tb2 % 2][:, 0:128]))
                    ev[("ptf", tb2 % 2)] = c.cnt["dv"]
                c.on("vector", _vev)
        ev[("vtr_done", hf)] = ("dv", c.cnt["dv"])

        if A1B_PHASES >= 4:
            for ch in range(min(NCHH, A1B_NCH)):
                t0 = ch * 64
                emit_chunk(c, ev, free, locals(), hf, ch, t0, scur)
                scur[0] ^= 1
            lastch = min(NCHH, A1B_NCH) - 1
            ev[("scan_done", hf)] = ev[("ochunk", hf, lastch)]
            ev[("rrfree", hf)] = (ev.get(("dM_mm", hf, lastch))
                                  or ev.get(("dA_mm", hf, lastch))
                                  or ev[("ochunk", hf, lastch)])

        def _sto(e, hf=hf):
            if A1B_PHASES >= 4:
                c.w(e, ev[("ochunk", hf, min(NCHH, A1B_NCH) - 1)])
            else:
                c.w(e, ("dv", c.cnt["dv"]))
                c.w(e, ("ac", c.cnt["ac"]))
            dst = o_d[:, hf * HLF:(hf + 1) * HLF].rearrange(
                "(ec p) t -> p ec t", p=128)
            ev[("ost", hf)] = c.dma(
                e, "q_o", dst,
                ob[:, :].rearrange("p (ec t) -> p ec t", t=HLF))
            c.w(e, ev[("ost", hf)])
        c.on("sync", _sto)

    def _fin(e):
        c.w(e, ev[("ost", min(NPART, A1B_NPT) - 1)])
    c.on("sync", _fin)

    c.close()
    for h in reversed(hold):
        h.__exit__(None, None, None)
    return nc


def _attn1_consts():
    eyem = np.zeros((128, 128), np.float32)
    for p in range(128):
        eyem[p, p % 64] = 1.0
        eyem[p, 64 + p % 64] = 1.0
    r = np.arange(128)
    blk = (r[:, None] // 64) == (r[None, :] // 64)
    low = r[:, None] > r[None, :]
    upi = r[:, None] <= r[None, :]
    mskA = np.where(blk & low, 0.0, 1e30).astype(np.float32)
    mskAT = np.where(blk & (~low) & (~np.eye(128, dtype=bool)), 0.0,
                     -1e30).astype(np.float32)
    mskM = np.where(blk & upi, 0.0, -1e30).astype(np.float32)
    return eyem, mskA, mskAT, mskM


def _run_attn1(x_full, inp, z4=None):
    nc_b = _get("attn1b", build_attn1b)
    nc_c = _get("attn1c", build_attn1c)
    shards = x_full.reshape(NCORES, TPC, D)
    if z4 is None:
        nc_a = _get("attn1a", build_attn1a)
        in_a = [{"x": np.ascontiguousarray(shards[i]), "ident": _IDENT}
                for i in range(NCORES)]
        ra = run_bass_kernel_spmd(nc_a, in_a, core_ids=list(range(NCORES)))
        z4 = np.ascontiguousarray(
            np.concatenate([r["z4"] for r in ra.results], 1))

    eyem, mskA, mskAT, mskM = _attn1_consts()
    Wq, Wk, Wv = inp["L1_attn_Wq"], inp["L1_attn_Wk"], inp["L1_attn_Wv"]
    Wa, Wb = inp["L1_attn_Wa"], inp["L1_attn_Wb"]
    qcw, qcb = inp["L1_attn_qcw"], inp["L1_attn_qcb"]
    kcw, kcb = inp["L1_attn_kcw"], inp["L1_attn_kcb"]
    vcw, vcb = inp["L1_attn_vcw"], inp["L1_attn_vcb"]
    in_b = []
    for i in range(NCORES):
        h0 = 2 * i
        wall = np.concatenate(
            [Wq[:, h0 * 128:(h0 + 2) * 128], Wk[:, h0 * 128:(h0 + 2) * 128],
             Wv[:, h0 * 256:(h0 + 2) * 256], Wa[:, h0:h0 + 2],
             Wb[:, h0:h0 + 2]], axis=1)
        cwh = np.concatenate(
            [qcw[h0 * 128:(h0 + 2) * 128], kcw[h0 * 128:(h0 + 2) * 128],
             vcw[h0 * 256:(h0 + 2) * 256]], axis=0)
        cwh = np.ascontiguousarray(cwh.reshape(8, 128, 4).transpose(1, 0, 2))
        cbh = np.concatenate(
            [qcb[h0 * 128:(h0 + 2) * 128], kcb[h0 * 128:(h0 + 2) * 128],
             vcb[h0 * 256:(h0 + 2) * 256]])
        cbh = np.ascontiguousarray(cbh.reshape(8, 128).T)
        id64 = np.ascontiguousarray(
            np.concatenate([np.eye(64, dtype=np.float32)] * 2, 0))
        in_b.append({"z4": z4, "wall": np.ascontiguousarray(wall),
                     "cwh": cwh, "cbh": cbh, "ident": _IDENT, "eyem": eyem,
                     "mskA": mskA, "mskAT": mskAT, "mskM": mskM,
                     "id64": id64})
    rb = run_bass_kernel_spmd(nc_b, in_b, core_ids=list(range(NCORES)))
    o_full = np.ascontiguousarray(
        np.concatenate([r["o"] for r in rb.results], 0))
    wo = np.ascontiguousarray(inp["L1_attn_Wo"])
    in_c = [{"x": np.ascontiguousarray(shards[i]),
             "ofm": np.ascontiguousarray(o_full[:, i * TPC:(i + 1) * TPC]),
             "wo": wo} for i in range(NCORES)]
    rc = run_bass_kernel_spmd(nc_c, in_c, core_ids=list(range(NCORES)))
    return np.concatenate([r["xo"] for r in rc.results], 0)
